# version 06
# baseline (speedup 1.0000x reference)
import numpy as np
import ml_dtypes
import concourse.bass as bass
import concourse.mybir as mybir

F32 = mybir.dt.float32
BF16 = mybir.dt.bfloat16
AF = mybir.ActivationFunctionType
ALU = mybir.AluOpType
AX = mybir.AxisListType.X
ds = bass.ds

B = 2
T = 2048
D = 1024
N = B * T
FC = D // 128
EPS_MOD = 1e-5
EPS_L1 = 1e-6


def host_tables():
    t = {}
    ident = np.eye(128)
    t["IDENT_BF"] = ident.astype(ml_dtypes.bfloat16)
    t["IDENT_F32"] = ident.astype(np.float32)
    p64 = np.zeros((128, 128))
    for i in range(32):
        p64[i + 32, i] = 1.0
        p64[i, i + 32] = 1.0
    t["P64_F32"] = p64.astype(np.float32)
    id64 = np.zeros((128, 128))
    id64[:64, :64] = np.eye(64)
    t["ID64_BF"] = id64.astype(ml_dtypes.bfloat16)
    t["ONES_1x128_F32"] = np.ones((1, 128), np.float32)
    t["ONES_128x1_BF"] = np.ones((128, 1), ml_dtypes.bfloat16)
    t["ONES_128x1_F32"] = np.ones((128, 1), np.float32)

    HD = 64
    nb = HD // 4
    inv = 1.0 / (10000.0 ** (np.arange(nb) / nb))
    ang = np.arange(T)[:, None] * inv[None, :]
    sin_h = np.repeat(np.sin(ang), 2, axis=-1)
    cos_h = np.repeat(np.cos(ang), 2, axis=-1)
    sin = np.concatenate([sin_h, sin_h], axis=-1).T
    cos = np.concatenate([cos_h, cos_h], axis=-1).T
    s0p = sin.copy()
    s0p[:32] *= -1.0
    t["C0"] = cos.astype(np.float32)
    t["S0P"] = s0p.astype(np.float32)

    f = 1.0 / (10000.0 ** (np.arange(0, HD, 2) / HD))
    tt = np.arange(T)
    sin_t = np.sin(tt[None, :] * f[:, None])
    cos_t = np.cos(tt[None, :] * f[:, None])
    T1 = np.concatenate([sin_t, sin_t], axis=0)
    T2p = np.concatenate([cos_t, -cos_t], axis=0)
    R2 = np.concatenate([cos_t, sin_t], axis=0)
    t["T1"] = T1.astype(np.float32)
    t["T2P"] = T2p.astype(np.float32)
    t["R2_BF"] = R2.astype(ml_dtypes.bfloat16)

    NEG = np.float32(-8e9)
    def m_l0(krange, qrange):
        k = np.arange(*krange)[:, None]
        q = np.arange(*qrange)[None, :]
        allowed = (np.abs(q - k) <= 12) | (q < 8) | (k < 8)
        return np.where(allowed, 0.0, NEG).astype(np.float32)
    t["M_FULL"] = m_l0((0, 128), (0, T))
    t["M_B1"] = m_l0((128, 256), (0, 384))
    k = np.arange(128)[:, None]
    q = np.arange(384)[None, :]
    band = (np.abs((q - 128) - k) <= 12)
    t["M_B"] = np.where(band, 0.0, NEG).astype(np.float32)
    qa = np.arange(128)[None, :]
    t["M_A"] = np.where(qa < 8, 0.0, NEG).astype(np.float32) * np.ones((128, 1), np.float32)
    kp = np.arange(128)[:, None]
    qc = np.arange(128)[None, :]
    t["M_DIAG"] = np.where(kp > qc, np.float32(-1e30), 0.0).astype(np.float32)
    sel = np.zeros((9, 9 * 128), np.float32)
    for e in range(9):
        sel[e, e * 128:(e + 1) * 128] = 1.0
    t["SEL"] = sel
    t["ONES_ROW"] = np.ones((1, N), np.float32)
    return t


def host_prep(inputs, core):
    g = dict()
    x = np.asarray(inputs["L0_attn_x"])
    g["x_in"] = np.ascontiguousarray(x[2 * core:2 * core + 2])

    g["l0_wq"] = np.ascontiguousarray(inputs["L0_attn_Wq"])
    g["l0_wk"] = np.ascontiguousarray(inputs["L0_attn_Wk"])
    g["l0_wv"] = np.ascontiguousarray(inputs["L0_attn_Wv"])
    g["l0_wo"] = np.ascontiguousarray(inputs["L0_attn_Wo"])
    g["router"] = np.ascontiguousarray(inputs["L0_ffn_w_router"])
    g["mw1"] = np.ascontiguousarray(np.concatenate(
        [inputs["L0_ffn_w1"], np.asarray(inputs["L0_ffn_sw1"])[None]], axis=0))
    g["mw3"] = np.ascontiguousarray(np.concatenate(
        [inputs["L0_ffn_w3"], np.asarray(inputs["L0_ffn_sw3"])[None]], axis=0))
    g["mw2"] = np.ascontiguousarray(np.concatenate(
        [inputs["L0_ffn_w2"], np.asarray(inputs["L0_ffn_sw2"])[None]], axis=0))
    wq = np.asarray(inputs["L1_attn_Wq"])
    wk = np.asarray(inputs["L1_attn_Wk"])
    wv = np.asarray(inputs["L1_attn_Wv"])
    wo = np.asarray(inputs["L1_attn_Wo"])
    qkvo = np.stack([wq.transpose(0, 2, 1), wk.transpose(0, 2, 1),
                     wv.transpose(0, 2, 1), wo.transpose(0, 2, 1)], axis=1)
    g["l1_qkvo"] = np.ascontiguousarray(qkvo)
    g["l1_w1"] = np.ascontiguousarray(np.asarray(inputs["L1_attn_W1"]).transpose(0, 2, 1))
    g["l1_w2"] = np.ascontiguousarray(np.asarray(inputs["L1_attn_W2"]).transpose(0, 2, 1))
    g["d_w1t"] = np.ascontiguousarray(np.asarray(inputs["L1_ffn_W1"]).T)
    g["d_w2t"] = np.ascontiguousarray(np.asarray(inputs["L1_ffn_W2"]).T)

    cols = {}
    buf = []

    def add(name, arr128):
        cols[name] = len(buf)
        assert arr128.shape[0] == 128
        for j in range(arr128.shape[1]):
            buf.append(np.ascontiguousarray(arr128[:, j]))
        return cols[name]

    def headify(vec, nh, base_row=0):
        a = np.zeros((128, nh), np.float32)
        a[base_row:base_row + 64] = np.asarray(vec, np.float32).reshape(nh, 64).T
        return a

    def fcify(vec):
        v = np.asarray(vec, np.float32)
        nfc = v.size // 128
        return v.reshape(nfc, 128).T.copy()

    add("l0_bq", headify(inputs["L0_attn_bq"], 16))
    add("l0_bk", headify(inputs["L0_attn_bk"], 4))
    add("l0_bv", headify(inputs["L0_attn_bv"], 4))
    hw = np.asarray(inputs["L0_attn_head_weights"], np.float32)
    hw128 = np.zeros((128, 16), np.float32)
    hw128[:64] = hw[None, :]
    add("l0_hw", hw128)
    add("l0_bo", fcify(inputs["L0_attn_bo"]))
    for l in range(4):
        add(f"l1_bq_{l}", headify(inputs["L1_attn_bq"][l], 16, base_row=0))
        add(f"l1_bk_{l}", headify(inputs["L1_attn_bk"][l], 16, base_row=0))
        add(f"l1_bv_{l}", headify(inputs["L1_attn_bv"][l], 16, base_row=0))
        u = np.asarray(inputs["L1_attn_u_bias"][l], np.float32)
        u2 = np.concatenate([u[:, 32:], u[:, :32]], axis=1)
        add(f"l1_u_{l}", headify(u.reshape(-1), 16, base_row=0))
        add(f"l1_u2_{l}", headify(u2.reshape(-1), 16, base_row=0))
        add(f"l1_bo_{l}", fcify(inputs["L1_attn_bo"][l]))
        add(f"l1_b1_{l}", fcify(inputs["L1_attn_b1"][l]))
        add(f"l1_b2_{l}", fcify(inputs["L1_attn_b2"][l]))
        add(f"l1_ln1w_{l}", fcify(inputs["L1_attn_ln1_w"][l]))
        add(f"l1_ln1b_{l}", fcify(inputs["L1_attn_ln1_b"][l]))
        add(f"l1_ln2w_{l}", fcify(inputs["L1_attn_ln2_w"][l]))
        add(f"l1_ln2b_{l}", fcify(inputs["L1_attn_ln2_b"][l]))
    add("d_b1", fcify(inputs["L1_ffn_b1"]))
    add("d_b2", fcify(inputs["L1_ffn_b2"]))
    pv = np.stack(buf, axis=1).astype(np.float32)
    g["pvec"] = np.ascontiguousarray(pv)
    g.update({k: np.ascontiguousarray(v) for k, v in host_tables().items()})
    return g, cols



class Body:
    def __init__(self, nc, names=("d", "d2", "d3", "g", "a", "c1", "c2", "c3", "c4", "c5")):
        self.nc = nc
        self.names = names
        self.sems = {n: nc.alloc_semaphore("bsem_" + n) for n in names}

    def __getitem__(self, n):
        return self.sems[n]

    def close_body(self):
        nc = self.nc
        nc.all_engine_barrier()
        for h in self.sems.values():
            nc.gpsimd.sem_clear(h)
        nc.all_engine_barrier()


class Ctr:
    def __init__(self, bd):
        self.bd = bd
        self.n = {k: 0 for k in bd.names}

    def inc(self, ins, s, k=1):
        self.n[s] += k
        ins.then_inc(self.bd[s], k)
        return self.n[s]

    def wait(self, eng, s, v=None):
        eng.wait_ge(self.bd[s], self.n[s] if v is None else v)


class NS:
    def __init__(self, **kw):
        self.__dict__.update(kw)


def close(E):
    E.bd.close_body()
    E.c = Ctr(E.bd)


def dma(nc, eng, E, sem, out, in_, **kw):
    getattr(nc, eng).dma_start(out=out, in_=in_, **kw).then_inc(E.bd[sem], 16)
    E.c.n[sem] += 16


SMALL_TABLES = {
    "SEL": ([9, 9 * 128], F32),
    "IDENT_BF": ([128, 128], BF16), "IDENT_F32": ([128, 128], F32),
    "P64_F32": ([128, 128], F32), "ID64_BF": ([128, 128], BF16),
    "ONES_1x128_F32": ([1, 128], F32), "ONES_128x1_BF": ([128, 1], BF16),
    "ONES_128x1_F32": ([128, 1], F32),
    "M_B1": ([128, 384], F32), "M_B": ([128, 384], F32),
    "M_A": ([128, 128], F32), "M_DIAG": ([128, 128], F32),
}
BIG_TABLES = {
    "ONES_ROW": ([1, N], F32),
    "C0": ([64, T], F32), "S0P": ([64, T], F32),
    "T1": ([64, T], F32), "T2P": ([64, T], F32), "R2_BF": ([64, T], BF16),
    "M_FULL": ([128, T], F32),
}
TABLE_SPECS = dict(SMALL_TABLES, **BIG_TABLES)

WEIGHT_SPECS = {
    "l0_wq": [D, D], "l0_wk": [D, 256], "l0_wv": [D, 256], "l0_wo": [D, D],
    "router": [D, 8],
    "mw1": [9, D, 2048], "mw3": [9, D, 2048], "mw2": [9, 2048, D],
    "l1_qkvo": [4, 4, D, D], "l1_w1": [4, D, 4096], "l1_w2": [4, 4096, D],
    "d_w1t": [D, 4096], "d_w2t": [4096, D],
}


def make_base(pv_ncol, xa_kind="Internal", with_x=True, xb_kind="Internal",
              xc_kind="Internal", with_out=True, uses=None, big_tabs=None):
    nc = bass.Bass("TRN2", target_bir_lowering=False, debug=False)
    E = NS(nc=nc)
    E.xa = nc.dram_tensor("XA", [FC, 128, N], F32, kind=xa_kind).ap()
    E.xb = nc.dram_tensor("XB", [FC, 128, N], F32, kind=xb_kind).ap()
    E.xc = nc.dram_tensor("XC", [FC, 128, N], F32, kind=xc_kind).ap()
    if with_out:
        E.out = nc.dram_tensor("OUT", [B, T, D], F32, kind="ExternalOutput").ap()
    E.gscr = nc.dram_tensor("GSCR", [128, 2], F32, kind="Internal").ap()
    E.wq2d = nc.dram_tensor("WQ2D", [32, 128, D], BF16, kind="Internal").ap()
    E.ctxd = nc.dram_tensor("CTXD", [16, 64, T], BF16, kind="Internal").ap()
    if with_x:
        E.x_in = nc.dram_tensor("x_in", [B, T, D], F32, kind="ExternalInput").ap()
    for k, shp in WEIGHT_SPECS.items():
        if uses is None or k in uses:
            setattr(E, k, nc.dram_tensor(k, shp, F32, kind="ExternalInput").ap())
    E.pvec_d = nc.dram_tensor("pvec", [128, pv_ncol], F32, kind="ExternalInput").ap()
    E.tab_d = {}
    for k, (shp, dt) in SMALL_TABLES.items():
        E.tab_d[k] = nc.dram_tensor(k, shp, dt, kind="ExternalInput").ap()
    for k in (big_tabs or []):
        shp, dt = BIG_TABLES[k]
        E.tab_d[k] = nc.dram_tensor(k, shp, dt, kind="ExternalInput").ap()

    E.pvec = nc.alloc_sbuf_tensor("pvec_sb", [128, pv_ncol], F32)
    sb = {}
    for k, (shp, dt) in SMALL_TABLES.items():
        sb[k] = nc.alloc_sbuf_tensor(k + "_sb", shp, dt)
    E.ident_bf = sb["IDENT_BF"]; E.ident_f32 = sb["IDENT_F32"]
    E.p64 = sb["P64_F32"]; E.id64_bf = sb["ID64_BF"]
    E.ones_1x128 = sb["ONES_1x128_F32"]; E.ones_128x1_bf = sb["ONES_128x1_BF"]
    E.ones_128x1_f32 = sb["ONES_128x1_F32"]
    E.sel = sb["SEL"]
    E.m_b1 = sb["M_B1"]
    E.m_b = sb["M_B"]; E.m_a = sb["M_A"]; E.m_diag = sb["M_DIAG"]

    E.psA = nc.alloc_psum_tensor("psA", [128, 2048], F32)
    E.psB = nc.alloc_psum_tensor("psB", [128, 2048], F32)
    E.bd = Body(nc)
    E.c = Ctr(E.bd)
    E.uid = 0
    E.cpool = nc.alloc_sbuf_tensor("cpool", [128, 2], F32)
    nc.vector.memset(E.cpool[:, 0:1], EPS_MOD)
    nc.vector.memset(E.cpool[:, 1:2], EPS_L1)
    nc.const_aps.register(F32, EPS_MOD, E.cpool[:, 0:1])
    nc.const_aps.register(F32, EPS_L1, E.cpool[:, 1:2])
    dma(nc, "sync", E, "d", E.pvec[:], E.pvec_d)
    for k in SMALL_TABLES:
        dma(nc, "sync", E, "d", sb[k][:], E.tab_d[k])
    E.c.wait(nc.sync, "d")
    close(E)
    return E


def load_tables(nc, E, st, names):
    out = {}
    E.uid += 1
    for k in names:
        shp, dt = TABLE_SPECS[k]
        h = st.enter_context(nc.sbuf_tensor(k + "_ph%d" % E.uid, shp, dt))
        dma(nc, "sync", E, "d", h[:], E.tab_d[k])
        out[k] = h
    E.c.wait(nc.sync, "d")
    close(E)
    return out



def emit_phase_d(nc, E, st):
    d_src = getattr(E, "d_src", E.xa)
    PV = E.pv
    pvec = E.pvec
    bd, PS_A, PS_B = E.bd, E.psA, E.psB
    outv = E.out.rearrange("b t d -> (b t) d")

    wq1 = st.enter_context(nc.sbuf_tensor("d_wq1", [128, 8 * 4096], BF16))
    rgb = st.enter_context(nc.sbuf_tensor("d_rgb", [128, 2], F32))

    with nc.sbuf_tensor("d_wbuf", [128, 2 * 4096], F32) as wbuf, \
         nc.sbuf_tensor("d_gacc", [128, 2], F32) as gacc, \
         nc.sbuf_tensor("d_gtmp", [128, 2], F32) as gtmp, \
         nc.sbuf_tensor("d_grow", [1, 256], F32) as grow, \
         nc.sbuf_tensor("d_rg", [1, 2], F32) as rg, \
         nc.sbuf_tensor("d_sgn", [128, 4096], BF16) as sgn, \
         nc.sbuf_tensor("d_msk", [128, 4096], BF16) as msk, \
         nc.sbuf_tensor("d_azf", [128, 4096], F32) as azf, \
         nc.sbuf_tensor("d_qot", [128, 4096], BF16) as qot:
        c = E.c
        nc.vector.memset(gacc[:], 0.0).then_inc(bd["c1"], 1)
        c.n["c1"] += 1
        chunks = [(E.d_w1t, kc, 4096, 0) for kc in range(8)] + \
                 [(E.d_w2t, kc, 1024, 1) for kc in range(32)]
        for i, (src, kc, w, col) in enumerate(chunks):
            buf = wbuf[:, (i % 2) * 4096:(i % 2) * 4096 + w]
            psem = ("d", "a")[i % 2]
            if i >= 2:
                c.wait(nc.sync, "c2", 2 * (i - 2) + 1)
            dma(nc, "sync", E, psem, buf, src[ds(kc * 128, 128), :])
            c.wait(nc.vector, psem)
            c.wait(nc.vector, "c1", 1)
            c.wait(nc.vector, "c2", 2 * i)
            nc.vector.reduce_sum(gtmp[:, col:col + 1], buf, axis=AX,
                                 apply_absolute_value=True).then_inc(bd["c2"], 1)
            c.n["c2"] += 1
            c.wait(nc.vector, "c2")
            nc.vector.tensor_tensor(gacc[:, col:col + 1], gacc[:, col:col + 1],
                                    gtmp[:, col:col + 1], ALU.add).then_inc(bd["c2"], 1)
            c.n["c2"] += 1
        c.wait(nc.scalar, "c2")
        dma(nc, "scalar", E, "g", E.gscr, gacc[:])
        c.wait(nc.scalar, "g")
        nc.scalar.dma_start(out=grow[0:1, 0:128],
                            in_=E.gscr[:, 0:1].rearrange("p c -> c p"),
                            allow_slow_non_contiguous=True).then_inc(bd["g"], 16)
        nc.scalar.dma_start(out=grow[0:1, 128:256],
                            in_=E.gscr[:, 1:2].rearrange("p c -> c p"),
                            allow_slow_non_contiguous=True).then_inc(bd["g"], 16)
        c.n["g"] += 32
        c.wait(nc.vector, "g")
        nc.vector.reduce_sum(rg[:, 0:1], grow[0:1, 0:128], axis=AX).then_inc(bd["c3"], 1)
        nc.vector.reduce_sum(rg[:, 1:2], grow[0:1, 128:256], axis=AX).then_inc(bd["c3"], 1)
        c.n["c3"] += 2
        c.wait(nc.vector, "c3")
        nc.vector.tensor_scalar(rg[:], rg[:], 1.0 / (4096.0 * 1024.0), 1e-5,
                                ALU.mult, ALU.add).then_inc(bd["c3"], 1)
        c.n["c3"] += 1
        c.wait(nc.vector, "c3")
        nc.vector.reciprocal(rg[:], rg[:]).then_inc(bd["c3"], 1)
        c.n["c3"] += 1
        c.wait(nc.tensor, "c3")
        nc.tensor.matmul(PS_A[:, 0:2], E.ones_1x128[0:1, :], rg[:, 0:2],
                         start=True, stop=True).then_inc(bd["c4"], 1)
        c.n["c4"] += 1
        c.wait(nc.scalar, "c4")
        nc.scalar.copy(rgb[:], PS_A[:, 0:2]).then_inc(bd["c5"], 1)
        close(E)

        for (src, nkc, w, col, to_sb) in ((E.d_w1t, 8, 4096, 0, True),
                                          (E.d_w2t, 32, 1024, 1, False)):
            c = E.c
            for kc in range(nkc):
                buf = wbuf[:, (kc % 2) * 4096:(kc % 2) * 4096 + w]
                psem = ("d", "a")[kc % 2]
                if kc >= 2:
                    c.wait(nc.sync, "c2", 2 * (kc - 2) + 2)
                dma(nc, "sync", E, psem, buf, src[ds(kc * 128, 128), :])
                c.wait(nc.scalar, psem)
                if kc >= 1:
                    c.wait(nc.scalar, "c1", kc)
                nc.scalar.activation(sgn[:, 0:w], buf, AF.Sign).then_inc(bd["c2"], 1)
                nc.scalar.activation(azf[:, 0:w], buf, AF.Abs,
                                     scale=rgb[:, col:col + 1]).then_inc(bd["c2"], 1)
                c.n["c2"] += 2
                c.wait(nc.vector, "c2")
                nc.vector.tensor_single_scalar(msk[:, 0:w], azf[:, 0:w], 0.5,
                                               ALU.is_ge).then_inc(bd["c3"], 1)
                c.n["c3"] += 1
                c.wait(nc.vector, "c3")
                if to_sb:
                    dst = wq1[:, kc * 4096:(kc + 1) * 4096]
                else:
                    if kc >= 1:
                        c.wait(nc.vector, "g", 16 * kc)
                    dst = qot[:, 0:w]
                ins = nc.vector.tensor_tensor(dst, sgn[:, 0:w], msk[:, 0:w], ALU.mult)
                c.inc(ins, "c1")
                if not to_sb:
                    c.wait(nc.scalar, "c1", kc + 1)
                    nc.scalar.dma_start(
                        out=E.wq2d[ds(kc, 1)].rearrange("o p f -> (o p) f"),
                        in_=qot[:, 0:w]).then_inc(bd["g"], 16)
                    c.n["g"] += 16
            if not to_sb:
                c.wait(nc.sync, "g")
            close(E)

    with nc.sbuf_tensor("d_xf", [128, FC * 512], F32) as xf, \
         nc.sbuf_tensor("d_xbf", [128, FC * 512], BF16) as xbf, \
         nc.sbuf_tensor("d_asb", [128, 32 * 512], BF16) as a_sb, \
         nc.sbuf_tensor("d_w2q", [128, 4 * 1024], BF16) as w2q, \
         nc.sbuf_tensor("d_wq1s", [128, 8 * 128], BF16) as wq1s, \
         nc.sbuf_tensor("d_wq1s2", [128, 8 * 128], BF16) as wq1s2, \
         nc.sbuf_tensor("d_bcol", [128, 2], F32) as bcol, \
         nc.sbuf_tensor("d_yf", [128, FC * 512], F32) as yf, \
         nc.sbuf_tensor("d_tmp", [128, 512], F32) as tmp, \
         nc.sbuf_tensor("d_zbf", [128, 2 * 512], BF16) as zbf, \
         nc.sbuf_tensor("d_zsq", [128, 2 * 512], BF16) as zsq, \
         nc.sbuf_tensor("d_rows", [1, 6 * 512], F32) as rows, \
         nc.sbuf_tensor("d_obuf", [128, 4 * 1024], F32) as obuf:
        with nc.Fori(0, 8) as tc:
            c = E.c
            dma(nc, "sync", E, "d",
                xf[:].rearrange("p (c f) -> p c f", c=FC),
                d_src[:, :, ds(tc * 512, 512)].rearrange("c p f -> p c f"))
            c.wait(nc.vector, "d")
            nc.vector.tensor_copy(xbf[:], xf[:])
            close(E)
            with nc.Fori(0, 16) as mhh:
                c = E.c
                w1v = wq1[:].rearrange("p (kc f) -> p kc f", kc=8)
                i1 = nc.vector.tensor_copy(
                    wq1s[:].rearrange("p (kc f) -> p kc f", kc=8),
                    w1v[:, :, ds(mhh * 256, 128)])
                c.inc(i1, "c1")
                i2 = nc.vector.tensor_copy(
                    wq1s2[:].rearrange("p (kc f) -> p kc f", kc=8),
                    w1v[:, :, ds(mhh * 256 + 128, 128)])
                c.inc(i2, "c1")
                i3 = nc.vector.tensor_copy(bcol[:], pvec[:, ds(PV["d_b1"] + mhh * 2, 2)])
                c.inc(i3, "c1")
                for half, wsrc in ((0, wq1s), (1, wq1s2)):
                    c.wait(nc.tensor, "c1", half + 2)
                    ps = PS_A[:, half * 512:half * 512 + 512]
                    for kc in range(8):
                        ins = nc.tensor.matmul(ps, wsrc[:, kc * 128:(kc + 1) * 128],
                                               xbf[:, kc * 512:(kc + 1) * 512],
                                               start=(kc == 0), stop=(kc == 7))
                    c.inc(ins, "c2")
                    c.wait(nc.scalar, "c2", half + 1)
                    c.wait(nc.scalar, "c1", 3)
                    ins = nc.scalar.activation(
                        a_sb[:, ds((mhh * 2 + half) * 512, 512)], ps, AF.Relu,
                        bias=bcol[:, half:half + 1])
                    c.inc(ins, "c3")
                c.wait(nc.sync, "c3", 2)
                close(E)
            psy = [PS_A[:, i * 512:(i + 1) * 512] for i in range(4)] + \
                  [PS_B[:, i * 512:(i + 1) * 512] for i in range(4)]
            c = E.c
            dma(nc, "sync", E, "d", w2q[:].rearrange("p (c f) -> p c f", c=4),
                E.wq2d[ds(0, 4)].rearrange("c p f -> p c f"))
            c.wait(nc.tensor, "d")
            for ch in range(4):
                for mo in range(8):
                    ins = nc.tensor.matmul(
                        psy[mo], w2q[:, ch * 1024 + mo * 128:ch * 1024 + (mo + 1) * 128],
                        a_sb[:, ch * 512:(ch + 1) * 512], start=(ch == 0), stop=False)
            c.inc(ins, "c2")
            close(E)
            with nc.Fori(1, 7) as my:
                c = E.c
                dma(nc, "sync", E, "d", w2q[:].rearrange("p (c f) -> p c f", c=4),
                    E.wq2d[ds(my * 4, 4)].rearrange("c p f -> p c f"))
                c.wait(nc.tensor, "d")
                for ch in range(4):
                    for mo in range(8):
                        ins = nc.tensor.matmul(
                            psy[mo], w2q[:, ch * 1024 + mo * 128:ch * 1024 + (mo + 1) * 128],
                            a_sb[:, ds(my * 2048 + ch * 512, 512)],
                            start=False, stop=False)
                c.inc(ins, "c2")
                close(E)
            c = E.c
            dma(nc, "sync", E, "d", w2q[:].rearrange("p (c f) -> p c f", c=4),
                E.wq2d[ds(28, 4)].rearrange("c p f -> p c f"))
            c.wait(nc.tensor, "d")
            for ch in range(4):
                for mo in range(8):
                    ins = nc.tensor.matmul(
                        psy[mo], w2q[:, ch * 1024 + mo * 128:ch * 1024 + (mo + 1) * 128],
                        a_sb[:, 28 * 512 + ch * 512:28 * 512 + (ch + 1) * 512],
                        start=False, stop=(ch == 3))
            c.inc(ins, "c2")
            close(E)
            c = E.c
            for mo in range(8):
                ins = nc.scalar.activation(
                    yf[:, mo * 512:(mo + 1) * 512], psy[mo], AF.Identity,
                    bias=pvec[:, PV["d_b2"] + mo:PV["d_b2"] + mo + 1])
                c.inc(ins, "a")
            for mo in range(8):
                c.wait(nc.vector, "a", mo + 1)
                ins = nc.vector.tensor_tensor(yf[:, mo * 512:(mo + 1) * 512],
                                              yf[:, mo * 512:(mo + 1) * 512],
                                              xf[:, mo * 512:(mo + 1) * 512], ALU.add)
                c.inc(ins, "c1")
            for kc in range(8):
                zb = zbf[:, (kc % 2) * 512:(kc % 2) * 512 + 512]
                zq = zsq[:, (kc % 2) * 512:(kc % 2) * 512 + 512]
                c.wait(nc.vector, "c1", kc + 1)
                if kc >= 2:
                    c.wait(nc.vector, "c2", 2 * (kc - 2) + 2)
                i1 = nc.vector.tensor_copy(zb, yf[:, kc * 512:(kc + 1) * 512])
                c.inc(i1, "c1")
                c.wait(nc.scalar, "c1", kc + 1)
                if kc >= 2:
                    c.wait(nc.scalar, "c2", 2 * (kc - 2) + 2)
                i2 = nc.scalar.activation(zq, yf[:, kc * 512:(kc + 1) * 512], AF.Square)
                c.inc(i2, "a")
                c.wait(nc.tensor, "c1", 8 + kc + 1)
                c.wait(nc.tensor, "a", 8 + kc + 1)
                i3 = nc.tensor.matmul(PS_A[0:1, 0:512], E.ones_128x1_bf[:], zb,
                                      start=(kc == 0), stop=(kc == 7))
                i4 = nc.tensor.matmul(PS_A[0:1, 512:1024], E.ones_128x1_bf[:], zq,
                                      start=(kc == 0), stop=(kc == 7))
                c.inc(i3, "c2"); c.inc(i4, "c2")
            m_r = rows[:, 0:512]; q_r = rows[:, 512:1024]; v_r = rows[:, 1024:1536]
            r_r = rows[:, 1536:2048]; cc_r = rows[:, 2048:2560]
            c.wait(nc.vector, "c2", 16)
            iv = nc.vector.tensor_scalar(m_r, PS_A[0:1, 0:512], 1.0 / D, None, ALU.mult)
            c.inc(iv, "c1")
            iv = nc.vector.tensor_scalar(q_r, PS_A[0:1, 512:1024], 1.0 / D, None, ALU.mult)
            c.inc(iv, "c1")
            c.wait(nc.vector, "c1", 18)
            iv = nc.vector.tensor_tensor(v_r, m_r, m_r, ALU.mult)
            c.inc(iv, "c1")
            c.wait(nc.vector, "c1", 19)
            iv = nc.vector.tensor_tensor(v_r, q_r, v_r, ALU.subtract)
            c.inc(iv, "c1")
            c.wait(nc.scalar, "c1", 20)
            iv = nc.scalar.activation(v_r, v_r, AF.Ln, bias=float(EPS_MOD))
            c.inc(iv, "a")
            c.wait(nc.scalar, "a", 17)
            iv = nc.scalar.activation(r_r, v_r, AF.Exp, scale=-0.5)
            c.inc(iv, "a")
            c.wait(nc.vector, "a", 18)
            iv = nc.vector.scalar_tensor_tensor(cc_r, m_r, -1.0, r_r, ALU.mult, ALU.mult)
            c.inc(iv, "c1")
            c.wait(nc.tensor, "c1", 21)
            c.wait(nc.tensor, "a", 18)
            i1 = nc.tensor.matmul(PS_B[:, 0:512], E.ones_1x128[0:1, :], r_r,
                                  start=True, stop=True)
            i2 = nc.tensor.matmul(PS_B[:, 512:1024], E.ones_1x128[0:1, :], cc_r,
                                  start=True, stop=True)
            c.inc(i1, "c2"); c.inc(i2, "c2")
            tpb = [PS_B[:, 1024:1024 + 128], PS_B[:, 1536:1536 + 128]]
            ntr = 0
            for kc in range(8):
                c.wait(nc.vector, "c2", 18)
                c.wait(nc.vector, "c1", 21 + 2 * kc)
                iv = nc.vector.tensor_tensor(tmp[:], yf[:, kc * 512:(kc + 1) * 512],
                                             PS_B[:, 0:512], ALU.mult)
                c.inc(iv, "c1")
                c.wait(nc.vector, "c1", 22 + 2 * kc)
                iv = nc.vector.tensor_tensor(xf[:, kc * 512:(kc + 1) * 512], tmp[:],
                                             PS_B[:, 512:1024], ALU.add)
                c.inc(iv, "c1")
                for tt in range(4):
                    c.wait(nc.tensor, "c1", 23 + 2 * kc)
                    if ntr >= 2:
                        c.wait(nc.tensor, "a", 18 + (ntr - 2) + 1)
                    it = nc.tensor.matmul(
                        tpb[ntr % 2],
                        xf[:, kc * 512 + tt * 128:kc * 512 + (tt + 1) * 128],
                        E.ident_f32[:], start=True, stop=True)
                    c.inc(it, "c2")
                    c.wait(nc.scalar, "c2", 18 + ntr + 1)
                    ie = nc.scalar.copy(
                        obuf[:, tt * 1024 + kc * 128:tt * 1024 + (kc + 1) * 128],
                        tpb[ntr % 2])
                    c.inc(ie, "a")
                    ntr += 1
            c.wait(nc.scalar, "a", 50)
            for tt in range(4):
                nc.scalar.dma_start(
                    out=outv[ds(tc * 512 + tt * 128, 128), :],
                    in_=obuf[:, tt * 1024:(tt + 1) * 1024]).then_inc(bd["g"], 16)
                c.n["g"] += 16
            c.wait(nc.sync, "g")
            close(E)

def build_for_test(phase, pv_cols, pv_ncol, phase_arg=None):
    import contextlib
    E = make_base(pv_ncol, xa_kind="ExternalInput", with_x=False,
                  xb_kind="ExternalOutput" if phase in ("A", "B", "C1") else "Internal",
                  big_tabs=list(BIG_TABLES))
    E.pv = pv_cols
    nc = E.nc
    with contextlib.ExitStack() as st:
        if phase == "D":
            emit_phase_d(nc, E, st)
        elif phase == "A":
            emit_phase_a(nc, E, st)
        elif phase == "B":
            emit_phase_b(nc, E, st)
        elif phase == "C1":
            emit_c_attn(nc, E, st, phase_arg)
        elif phase == "C2":
            emit_c_ffn(nc, E, st, phase_arg)
        nc.all_engine_barrier()
    return E



def emit_input(nc, E, st):
    bd, PS_B = E.bd, E.psB
    xv = E.x_in.rearrange("b t d -> (b t) d")
    with nc.sbuf_tensor("in_xt", [128, D], F32) as xt, \
         nc.sbuf_tensor("in_fm", [128, FC * 128], F32) as fm:
        with nc.Fori(0, 32) as tt:
            c = E.c
            dma(nc, "sync", E, "d", xt[:], xv[ds(tt * 128, 128), :])
            c.wait(nc.tensor, "d")
            for fc in range(FC):
                slot = PS_B[:, (fc % 2) * 512:(fc % 2) * 512 + 128]
                if fc >= 2:
                    c.wait(nc.tensor, "a", fc - 1)
                it = nc.tensor.matmul(slot, xt[:, fc * 128:(fc + 1) * 128],
                                      E.ident_f32[:], start=True, stop=True)
                c.inc(it, "c2")
                c.wait(nc.scalar, "c2", fc + 1)
                ie = nc.scalar.copy(fm[:, fc * 128:(fc + 1) * 128], slot)
                c.inc(ie, "a")
            c.wait(nc.sync, "a", FC)
            dma(nc, "sync", E, "d",
                E.xa[:, :, ds(tt * 128, 128)].rearrange("c p f -> p c f"),
                fm[:].rearrange("p (c f) -> p c f", c=FC))
            c.wait(nc.sync, "d")
            close(E)



def emit_postln(nc, E, st, src_dram, dst_dram, xbf_dst=None, xbf_half=None,
                dst2=None, res_dram=None):
    bd, PS_A, PS_B = E.bd, E.psA, E.psB
    with nc.sbuf_tensor(f"pl_zf_{E.uid}", [128, FC * 512], F32) as zf, \
         nc.sbuf_tensor(f"pl_of_{E.uid}", [128, FC * 512], F32) as of, \
         nc.sbuf_tensor(f"pl_zb_{E.uid}", [128, 2 * 512], BF16) as zbf, \
         nc.sbuf_tensor(f"pl_zq_{E.uid}", [128, 2 * 512], BF16) as zsq, \
         nc.sbuf_tensor(f"pl_tmp_{E.uid}", [128, 512], F32) as tmp, \
         nc.sbuf_tensor(f"pl_rows_{E.uid}", [1, 5 * 512], F32) as rows, \
         nc.sbuf_tensor(f"pl_rsf_{E.uid}", [128, FC * 512 if res_dram is not None else 1], F32) as rsf:
        E.uid += 1
        with nc.Fori(0, 8) as tc:
            c = E.c
            dma(nc, "sync", E, "d", zf[:].rearrange("p (c f) -> p c f", c=FC),
                src_dram[:, :, ds(tc * 512, 512)].rearrange("c p f -> p c f"))
            if res_dram is not None:
                dma(nc, "sync", E, "d2", rsf[:].rearrange("p (c f) -> p c f", c=FC),
                    res_dram[:, :, ds(tc * 512, 512)].rearrange("c p f -> p c f"))
                c.wait(nc.vector, "d")
                c.wait(nc.vector, "d2")
                iv = nc.vector.tensor_tensor(zf[:], zf[:], rsf[:], ALU.add)
                c.inc(iv, "c3")
                c.wait(nc.scalar, "c3", 1)
                c.wait(nc.vector, "c3", 1)
                c.wait(nc.tensor, "c3", 1)
            c.wait(nc.vector, "d")
            c.wait(nc.scalar, "d")
            for kc in range(8):
                zb = zbf[:, (kc % 2) * 512:(kc % 2) * 512 + 512]
                zq = zsq[:, (kc % 2) * 512:(kc % 2) * 512 + 512]
                if kc >= 2:
                    c.wait(nc.vector, "c2", 2 * (kc - 2) + 2)
                    c.wait(nc.scalar, "c2", 2 * (kc - 2) + 2)
                i1 = nc.vector.tensor_copy(zb, zf[:, kc * 512:(kc + 1) * 512])
                c.inc(i1, "c1")
                i2 = nc.scalar.activation(zq, zf[:, kc * 512:(kc + 1) * 512], AF.Square)
                c.inc(i2, "a")
                c.wait(nc.tensor, "c1", kc + 1)
                c.wait(nc.tensor, "a", kc + 1)
                i3 = nc.tensor.matmul(PS_A[0:1, 0:512], E.ones_128x1_bf[:], zb,
                                      start=(kc == 0), stop=(kc == 7))
                i4 = nc.tensor.matmul(PS_A[0:1, 512:1024], E.ones_128x1_bf[:], zq,
                                      start=(kc == 0), stop=(kc == 7))
                c.inc(i3, "c2"); c.inc(i4, "c2")
            m_r = rows[:, 0:512]; q_r = rows[:, 512:1024]; v_r = rows[:, 1024:1536]
            r_r = rows[:, 1536:2048]; cc_r = rows[:, 2048:2560]
            c.wait(nc.vector, "c2", 16)
            iv = nc.vector.tensor_scalar(m_r, PS_A[0:1, 0:512], 1.0 / D, None, ALU.mult)
            c.inc(iv, "c1")
            iv = nc.vector.tensor_scalar(q_r, PS_A[0:1, 512:1024], 1.0 / D, None, ALU.mult)
            c.inc(iv, "c1")
            c.wait(nc.vector, "c1", 10)
            iv = nc.vector.tensor_tensor(v_r, m_r, m_r, ALU.mult)
            c.inc(iv, "c1")
            c.wait(nc.vector, "c1", 11)
            iv = nc.vector.tensor_tensor(v_r, q_r, v_r, ALU.subtract)
            c.inc(iv, "c1")
            c.wait(nc.scalar, "c1", 12)
            iv = nc.scalar.activation(v_r, v_r, AF.Ln, bias=float(EPS_MOD))
            c.inc(iv, "a")
            c.wait(nc.scalar, "a", 9)
            iv = nc.scalar.activation(r_r, v_r, AF.Exp, scale=-0.5)
            c.inc(iv, "a")
            c.wait(nc.vector, "a", 10)
            iv = nc.vector.scalar_tensor_tensor(cc_r, m_r, -1.0, r_r, ALU.mult, ALU.mult)
            c.inc(iv, "c1")
            c.wait(nc.tensor, "c1", 13)
            c.wait(nc.tensor, "a", 10)
            i1 = nc.tensor.matmul(PS_B[:, 0:512], E.ones_1x128[0:1, :], r_r,
                                  start=True, stop=True)
            i2 = nc.tensor.matmul(PS_B[:, 512:1024], E.ones_1x128[0:1, :], cc_r,
                                  start=True, stop=True)
            c.inc(i1, "c2"); c.inc(i2, "c2")
            nv = 13
            for kc in range(8):
                c.wait(nc.vector, "c2", 18)
                c.wait(nc.vector, "c1", nv)
                iv = nc.vector.tensor_tensor(tmp[:], zf[:, kc * 512:(kc + 1) * 512],
                                             PS_B[:, 0:512], ALU.mult)
                c.inc(iv, "c1"); nv += 1
                c.wait(nc.vector, "c1", nv)
                iv = nc.vector.tensor_tensor(of[:, kc * 512:(kc + 1) * 512], tmp[:],
                                             PS_B[:, 512:1024], ALU.add)
                c.inc(iv, "c1"); nv += 1
            if xbf_dst is not None:
                c.wait(nc.vector, "c1", nv)
                assert xbf_half is None
                dst = xbf_dst[:].rearrange("p (c f) -> p c f", c=FC)[:, :, ds(tc * 512, 512)]
                iv = nc.vector.tensor_copy(
                    dst, of[:].rearrange("p (c f) -> p c f", c=FC))
                c.inc(iv, "c1"); nv += 1
            c.wait(nc.sync, "c1", nv)
            dma(nc, "sync", E, "d",
                dst_dram[:, :, ds(tc * 512, 512)].rearrange("c p f -> p c f"),
                of[:].rearrange("p (c f) -> p c f", c=FC))
            if dst2 is not None:
                dma(nc, "sync", E, "d3",
                    dst2[:, :, ds(tc * 512, 512)].rearrange("c p f -> p c f"),
                    of[:].rearrange("p (c f) -> p c f", c=FC))
                c.wait(nc.sync, "d3")
            c.wait(nc.sync, "d")
            close(E)



def emit_phase_a(nc, E, st):
    PV = E.pv
    pvec = E.pvec
    bd, PS_A, PS_B = E.bd, E.psA, E.psB
    tabs = load_tables(nc, E, st, ["C0", "S0P", "M_FULL"])
    C0, S0P, M_FULL = tabs["C0"], tabs["S0P"], tabs["M_FULL"]

    xbf = st.enter_context(nc.sbuf_tensor("a_xbf", [128, FC * T], BF16))
    wstg = st.enter_context(nc.sbuf_tensor("a_wstg", [128, 512], F32))
    ws = st.enter_context(nc.sbuf_tensor("a_ws", [128, 8 * 64], BF16))
    ws2 = st.enter_context(nc.sbuf_tensor("a_ws2", [128, 8 * 64], BF16))
    wos_f = st.enter_context(nc.sbuf_tensor("a_wosf", [64, 16 * 128], F32))
    wos = st.enter_context(nc.sbuf_tensor("a_wos", [64, 16 * 128], BF16))
    bcol = st.enter_context(nc.sbuf_tensor("a_bcol", [128, 2], F32))
    qfp = st.enter_context(nc.sbuf_tensor("a_qfp", [64, T], F32))
    tmpa = st.enter_context(nc.sbuf_tensor("a_tmpa", [64, T], F32))
    tmpb = st.enter_context(nc.sbuf_tensor("a_tmpb", [64, T], F32))
    qp = st.enter_context(nc.sbuf_tensor("a_qp", [64, T], BF16))
    kr = st.enter_context(nc.sbuf_tensor("a_kr", [64, T], BF16))
    vbfh = st.enter_context(nc.sbuf_tensor("a_vbfh", [64, T], BF16))
    vt = st.enter_context(nc.sbuf_tensor("a_vt", [128, 16 * 65], BF16))
    pb0 = st.enter_context(nc.sbuf_tensor("a_pb0", [128, 1024], BF16))
    pb1 = st.enter_context(nc.sbuf_tensor("a_pb1", [128, 1024], BF16))
    pb2 = st.enter_context(nc.sbuf_tensor("a_pb2", [128, 1], BF16))
    cfp = st.enter_context(nc.sbuf_tensor("a_cfp", [64, T], F32))
    rrow = st.enter_context(nc.sbuf_tensor("a_rrow", [1, T], F32))
    ctmp = st.enter_context(nc.sbuf_tensor("a_ctmp", [64, T], BF16))
    ctx_tc = st.enter_context(nc.sbuf_tensor("a_ctxtc", [64, 16 * 512], BF16))
    xres = st.enter_context(nc.sbuf_tensor("a_xres", [128, FC * 512], F32))
    ofp = st.enter_context(nc.sbuf_tensor("a_ofp", [128, FC * 512], F32))

    nc.vector.memset(vt[:].rearrange("p (kt c) -> p kt c", kt=16)[:, :, 64:65], 1.0)
    close(E)

    wqv = E.l0_wq.rearrange("(c p) m -> p c m", p=128)
    wkv = E.l0_wk.rearrange("(c p) m -> p c m", p=128)
    wvv = E.l0_wv.rearrange("(c p) m -> p c m", p=128)
    wov = E.l0_wo.rearrange("(h p) m -> p h m", p=64)

    with nc.Fori(0, B) as b:
        with nc.Fori(0, 4) as tc:
            c = E.c
            dma(nc, "sync", E, "d", xres[:].rearrange("p (c f) -> p c f", c=FC),
                E.xa[:, :, ds(b * T + tc * 512, 512)].rearrange("c p f -> p c f"))
            c.wait(nc.vector, "d")
            iv = nc.vector.tensor_copy(
                xbf[:].rearrange("p (c f) -> p c f", c=FC)[:, :, ds(tc * 512, 512)],
                xres[:].rearrange("p (c f) -> p c f", c=FC))
            c.inc(iv, "c1")
            c.wait(nc.sync, "c1")
            close(E)
        with nc.Fori(0, 4) as g:
            c = E.c
            dma(nc, "sync", E, "d", wstg[:, 0:512].rearrange("p (c f) -> p c f", c=8),
                wkv[:, :, ds(g * 64, 64)])
            c.wait(nc.vector, "d")
            iv = nc.vector.tensor_copy(ws[:], wstg[:, 0:512])
            c.inc(iv, "c1")
            c.wait(nc.sync, "c1", 1)
            dma(nc, "sync", E, "d2", wstg[:, 0:512].rearrange("p (c f) -> p c f", c=8),
                wvv[:, :, ds(g * 64, 64)])
            c.wait(nc.vector, "d2")
            c.wait(nc.vector, "c1", 1)
            iv = nc.vector.tensor_copy(ws2[:], wstg[:, 0:512])
            c.inc(iv, "c1")
            iv = nc.vector.tensor_copy(bcol[:, 0:1], pvec[:, ds(PV["l0_bk"] + g, 1)])
            c.inc(iv, "c1")
            iv = nc.vector.tensor_copy(bcol[:, 1:2], pvec[:, ds(PV["l0_bv"] + g, 1)])
            c.inc(iv, "c1")
            c.wait(nc.tensor, "c1", 2)
            for tc in range(4):
                for kc in range(8):
                    ins = nc.tensor.matmul(
                        PS_A[0:64, tc * 512:(tc + 1) * 512],
                        ws[:, kc * 64:(kc + 1) * 64],
                        xbf[:, kc * T + tc * 512:kc * T + (tc + 1) * 512],
                        start=(kc == 0), stop=(kc == 7))
                c.inc(ins, "c2")
            c.wait(nc.scalar, "c1", 4)
            for tc in range(4):
                c.wait(nc.scalar, "c2", tc + 1)
                ie = nc.scalar.activation(qfp[:, tc * 512:(tc + 1) * 512],
                                          PS_A[0:64, tc * 512:(tc + 1) * 512],
                                          AF.Identity, bias=bcol[0:64, 0:1])
                c.inc(ie, "a")
            c.wait(nc.tensor, "a", 4)
            for tc in range(4):
                ins = nc.tensor.matmul(PS_B[0:64, tc * 512:(tc + 1) * 512],
                                       E.p64[0:64, 0:64], qfp[:, tc * 512:(tc + 1) * 512],
                                       start=True, stop=True)
                c.inc(ins, "c4")
            c.wait(nc.vector, "a", 4)
            iv = nc.vector.tensor_tensor(tmpa[:], qfp[:], C0[:], ALU.mult)
            c.inc(iv, "c3")
            c.wait(nc.vector, "c4", 4)
            iv = nc.vector.tensor_tensor(tmpb[:], PS_B[0:64, 0:T], S0P[:], ALU.mult)
            c.inc(iv, "c3")
            c.wait(nc.vector, "c3", 2)
            iv = nc.vector.tensor_tensor(kr[:], tmpa[:], tmpb[:], ALU.add)
            c.inc(iv, "c3")
            c.wait(nc.tensor, "c3", 3)
            for tc in range(4):
                for kc in range(8):
                    ins = nc.tensor.matmul(
                        PS_A[0:64, tc * 512:(tc + 1) * 512],
                        ws2[:, kc * 64:(kc + 1) * 64],
                        xbf[:, kc * T + tc * 512:kc * T + (tc + 1) * 512],
                        start=(kc == 0), stop=(kc == 7))
                c.inc(ins, "c2")
            for tc in range(4):
                c.wait(nc.scalar, "c2", 4 + tc + 1)
                ie = nc.scalar.activation(vbfh[:, tc * 512:(tc + 1) * 512],
                                          PS_A[0:64, tc * 512:(tc + 1) * 512],
                                          AF.Identity, bias=bcol[0:64, 1:2])
                c.inc(ie, "a")
            for kt in range(16):
                c.wait(nc.tensor, "a", 4 + min(4, (kt // 4) + 1))
                if kt >= 2:
                    c.wait(nc.tensor, "a", 8 + (kt - 2) + 1)
                slot = PS_B[:, 1024 + (kt % 2) * 512:1024 + (kt % 2) * 512 + 64]
                ins = nc.tensor.matmul(slot, vbfh[:, kt * 128:(kt + 1) * 128],
                                       E.ident_bf[0:64, 0:64], start=True, stop=True)
                c.inc(ins, "c4")
                c.wait(nc.scalar, "c4", 4 + kt + 1)
                ie = nc.scalar.copy(vt[:, kt * 65:kt * 65 + 64], slot)
                c.inc(ie, "a")
            close(E)
            with nc.Fori(0, 4) as hh:
                h = g * 4 + hh
                c = E.c
                dma(nc, "sync", E, "d", wstg[:, 0:512].rearrange("p (c f) -> p c f", c=8),
                    wqv[:, :, ds(h * 64, 64)])
                c.wait(nc.vector, "d")
                iv = nc.vector.tensor_copy(ws[:], wstg[:, 0:512])
                c.inc(iv, "c1")
                iv = nc.vector.tensor_copy(bcol[:, 0:1], pvec[:, ds(PV["l0_bq"] + h, 1)])
                c.inc(iv, "c1")
                iv = nc.vector.tensor_copy(bcol[:, 1:2], pvec[:, ds(PV["l0_hw"] + h, 1)])
                c.inc(iv, "c1")
                c.wait(nc.tensor, "c1", 1)
                for tc in range(4):
                    for kc in range(8):
                        ins = nc.tensor.matmul(
                            PS_A[0:64, tc * 512:(tc + 1) * 512],
                            ws[:, kc * 64:(kc + 1) * 64],
                            xbf[:, kc * T + tc * 512:kc * T + (tc + 1) * 512],
                            start=(kc == 0), stop=(kc == 7))
                    c.inc(ins, "c2")
                c.wait(nc.scalar, "c1", 2)
                for tc in range(4):
                    c.wait(nc.scalar, "c2", tc + 1)
                    ie = nc.scalar.activation(qfp[:, tc * 512:(tc + 1) * 512],
                                              PS_A[0:64, tc * 512:(tc + 1) * 512],
                                              AF.Identity, bias=bcol[0:64, 0:1])
                    c.inc(ie, "a")
                c.wait(nc.tensor, "a", 4)
                for tc in range(4):
                    ins = nc.tensor.matmul(PS_B[0:64, tc * 512:(tc + 1) * 512],
                                           E.p64[0:64, 0:64],
                                           qfp[:, tc * 512:(tc + 1) * 512],
                                           start=True, stop=True)
                    c.inc(ins, "c4")
                c.wait(nc.vector, "a", 4)
                iv = nc.vector.tensor_tensor(tmpa[:], qfp[:], C0[:], ALU.mult)
                c.inc(iv, "c3")
                c.wait(nc.vector, "c4", 4)
                iv = nc.vector.tensor_tensor(tmpb[:], PS_B[0:64, 0:T], S0P[:], ALU.mult)
                c.inc(iv, "c3")
                c.wait(nc.vector, "c3", 2)
                iv = nc.vector.tensor_tensor(tmpa[:], tmpa[:], tmpb[:], ALU.add)
                c.inc(iv, "c3")
                c.wait(nc.vector, "c3", 3)
                c.wait(nc.vector, "c1", 3)
                iv = nc.vector.tensor_scalar_mul(qp[:], tmpa[:], bcol[0:64, 1:2])
                c.inc(iv, "c3")
                close(E)
                c = E.c
                na = 0
                nm = 0
                nmm = 0
                ctx_open = {}
                plan = []
                for kt in range(16):
                    if kt == 0:
                        plan.append((0, [(0, T, "full")]))
                    elif kt == 1:
                        plan.append((1, [(0, 384, "b1")]))
                    else:
                        w = 256 if kt == 15 else 384
                        plan.append((kt, [(0, 128, "qa"), (128 * (kt - 1), w, "b")]))
                bank_last = {}
                for kt, pieces in plan:
                    for (q0, w, mk) in pieces:
                        for j in range(0, w, 512):
                            seg = min(512, w - j)
                            b0 = (q0 + j) // 512
                            bank_last[(kt, q0 + j)] = None
                touch = {}
                for kt, pieces in plan:
                    for (q0, w, mk) in pieces:
                        lo = q0 // 512
                        hi = (q0 + w - 1) // 512
                        for bk in range(lo, hi + 1):
                            touch[bk] = (kt, q0)
                pbufs = [pb0, pb1, pb2]
                nch = 0
                for kt, pieces in plan:
                    for (q0, w, mk) in pieces:
                        jj0 = 0
                        while jj0 < w:
                            chunk = min(1024, w - jj0)
                            qa0 = q0 + jj0
                            reg = PS_A[:, (nch % 2) * 1024:(nch % 2) * 1024 + 1024]
                            if nch >= 2:
                                c.wait(nc.tensor, "a", nch - 1)
                            kk = 0
                            while kk < chunk:
                                seg = min(512, chunk - kk)
                                ins = nc.tensor.matmul(
                                    reg[:, kk:kk + seg],
                                    kr[:, kt * 128:(kt + 1) * 128],
                                    qp[:, qa0 + kk:qa0 + kk + seg],
                                    start=True, stop=True)
                                c.inc(ins, "c2"); nmm += 1
                                kk += seg
                            c.wait(nc.vector, "c2", c.n["c2"])
                            if mk == "full":
                                msrc = M_FULL[:, jj0:jj0 + chunk]
                            elif mk == "b1":
                                msrc = E.m_b1[:, jj0:jj0 + chunk]
                            elif mk == "qa":
                                msrc = E.m_a[:, 0:chunk]
                            else:
                                msrc = E.m_b[:, jj0:jj0 + chunk]
                            iv = nc.vector.tensor_tensor(reg[:, 0:chunk],
                                                         reg[:, 0:chunk], msrc, ALU.add)
                            c.inc(iv, "c1"); nm += 1
                            pbuf = pbufs[nch % 2]
                            c.wait(nc.scalar, "c1", nm)
                            c.wait(nc.scalar, "c2", c.n["c2"])
                            ie = nc.scalar.activation(pbuf[:, 0:chunk], reg[:, 0:chunk],
                                                      AF.Exp, scale=0.125)
                            c.inc(ie, "a"); na += 1; nch += 1
                            c.wait(nc.tensor, "a", na)
                            kk = 0
                            while kk < chunk:
                                qabs = qa0 + kk
                                bank = qabs // 512
                                seg = min(512 - (qabs % 512), chunk - kk)
                                stt_flag = bank not in ctx_open
                                ctx_open[bank] = True
                                ins = nc.tensor.matmul(
                                    PS_B[0:65, qabs:qabs + seg],
                                    vt[:, kt * 65:(kt + 1) * 65],
                                    pbuf[:, kk:kk + seg],
                                    start=stt_flag, stop=(touch[bank] == (kt, q0)))
                                c.inc(ins, "c2")
                                kk += seg
                            jj0 += chunk
                c.wait(nc.vector, "c2", c.n["c2"])
                iv = nc.vector.reciprocal(rrow[:], PS_B[64:65, 0:T])
                c.inc(iv, "c1"); nm += 1
                c.wait(nc.tensor, "c1", nm)
                for tc in range(4):
                    ins = nc.tensor.matmul(PS_A[0:64, tc * 512:(tc + 1) * 512],
                                           E.ones_1x128[0:1, 0:64],
                                           rrow[:, tc * 512:(tc + 1) * 512],
                                           start=True, stop=True)
                    c.inc(ins, "c4")
                c.wait(nc.scalar, "c2", c.n["c2"])
                ie = nc.scalar.copy(cfp[:], PS_B[0:64, 0:T])
                c.inc(ie, "a"); na += 1
                c.wait(nc.vector, "a", na)
                c.wait(nc.vector, "c4", 4)
                iv = nc.vector.tensor_tensor(ctmp[:], cfp[:], PS_A[0:64, 0:T], ALU.mult)
                c.inc(iv, "c1")
                c.wait(nc.scalar, "c1", c.n["c1"])
                nc.scalar.dma_start(
                    out=E.ctxd[ds(h, 1)].rearrange("o p f -> (o p) f"),
                    in_=ctmp[:]).then_inc(bd["g"], 16)
                c.n["g"] += 16
                c.wait(nc.sync, "g")
                close(E)
        with nc.Fori(0, 4) as tc:
            c = E.c
            dma(nc, "sync", E, "d", ctx_tc[:].rearrange("p (h f) -> p h f", h=16),
                E.ctxd[:, :, ds(tc * 512, 512)].rearrange("h p f -> p h f"))
            dma(nc, "sync", E, "d", xres[:].rearrange("p (c f) -> p c f", c=FC),
                E.xa[:, :, ds(b * T + tc * 512, 512)].rearrange("c p f -> p c f"))
            for m in range(8):
                psem = ("d2", "d3")[m % 2]
                if m >= 1:
                    c.wait(nc.sync, "c1", m)
                dma(nc, "sync", E, psem, wos_f[:].rearrange("p (h f) -> p h f", h=16),
                    wov[:, :, ds(m * 128, 128)])
                c.wait(nc.vector, psem)
                if m >= 1:
                    c.wait(nc.vector, "c2", m)
                iv = nc.vector.tensor_copy(wos[:], wos_f[:])
                c.inc(iv, "c1")
                c.wait(nc.tensor, "c1", m + 1)
                c.wait(nc.tensor, "d", 32)
                if m >= 2:
                    c.wait(nc.tensor, "a", m - 1)
                pbank = PS_A[:, (m % 2) * 512:(m % 2) * 512 + 512]
                for hx in range(16):
                    ins = nc.tensor.matmul(
                        pbank, wos[0:64, hx * 128:(hx + 1) * 128],
                        ctx_tc[0:64, hx * 512:(hx + 1) * 512],
                        start=(hx == 0), stop=(hx == 15))
                c.inc(ins, "c2")
                c.wait(nc.scalar, "c2", m + 1)
                ie = nc.scalar.activation(ofp[:, m * 512:(m + 1) * 512], pbank,
                                          AF.Identity,
                                          bias=pvec[:, PV["l0_bo"] + m:PV["l0_bo"] + m + 1])
                c.inc(ie, "a")
                c.wait(nc.vector, "a", m + 1)
                iv = nc.vector.tensor_tensor(ofp[:, m * 512:(m + 1) * 512],
                                             ofp[:, m * 512:(m + 1) * 512],
                                             xres[:, m * 512:(m + 1) * 512], ALU.add)
                c.inc(iv, "c3")
            c.wait(nc.sync, "c3", 8)
            dma(nc, "sync", E, "d",
                E.xb[:, :, ds(b * T + tc * 512, 512)].rearrange("c p f -> p c f"),
                ofp[:].rearrange("p (c f) -> p c f", c=FC))
            c.wait(nc.sync, "d")
            close(E)



def emit_phase_b(nc, E, st):
    PV = E.pv
    bd, PS_A, PS_B = E.bd, E.psA, E.psB

    stg2 = st.enter_context(nc.sbuf_tensor("b_stg2", [128, 8192], F32))
    xq = st.enter_context(nc.sbuf_tensor("b_xq", [128, 8 * 1024], BF16))
    oacc = st.enter_context(nc.sbuf_tensor("b_oacc", [128, 8 * 1024], F32))
    w1b = st.enter_context(nc.sbuf_tensor("b_w1b", [128, 8 * 1024], BF16))
    w3b = st.enter_context(nc.sbuf_tensor("b_w3b", [128, 8 * 1024], BF16))
    w2b = st.enter_context(nc.sbuf_tensor("b_w2b", [128, 8 * 1024], BF16))
    a_sb = st.enter_context(nc.sbuf_tensor("b_asb", [128, 8 * 512], BF16))
    sfp = st.enter_context(nc.sbuf_tensor("b_sfp", [128, 512], F32))
    t2f = st.enter_context(nc.sbuf_tensor("b_t2f", [128, 512], F32))
    cwb = st.enter_context(nc.sbuf_tensor("b_cwb", [128, 1024], F32))
    sel_s = st.enter_context(nc.sbuf_tensor("b_sel", [9, 128], F32))
    rbf = st.enter_context(nc.sbuf_tensor("b_rbf", [128, 64], BF16))
    lrow = st.enter_context(nc.sbuf_tensor("b_lrow", [8, 512], F32))
    ltok = st.enter_context(nc.sbuf_tensor("b_ltok", [128, 4 * 8], F32))
    prb = st.enter_context(nc.sbuf_tensor("b_prb", [128, 8], F32))
    tmp8 = st.enter_context(nc.sbuf_tensor("b_tmp8", [128, 8], F32))
    cw_tok = st.enter_context(nc.sbuf_tensor("b_cwtok", [128, 8], F32))
    sc1 = st.enter_context(nc.sbuf_tensor("b_sc1", [128, 6], F32))
    cwt = st.enter_context(nc.sbuf_tensor("b_cwt", [9, N], F32))
    xres = st.enter_context(nc.sbuf_tensor("b_xres", [128, FC * 512], F32))

    c = E.c
    dma(nc, "sync", E, "d", stg2[:, 0:64].rearrange("p (c f) -> p c f", c=8),
        E.router.rearrange("(c p) f -> p c f", p=128))
    c.wait(nc.vector, "d")
    iv = nc.vector.tensor_copy(rbf[:], stg2[:, 0:64])
    c.inc(iv, "c1")
    dma(nc, "sync", E, "d2", cwt[8:9, :], E.tab_d["ONES_ROW"])
    c.wait(nc.sync, "d2")
    c.wait(nc.sync, "c1", 1)
    close(E)

    with nc.Fori(0, 8) as tc:
        c = E.c
        dma(nc, "sync", E, "d",
            stg2[:, 0:4096].rearrange("p (c f) -> p c f", c=FC)[:, :, 0:512],
            E.xa[:, :, ds(tc * 512, 512)].rearrange("c p f -> p c f"))
        c.wait(nc.vector, "d")
        iv = nc.vector.tensor_copy(xq[:, 0:4096].rearrange("p (c f) -> p c f", c=8),
                                   stg2[:, 0:4096].rearrange("p (c f) -> p c f", c=FC)[:, :, 0:512])
        c.inc(iv, "c1")
        c.wait(nc.tensor, "c1", 1)
        for kc in range(8):
            ins = nc.tensor.matmul(PS_A[0:8, 0:512], rbf[:, kc * 8:(kc + 1) * 8],
                                   xq[:, kc * 512:(kc + 1) * 512],
                                   start=(kc == 0), stop=(kc == 7))
        c.inc(ins, "c2")
        c.wait(nc.scalar, "c2", 1)
        ie = nc.scalar.copy(lrow[:], PS_A[0:8, 0:512])
        c.inc(ie, "a")
        n1, na, n2 = 1, 1, 1
        for tt in range(4):
            c.wait(nc.tensor, "a", 1)
            it = nc.tensor.matmul(PS_B[:, tt * 512:tt * 512 + 8],
                                  lrow[0:8, tt * 128:(tt + 1) * 128],
                                  E.ident_f32[0:8, 0:8], start=True, stop=True)
            c.inc(it, "c2"); n2 += 1
            c.wait(nc.scalar, "c2", n2)
            ie = nc.scalar.copy(ltok[:, tt * 8:(tt + 1) * 8], PS_B[:, tt * 512:tt * 512 + 8])
            c.inc(ie, "a"); na += 1
            lt = ltok[:, tt * 8:(tt + 1) * 8]
            c.wait(nc.vector, "a", na)
            iv = nc.vector.reduce_max(sc1[:, 0:1], lt, axis=AX, negate=True)
            c.inc(iv, "c1"); n1 += 1
            c.wait(nc.scalar, "c1", n1)
            ie = nc.scalar.activation(prb[:], lt, AF.Exp, bias=sc1[:, 0:1],
                                      accum_out=sc1[:, 1:2])
            c.inc(ie, "a"); na += 1
            c.wait(nc.vector, "a", na)
            iv = nc.vector.reciprocal(sc1[:, 2:3], sc1[:, 1:2])
            c.inc(iv, "c1"); n1 += 1
            c.wait(nc.vector, "c1", n1)
            iv = nc.vector.tensor_scalar_mul(prb[:], prb[:], sc1[:, 2:3])
            c.inc(iv, "c1"); n1 += 1
            c.wait(nc.vector, "c1", n1)
            iv = nc.vector.reduce_max(sc1[:, 3:4], prb[:], axis=AX)
            c.inc(iv, "c1"); n1 += 1
            c.wait(nc.vector, "c1", n1)
            iv = nc.vector.tensor_single_scalar(cw_tok[:], prb[:], sc1[:, 3:4], ALU.is_ge)
            c.inc(iv, "c1"); n1 += 1
            c.wait(nc.vector, "c1", n1)
            iv = nc.vector.scalar_tensor_tensor(tmp8[:], cw_tok[:], -1e3, prb[:],
                                                ALU.mult, ALU.add)
            c.inc(iv, "c1"); n1 += 1
            c.wait(nc.vector, "c1", n1)
            iv = nc.vector.reduce_max(sc1[:, 4:5], tmp8[:], axis=AX)
            c.inc(iv, "c1"); n1 += 1
            c.wait(nc.vector, "c1", n1)
            iv = nc.vector.tensor_single_scalar(cw_tok[:], prb[:], sc1[:, 4:5], ALU.is_ge)
            c.inc(iv, "c1"); n1 += 1
            c.wait(nc.vector, "c1", n1)
            iv = nc.vector.tensor_tensor(sc1[:, 5:6], sc1[:, 3:4], sc1[:, 4:5], ALU.add)
            c.inc(iv, "c1"); n1 += 1
            c.wait(nc.vector, "c1", n1)
            iv = nc.vector.reciprocal(sc1[:, 5:6], sc1[:, 5:6])
            c.inc(iv, "c1"); n1 += 1
            c.wait(nc.vector, "c1", n1)
            iv = nc.vector.tensor_tensor(cw_tok[:], cw_tok[:], prb[:], ALU.mult)
            c.inc(iv, "c1"); n1 += 1
            c.wait(nc.vector, "c1", n1)
            iv = nc.vector.tensor_scalar_mul(cw_tok[:], cw_tok[:], sc1[:, 5:6])
            c.inc(iv, "c1"); n1 += 1
            c.wait(nc.tensor, "c1", n1)
            if tt >= 1:
                c.wait(nc.tensor, "a", na)
            it = nc.tensor.matmul(PS_B[0:8, 1024 + tt * 128:1024 + (tt + 1) * 128],
                                  cw_tok[:], E.ident_f32[:], start=True, stop=True)
            c.inc(it, "c2"); n2 += 1
            c.wait(nc.scalar, "c2", n2)
            ie = nc.scalar.copy(cwt[0:8, ds(tc * 512 + tt * 128, 128)],
                                PS_B[0:8, 1024 + tt * 128:1024 + (tt + 1) * 128])
            c.inc(ie, "a"); na += 1
        c.wait(nc.sync, "a", na)
        close(E)

    with nc.Fori(0, 4) as qi:
        c = E.c
        for r in range(2):
            dma(nc, "sync", E, ("d", "d2")[r],
                stg2[:, 0:4096].rearrange("p (c f) -> p c f", c=FC)[:, :, 0:512],
                E.xa[:, :, ds(qi * 1024 + r * 512, 512)].rearrange("c p f -> p c f"))
            c.wait(nc.vector, ("d", "d2")[r])
            if r == 1:
                c.wait(nc.vector, "c1", 1)
            iv = nc.vector.tensor_copy(
                xq[:].rearrange("p (c f) -> p c f", c=8)[:, :, r * 512:(r + 1) * 512],
                stg2[:, 0:4096].rearrange("p (c f) -> p c f", c=FC)[:, :, 0:512])
            c.inc(iv, "c1")
            if r == 0:
                c.wait(nc.sync, "c1", 1)
        iv = nc.vector.memset(oacc[:], 0.0)
        c.inc(iv, "c1")
        c.wait(nc.sync, "c1", 3)
        close(E)
        with nc.Fori(0, 9) as e:
            c = E.c
            iv = nc.vector.tensor_copy(sel_s[:], E.sel[:, ds(e * 128, 128)])
            c.inc(iv, "c1")
            c.wait(nc.tensor, "c1", 1)
            for half in range(2):
                it = nc.tensor.matmul(PS_B[:, 1024 + half * 512:1024 + (half + 1) * 512],
                                      sel_s[:], cwt[0:9, ds(qi * 1024 + half * 512, 512)],
                                      start=True, stop=True)
                c.inc(it, "c2")
            c.wait(nc.scalar, "c2", 2)
            ie = nc.scalar.copy(cwb[:], PS_B[:, 1024:2048])
            c.inc(ie, "a")
            c.wait(nc.sync, "a", 1)
            close(E)
            with nc.Fori(0, 2) as hh:
                c = E.c
                for wi, (wsrc, wdst) in enumerate(((E.mw1, w1b), (E.mw3, w3b), (E.mw2, w2b))):
                    psem = ("d", "d2", "d3")[wi]
                    sbuf_half = stg2[:]
                    if wi >= 1:
                        c.wait(nc.sync, "c1", wi)
                    if wi < 2:
                        srcap = wsrc[ds(e, 1), :, ds(hh * 1024, 1024)]
                    else:
                        srcap = wsrc[ds(e, 1), ds(hh * 1024, 1024), :]
                    dma(nc, "sync", E, psem,
                        sbuf_half.rearrange("p (c f) -> p c f", c=8),
                        srcap.rearrange("o (kc p) f -> p (o kc) f", p=128))
                    c.wait(nc.vector, psem)
                    iv = nc.vector.tensor_copy(
                        wdst[:].rearrange("p (c f) -> p c f", c=8),
                        sbuf_half.rearrange("p (c f) -> p c f", c=8))
                    c.inc(iv, "c1")
                c.wait(nc.sync, "c1", 3)
                close(E)
                c = E.c
                n1 = n2 = na = n3 = 0
                for tc in range(2):
                    for mh in range(8):
                        psh1 = PS_A[:, (mh % 2) * 512:(mh % 2) * 512 + 512]
                        psh3 = PS_A[:, 1024 + (mh % 2) * 512:1024 + (mh % 2) * 512 + 512]
                        if (tc, mh) > (0, 1):
                            nprev = (tc * 8 + mh) - 2
                            c.wait(nc.tensor, "a", nprev + 1)
                            c.wait(nc.tensor, "c1", 2 * nprev + 2)
                        for kc in range(8):
                            i1 = nc.tensor.matmul(psh1, w1b[:, kc * 1024 + mh * 128:kc * 1024 + (mh + 1) * 128],
                                                  xq[:, kc * 1024 + tc * 512:kc * 1024 + tc * 512 + 512],
                                                  start=(kc == 0), stop=(kc == 7))
                        c.inc(i1, "c2"); n2 += 1
                        for kc in range(8):
                            i1 = nc.tensor.matmul(psh3, w3b[:, kc * 1024 + mh * 128:kc * 1024 + (mh + 1) * 128],
                                                  xq[:, kc * 1024 + tc * 512:kc * 1024 + tc * 512 + 512],
                                                  start=(kc == 0), stop=(kc == 7))
                        c.inc(i1, "c2"); n2 += 1
                        c.wait(nc.scalar, "c2", n2 - 1)
                        if n1 >= 1:
                            c.wait(nc.scalar, "c1", n1 - 1)
                        ie = nc.scalar.activation(sfp[:], psh1, AF.Silu)
                        c.inc(ie, "a"); na += 1
                        c.wait(nc.vector, "a", na)
                        c.wait(nc.vector, "c2", n2)
                        c.wait(nc.vector, "c1", n1)
                        iv = nc.vector.tensor_tensor(t2f[:], sfp[:], psh3, ALU.mult)
                        c.inc(iv, "c1"); n1 += 1
                        c.wait(nc.vector, "c1", n1)
                        iv = nc.vector.tensor_tensor(
                            a_sb[:, mh * 512:(mh + 1) * 512], t2f[:],
                            cwb[:, tc * 512:(tc + 1) * 512], ALU.mult)
                        c.inc(iv, "c1"); n1 += 1
                    for mo in range(8):
                        psy = PS_B[:, (mo % 2) * 512:(mo % 2) * 512 + 512]
                        c.wait(nc.tensor, "c1", n1)
                        if mo >= 2:
                            c.wait(nc.tensor, "c3", n3 - 1)
                        for kch in range(8):
                            i1 = nc.tensor.matmul(psy, w2b[:, kch * 1024 + mo * 128:kch * 1024 + (mo + 1) * 128],
                                                  a_sb[:, kch * 512:(kch + 1) * 512],
                                                  start=(kch == 0), stop=(kch == 7))
                        c.inc(i1, "c2"); n2 += 1
                        c.wait(nc.vector, "c2", n2)
                        iv = nc.vector.tensor_tensor(
                            oacc[:, mo * 1024 + tc * 512:mo * 1024 + tc * 512 + 512],
                            oacc[:, mo * 1024 + tc * 512:mo * 1024 + tc * 512 + 512],
                            psy, ALU.add)
                        c.inc(iv, "c3"); n3 += 1
                c.wait(nc.sync, "c3", n3)
                close(E)
        c = E.c
        for r in range(2):
            dma(nc, "sync", E, ("d", "d2")[r],
                xres[:].rearrange("p (c f) -> p c f", c=FC),
                E.xa[:, :, ds(qi * 1024 + r * 512, 512)].rearrange("c p f -> p c f"))
            c.wait(nc.vector, ("d", "d2")[r])
            for mo in range(8):
                iv = nc.vector.tensor_tensor(
                    oacc[:, mo * 1024 + r * 512:mo * 1024 + r * 512 + 512],
                    oacc[:, mo * 1024 + r * 512:mo * 1024 + r * 512 + 512],
                    xres[:, mo * 512:(mo + 1) * 512], ALU.add)
                c.inc(iv, "c1")
            c.wait(nc.sync, "c1")
            dma(nc, "sync", E, "d3",
                E.xb[:, :, ds(qi * 1024 + r * 512, 512)].rearrange("c p f -> p c f"),
                oacc[:].rearrange("p (c f) -> p c f", c=8)[:, :, r * 512:(r + 1) * 512])
        c.wait(nc.sync, "d")
        c.wait(nc.sync, "d2")
        c.wait(nc.sync, "d3")
        close(E)



def c_ln_pass(nc, E, src_dram, xnbf, lnsta, woff, boff, tag):
    bd, PS_A, PS_B = E.bd, E.psA, E.psB
    import contextlib
    with contextlib.ExitStack() as stl:
        zf = stl.enter_context(nc.sbuf_tensor(f"cl_zf_{tag}", [128, FC * 512], F32))
        zbf = stl.enter_context(nc.sbuf_tensor(f"cl_zb_{tag}", [128, 2 * 512], BF16))
        zsq = stl.enter_context(nc.sbuf_tensor(f"cl_zq_{tag}", [128, 2 * 512], BF16))
        tmp = stl.enter_context(nc.sbuf_tensor(f"cl_tmp_{tag}", [128, 512], F32))
        tmp2 = stl.enter_context(nc.sbuf_tensor(f"cl_tmp2_{tag}", [128, 512], F32))
        rows = stl.enter_context(nc.sbuf_tensor(f"cl_rows_{tag}", [1, 5 * 512], F32))
        with nc.Fori(0, 8) as tc:
            c = E.c
            dma(nc, "sync", E, "d", zf[:].rearrange("p (c f) -> p c f", c=FC),
                src_dram[:, :, ds(tc * 512, 512)].rearrange("c p f -> p c f"))
            c.wait(nc.vector, "d")
            c.wait(nc.scalar, "d")
            for kc in range(8):
                zb = zbf[:, (kc % 2) * 512:(kc % 2) * 512 + 512]
                zq = zsq[:, (kc % 2) * 512:(kc % 2) * 512 + 512]
                if kc >= 2:
                    c.wait(nc.vector, "c2", 2 * (kc - 2) + 2)
                    c.wait(nc.scalar, "c2", 2 * (kc - 2) + 2)
                i1 = nc.vector.tensor_copy(zb, zf[:, kc * 512:(kc + 1) * 512])
                c.inc(i1, "c1")
                i2 = nc.scalar.activation(zq, zf[:, kc * 512:(kc + 1) * 512], AF.Square)
                c.inc(i2, "a")
                c.wait(nc.tensor, "c1", kc + 1)
                c.wait(nc.tensor, "a", kc + 1)
                i3 = nc.tensor.matmul(PS_A[0:1, 0:512], E.ones_128x1_bf[:], zb,
                                      start=(kc == 0), stop=(kc == 7))
                i4 = nc.tensor.matmul(PS_A[0:1, 512:1024], E.ones_128x1_bf[:], zq,
                                      start=(kc == 0), stop=(kc == 7))
                c.inc(i3, "c2"); c.inc(i4, "c2")
            m_r = rows[:, 0:512]; q_r = rows[:, 512:1024]; v_r = rows[:, 1024:1536]
            r_r = rows[:, 1536:2048]; cc_r = rows[:, 2048:2560]
            c.wait(nc.vector, "c2", 16)
            iv = nc.vector.tensor_scalar(m_r, PS_A[0:1, 0:512], 1.0 / D, None, ALU.mult)
            c.inc(iv, "c1")
            iv = nc.vector.tensor_scalar(q_r, PS_A[0:1, 512:1024], 1.0 / D, None, ALU.mult)
            c.inc(iv, "c1")
            c.wait(nc.vector, "c1", 10)
            iv = nc.vector.tensor_tensor(v_r, m_r, m_r, ALU.mult)
            c.inc(iv, "c1")
            c.wait(nc.vector, "c1", 11)
            iv = nc.vector.tensor_tensor(v_r, q_r, v_r, ALU.subtract)
            c.inc(iv, "c1")
            c.wait(nc.scalar, "c1", 12)
            iv = nc.scalar.activation(v_r, v_r, AF.Ln, bias=float(EPS_L1))
            c.inc(iv, "a")
            c.wait(nc.scalar, "a", 9)
            iv = nc.scalar.activation(r_r, v_r, AF.Exp, scale=-0.5)
            c.inc(iv, "a")
            c.wait(nc.vector, "a", 10)
            iv = nc.vector.scalar_tensor_tensor(cc_r, m_r, -1.0, r_r, ALU.mult, ALU.mult)
            c.inc(iv, "c1")
            c.wait(nc.tensor, "c1", 13)
            c.wait(nc.tensor, "a", 10)
            i1 = nc.tensor.matmul(PS_B[:, 0:512], E.ones_1x128[0:1, :], r_r,
                                  start=True, stop=True)
            i2 = nc.tensor.matmul(PS_B[:, 512:1024], E.ones_1x128[0:1, :], cc_r,
                                  start=True, stop=True)
            c.inc(i1, "c2"); c.inc(i2, "c2")
            nv = 13
            nsc = 10
            for kc in range(8):
                c.wait(nc.vector, "c2", 18)
                c.wait(nc.vector, "a", nsc)
                iv = nc.vector.tensor_tensor(tmp[:], zf[:, kc * 512:(kc + 1) * 512],
                                             PS_B[:, 0:512], ALU.mult)
                c.inc(iv, "c1"); nv += 1
                c.wait(nc.vector, "c1", nv)
                iv = nc.vector.tensor_tensor(tmp2[:], tmp[:], PS_B[:, 512:1024], ALU.add)
                c.inc(iv, "c1"); nv += 1
                c.wait(nc.scalar, "c1", nv)
                ie = nc.scalar.activation(
                    xnbf[:, ds(kc * N + tc * 512, 512)],
                    tmp2[:], AF.Identity, scale=lnsta[:, woff + kc:woff + kc + 1],
                    bias=lnsta[:, boff + kc:boff + kc + 1])
                c.inc(ie, "a"); nsc += 1
            c.wait(nc.sync, "a", nsc)
            close(E)


def c_ln_pass_b(nc, E, src_dram, b, xnbf, lnsta, woff, boff, tag):
    bd, PS_A, PS_B = E.bd, E.psA, E.psB
    import contextlib
    with contextlib.ExitStack() as stl:
        zf = stl.enter_context(nc.sbuf_tensor(f"cb_zf_{tag}", [128, FC * 512], F32))
        zbf = stl.enter_context(nc.sbuf_tensor(f"cb_zb_{tag}", [128, 2 * 512], BF16))
        zsq = stl.enter_context(nc.sbuf_tensor(f"cb_zq_{tag}", [128, 2 * 512], BF16))
        tmp = stl.enter_context(nc.sbuf_tensor(f"cb_tmp_{tag}", [128, 512], F32))
        tmp2 = stl.enter_context(nc.sbuf_tensor(f"cb_tmp2_{tag}", [128, 512], F32))
        rows = stl.enter_context(nc.sbuf_tensor(f"cb_rows_{tag}", [1, 5 * 512], F32))
        with nc.Fori(0, 4) as tc:
            c = E.c
            dma(nc, "sync", E, "d", zf[:].rearrange("p (c f) -> p c f", c=FC),
                src_dram[:, :, ds(b * T + tc * 512, 512)].rearrange("c p f -> p c f"))
            c.wait(nc.vector, "d")
            c.wait(nc.scalar, "d")
            for kc in range(8):
                zb = zbf[:, (kc % 2) * 512:(kc % 2) * 512 + 512]
                zq = zsq[:, (kc % 2) * 512:(kc % 2) * 512 + 512]
                if kc >= 2:
                    c.wait(nc.vector, "c2", 2 * (kc - 2) + 2)
                    c.wait(nc.scalar, "c2", 2 * (kc - 2) + 2)
                i1 = nc.vector.tensor_copy(zb, zf[:, kc * 512:(kc + 1) * 512])
                c.inc(i1, "c1")
                i2 = nc.scalar.activation(zq, zf[:, kc * 512:(kc + 1) * 512], AF.Square)
                c.inc(i2, "a")
                c.wait(nc.tensor, "c1", kc + 1)
                c.wait(nc.tensor, "a", kc + 1)
                i3 = nc.tensor.matmul(PS_A[0:1, 0:512], E.ones_128x1_bf[:], zb,
                                      start=(kc == 0), stop=(kc == 7))
                i4 = nc.tensor.matmul(PS_A[0:1, 512:1024], E.ones_128x1_bf[:], zq,
                                      start=(kc == 0), stop=(kc == 7))
                c.inc(i3, "c2"); c.inc(i4, "c2")
            m_r = rows[:, 0:512]; q_r = rows[:, 512:1024]; v_r = rows[:, 1024:1536]
            r_r = rows[:, 1536:2048]; cc_r = rows[:, 2048:2560]
            c.wait(nc.vector, "c2", 16)
            iv = nc.vector.tensor_scalar(m_r, PS_A[0:1, 0:512], 1.0 / D, None, ALU.mult)
            c.inc(iv, "c1")
            iv = nc.vector.tensor_scalar(q_r, PS_A[0:1, 512:1024], 1.0 / D, None, ALU.mult)
            c.inc(iv, "c1")
            c.wait(nc.vector, "c1", 10)
            iv = nc.vector.tensor_tensor(v_r, m_r, m_r, ALU.mult)
            c.inc(iv, "c1")
            c.wait(nc.vector, "c1", 11)
            iv = nc.vector.tensor_tensor(v_r, q_r, v_r, ALU.subtract)
            c.inc(iv, "c1")
            c.wait(nc.scalar, "c1", 12)
            iv = nc.scalar.activation(v_r, v_r, AF.Ln, bias=float(EPS_L1))
            c.inc(iv, "a")
            c.wait(nc.scalar, "a", 9)
            iv = nc.scalar.activation(r_r, v_r, AF.Exp, scale=-0.5)
            c.inc(iv, "a")
            c.wait(nc.vector, "a", 10)
            iv = nc.vector.scalar_tensor_tensor(cc_r, m_r, -1.0, r_r, ALU.mult, ALU.mult)
            c.inc(iv, "c1")
            c.wait(nc.tensor, "c1", 13)
            c.wait(nc.tensor, "a", 10)
            i1 = nc.tensor.matmul(PS_B[:, 0:512], E.ones_1x128[0:1, :], r_r,
                                  start=True, stop=True)
            i2 = nc.tensor.matmul(PS_B[:, 512:1024], E.ones_1x128[0:1, :], cc_r,
                                  start=True, stop=True)
            c.inc(i1, "c2"); c.inc(i2, "c2")
            nv = 13
            nsc = 10
            for kc in range(8):
                c.wait(nc.vector, "c2", 18)
                c.wait(nc.vector, "a", nsc)
                iv = nc.vector.tensor_tensor(tmp[:], zf[:, kc * 512:(kc + 1) * 512],
                                             PS_B[:, 0:512], ALU.mult)
                c.inc(iv, "c1"); nv += 1
                c.wait(nc.vector, "c1", nv)
                iv = nc.vector.tensor_tensor(tmp2[:], tmp[:], PS_B[:, 512:1024], ALU.add)
                c.inc(iv, "c1"); nv += 1
                c.wait(nc.scalar, "c1", nv)
                ie = nc.scalar.activation(
                    xnbf[:, ds(kc * T + tc * 512, 512)],
                    tmp2[:], AF.Identity, scale=lnsta[:, woff + kc:woff + kc + 1],
                    bias=lnsta[:, boff + kc:boff + kc + 1])
                c.inc(ie, "a"); nsc += 1
            c.wait(nc.sync, "a", nsc)
            close(E)


def emit_c_attn(nc, E, st, l):
    import contextlib
    PV = E.pv
    pvec = E.pvec
    bd, PS_A, PS_B = E.bd, E.psA, E.psB
    LSTR = PV["l1_bq_1"] - PV["l1_bq_0"]
    base0 = PV["l1_bq_0"]
    OFF = dict(bq=0, bk=16, bv=32, u=48, u2=64, bo=80, b1=88, b2=120,
               ln1w=128, ln1b=136, ln2w=144, ln2b=152)
    lbase = base0 + l * LSTR
    tabs = load_tables(nc, E, st, ["T1", "T2P", "R2_BF"])
    T1, T2P, R2 = tabs["T1"], tabs["T2P"], tabs["R2_BF"]
    with contextlib.ExitStack() as s1:
        E.uid += 1
        xnbf = s1.enter_context(nc.sbuf_tensor("c_xnbf%d" % E.uid, [128, FC * T], BF16))
        lnsta = s1.enter_context(nc.sbuf_tensor("c_lnsta%d" % E.uid, [128, 48], F32))
        c = E.c
        iv = nc.vector.tensor_copy(lnsta[:, 0:16], pvec[:, ds(lbase + OFF["ln1w"], 16)])
        c.inc(iv, "c1")
        c.wait(nc.sync, "c1", 1)
        close(E)
        with nc.Fori(0, B) as b:
            def srcsl(tcoff):
                return E.xa[:, :, ds(b * T + tcoff * 512, 512)]
            c_ln_pass_b(nc, E, E.xa, b, xnbf, lnsta, 0, 8, f"l1a_{E.uid}")
            E.uid += 1
            with contextlib.ExitStack() as sA:
                wstg = sA.enter_context(nc.sbuf_tensor(f"c_wstg_{E.uid}", [128, 1536], F32))
                wqs = sA.enter_context(nc.sbuf_tensor(f"c_wqs_{E.uid}", [128, 512], BF16))
                wks = sA.enter_context(nc.sbuf_tensor(f"c_wks_{E.uid}", [128, 512], BF16))
                wvs = sA.enter_context(nc.sbuf_tensor(f"c_wvs_{E.uid}", [128, 512], BF16))
                bcol = sA.enter_context(nc.sbuf_tensor(f"c_bcol_{E.uid}", [128, 5], F32))
                qfp = sA.enter_context(nc.sbuf_tensor(f"c_qfp_{E.uid}", [64, T], F32))
                qbf0 = sA.enter_context(nc.sbuf_tensor(f"c_qbf0_{E.uid}", [64, T], BF16))
                tmpa = sA.enter_context(nc.sbuf_tensor(f"c_tmpa_{E.uid}", [64, T], F32))
                tmpb = sA.enter_context(nc.sbuf_tensor(f"c_tmpb_{E.uid}", [64, T], F32))
                qp = sA.enter_context(nc.sbuf_tensor(f"c_qp_{E.uid}", [128, T], BF16))
                kr = sA.enter_context(nc.sbuf_tensor(f"c_kr_{E.uid}", [128, T], BF16))
                vbf = sA.enter_context(nc.sbuf_tensor(f"c_vbf_{E.uid}", [64, T], BF16))
                vt = sA.enter_context(nc.sbuf_tensor(f"c_vt_{E.uid}", [128, 16 * 65], BF16))
                pb = [sA.enter_context(nc.sbuf_tensor(f"c_pb{i}_{E.uid}", [128, 1024], BF16)) for i in range(2)]
                cfp = sA.enter_context(nc.sbuf_tensor(f"c_cfp_{E.uid}", [64, T], F32))
                ctmp = sA.enter_context(nc.sbuf_tensor(f"c_ctmp_{E.uid}", [64, T], BF16))
                rrow = sA.enter_context(nc.sbuf_tensor(f"c_rrow_{E.uid}", [1, T], F32))
                E.uid += 1
                c = E.c
                iv = nc.vector.tensor_copy(kr[0:64, :], R2[:])
                c.inc(iv, "c1")
                iv = nc.vector.memset(
                    vt[:].rearrange("p (kt c) -> p kt c", kt=16)[:, :, 64:65], 1.0)
                c.inc(iv, "c1")
                c.wait(nc.sync, "c1", 2)
                close(E)
                qkvov = E.l1_qkvo
                with nc.Fori(0, 16) as h:
                    c = E.c
                    dma(nc, "sync", E, "d",
                        wstg[:].rearrange("p (a c f) -> p a c f", a=3, c=8),
                        qkvov[ds(l, 1), 0:3, :, ds(h * 64, 64)]
                        .rearrange("o a (c p) f -> p (o a) c f", p=128))
                    c.wait(nc.vector, "d")
                    for wi, wdst in ((0, wqs), (1, wks), (2, wvs)):
                        iv = nc.vector.tensor_copy(wdst[:], wstg[:, wi * 512:(wi + 1) * 512])
                        c.inc(iv, "c1")
                    iv = nc.vector.tensor_copy(
                        bcol[:],
                        pvec[:, ds(lbase, 80)].rearrange("p (a e) -> p a e", a=5)[:, :, ds(h, 1)].rearrange("p a e -> p (a e)"))
                    c.inc(iv, "c1")
                    c.wait(nc.tensor, "c1", 1)
                    for tc in range(4):
                        for kc in range(8):
                            i1 = nc.tensor.matmul(
                                PS_A[0:64, tc * 512:(tc + 1) * 512],
                                wqs[:, kc * 64:(kc + 1) * 64],
                                xnbf[:, kc * T + tc * 512:kc * T + (tc + 1) * 512],
                                start=(kc == 0), stop=(kc == 7))
                        c.inc(i1, "c2")
                    c.wait(nc.scalar, "c1", 4)
                    c.wait(nc.scalar, "c2", 4)
                    ie = nc.scalar.activation(qfp[:], PS_A[0:64, 0:T],
                                              AF.Identity, bias=bcol[0:64, 0:1])
                    c.inc(ie, "a")
                    ie = nc.scalar.activation(qbf0[:], PS_A[0:64, 0:T],
                                              AF.Identity, bias=bcol[0:64, 0:1])
                    c.inc(ie, "a")
                    c.n["a"] += 6
                    nc.scalar.sem_inc(E.bd["a"], 6)
                    c.wait(nc.tensor, "a", 8)
                    for tc in range(4):
                        i1 = nc.tensor.matmul(PS_B[0:64, tc * 512:(tc + 1) * 512],
                                              E.id64_bf[0:64, 0:64],
                                              qbf0[:, tc * 512:(tc + 1) * 512],
                                              start=True, stop=True)
                        c.inc(i1, "c4")
                    c.wait(nc.scalar, "c4", 4)
                    ie = nc.scalar.copy(qp[64:128, 0:T], PS_B[0:64, 0:T])
                    c.inc(ie, "a")
                    c.n["a"] += 3
                    nc.scalar.sem_inc(E.bd["a"], 3)
                    for tc in range(4):
                        i1 = nc.tensor.matmul(PS_A[0:64, tc * 512:(tc + 1) * 512],
                                              E.p64[0:64, 0:64],
                                              qfp[:, tc * 512:(tc + 1) * 512],
                                              start=True, stop=True)
                        c.inc(i1, "c4")
                    c.wait(nc.vector, "a", 8)
                    iv = nc.vector.scalar_tensor_tensor(tmpa[:], qfp[:], bcol[0:64, 3:4],
                                                        T1[:], ALU.add, ALU.mult)
                    c.inc(iv, "c3")
                    c.wait(nc.vector, "c4", 8)
                    iv = nc.vector.scalar_tensor_tensor(tmpb[:], PS_A[0:64, 0:T],
                                                        bcol[0:64, 4:5], T2P[:],
                                                        ALU.add, ALU.mult)
                    c.inc(iv, "c3")
                    c.wait(nc.vector, "c3", 2)
                    iv = nc.vector.tensor_tensor(qp[0:64, :], tmpa[:], tmpb[:], ALU.add)
                    c.inc(iv, "c3")
                    c.wait(nc.tensor, "c3", 2)
                    for tc in range(4):
                        for kc in range(8):
                            i1 = nc.tensor.matmul(
                                PS_A[0:64, tc * 512:(tc + 1) * 512],
                                wks[:, kc * 64:(kc + 1) * 64],
                                xnbf[:, kc * T + tc * 512:kc * T + (tc + 1) * 512],
                                start=(kc == 0), stop=(kc == 7))
                        c.inc(i1, "c2")
                    c.wait(nc.scalar, "c2", 8)
                    ie = nc.scalar.activation(qbf0[:], PS_A[0:64, 0:T],
                                              AF.Identity, bias=bcol[0:64, 1:2])
                    c.inc(ie, "a")
                    c.n["a"] += 3
                    nc.scalar.sem_inc(E.bd["a"], 3)
                    c.wait(nc.tensor, "a", 16)
                    for tc in range(4):
                        i1 = nc.tensor.matmul(PS_B[0:64, tc * 512:(tc + 1) * 512],
                                              E.id64_bf[0:64, 0:64],
                                              qbf0[:, tc * 512:(tc + 1) * 512],
                                              start=True, stop=True)
                        c.inc(i1, "c4")
                    c.wait(nc.scalar, "c4", 12)
                    ie = nc.scalar.copy(kr[64:128, 0:T], PS_B[0:64, 0:T])
                    c.inc(ie, "a")
                    c.n["a"] += 3
                    nc.scalar.sem_inc(E.bd["a"], 3)
                    c.wait(nc.tensor, "a", 16)
                    for tc in range(4):
                        for kc in range(8):
                            i1 = nc.tensor.matmul(
                                PS_A[0:64, tc * 512:(tc + 1) * 512],
                                wvs[:, kc * 64:(kc + 1) * 64],
                                xnbf[:, kc * T + tc * 512:kc * T + (tc + 1) * 512],
                                start=(kc == 0), stop=(kc == 7))
                        c.inc(i1, "c2")
                    c.wait(nc.scalar, "c2", 12)
                    ie = nc.scalar.activation(vbf[:], PS_A[0:64, 0:T],
                                              AF.Identity, bias=bcol[0:64, 2:3])
                    c.inc(ie, "a")
                    c.n["a"] += 3
                    nc.scalar.sem_inc(E.bd["a"], 3)
                    nmm = 12; n4 = 12; na = 24; n3 = 3; n1 = 4
                    for kt in range(16):
                        c.wait(nc.tensor, "a", 24)
                        if kt >= 2:
                            c.wait(nc.tensor, "a", 24 + (kt - 2) + 1)
                        slot = PS_B[:, 1024 + (kt % 2) * 512:1024 + (kt % 2) * 512 + 64]
                        i1 = nc.tensor.matmul(slot, vbf[:, kt * 128:(kt + 1) * 128],
                                              E.ident_bf[0:64, 0:64], start=True, stop=True)
                        c.inc(i1, "c4"); n4 += 1
                        c.wait(nc.scalar, "c4", n4)
                        ie = nc.scalar.copy(vt[:, kt * 65:kt * 65 + 64], slot)
                        c.inc(ie, "a"); na += 1
                    c.wait(nc.sync, "a", na)
                    close(E)
                    c = E.c
                    na = 0; n1 = 0; n2 = 0
                    last_kt = {bk: min(15, 4 * bk + 3) for bk in range(4)}
                    first_touch = set()
                    nch = 0
                    nmask = 0
                    for kt in range(16):
                        q0 = kt * 128
                        w = T - q0
                        jj = 0
                        while jj < w:
                            chunk = min(1024, w - jj)
                            qa = q0 + jj
                            reg = PS_A[:, (nch % 2) * 1024:(nch % 2) * 1024 + 1024]
                            if nch >= 2:
                                c.wait(nc.tensor, "a", nch - 1)
                            kk = 0
                            while kk < chunk:
                                seg = min(512, chunk - kk)
                                i1 = nc.tensor.matmul(reg[:, kk:kk + seg],
                                                      kr[:, kt * 128:(kt + 1) * 128],
                                                      qp[:, qa + kk:qa + kk + seg],
                                                      start=True, stop=True)
                                c.inc(i1, "c2"); n2 += 1
                                kk += seg
                            if jj == 0:
                                c.wait(nc.vector, "c2", n2)
                                iv = nc.vector.tensor_tensor(reg[:, 0:128], reg[:, 0:128],
                                                             E.m_diag[:], ALU.add)
                                c.inc(iv, "c1"); nmask += 1
                            pbx = pb[nch % 2]
                            c.wait(nc.scalar, "c1", nmask)
                            c.wait(nc.scalar, "c2", n2)
                            ie = nc.scalar.activation(pbx[:, 0:chunk], reg[:, 0:chunk],
                                                      AF.Exp, scale=0.125)
                            c.inc(ie, "a"); na += 1; nch += 1
                            c.wait(nc.tensor, "a", na)
                            kk = 0
                            while kk < chunk:
                                qq = qa + kk
                                bank = qq // 512
                                seg = min(512 - (qq % 512), chunk - kk)
                                stt_f = bank not in first_touch
                                first_touch.add(bank)
                                stp_f = (last_kt[bank] == kt) and (qq + seg == min(T, (bank + 1) * 512))
                                i1 = nc.tensor.matmul(PS_B[0:65, qq:qq + seg],
                                                      vt[:, kt * 65:(kt + 1) * 65],
                                                      pbx[:, kk:kk + seg],
                                                      start=stt_f, stop=stp_f)
                                c.inc(i1, "c2"); n2 += 1
                                kk += seg
                            jj += chunk
                    n1 = nmask
                    c.wait(nc.vector, "c2", n2)
                    iv = nc.vector.reciprocal(rrow[:], PS_B[64:65, 0:T])
                    c.inc(iv, "c1"); n1 += 1
                    c.wait(nc.tensor, "c1", n1)
                    for tc in range(4):
                        i1 = nc.tensor.matmul(PS_A[0:64, tc * 512:(tc + 1) * 512],
                                              E.ones_1x128[0:1, 0:64],
                                              rrow[:, tc * 512:(tc + 1) * 512],
                                              start=True, stop=True)
                        c.inc(i1, "c4")
                    c.wait(nc.scalar, "c2", n2)
                    ie = nc.scalar.copy(cfp[:], PS_B[0:64, 0:T])
                    c.inc(ie, "a"); na += 1
                    c.wait(nc.vector, "a", na)
                    c.wait(nc.vector, "c4", 4)
                    iv = nc.vector.tensor_tensor(ctmp[:], cfp[:], PS_A[0:64, 0:T], ALU.mult)
                    c.inc(iv, "c1"); n1 += 1
                    c.wait(nc.scalar, "c1", n1)
                    nc.scalar.dma_start(
                        out=E.ctxd[ds(h, 1)].rearrange("o p f -> (o p) f"),
                        in_=ctmp[:]).then_inc(bd["g"], 16)
                    c.n["g"] += 16
                    c.wait(nc.sync, "g")
                    close(E)
            with contextlib.ExitStack() as sW:
                ctx_tc = sW.enter_context(nc.sbuf_tensor(f"c_ctc_{E.uid}", [64, 16 * 512], BF16))
                wos_f = sW.enter_context(nc.sbuf_tensor(f"c_wosf_{E.uid}", [64, 16 * 128], F32))
                wos = sW.enter_context(nc.sbuf_tensor(f"c_wos_{E.uid}", [64, 16 * 128], BF16))
                xres = sW.enter_context(nc.sbuf_tensor(f"c_xres_{E.uid}", [128, FC * 512], F32))
                ofp = sW.enter_context(nc.sbuf_tensor(f"c_ofp_{E.uid}", [128, FC * 512], F32))
                bos = sW.enter_context(nc.sbuf_tensor(f"c_bos_{E.uid}", [128, 8], F32))
                E.uid += 1
                wov = E.l1_qkvo
                with nc.Fori(0, 4) as tc:
                    c = E.c
                    iv = nc.vector.tensor_copy(bos[:], pvec[:, ds(lbase + OFF["bo"], 8)])
                    c.inc(iv, "c1")
                    dma(nc, "sync", E, "d", ctx_tc[:].rearrange("p (h f) -> p h f", h=16),
                        E.ctxd[:, :, ds(tc * 512, 512)].rearrange("h p f -> p h f"))
                    dma(nc, "sync", E, "d", xres[:].rearrange("p (c f) -> p c f", c=FC),
                        E.xa[:, :, ds(b * T + tc * 512, 512)].rearrange("c p f -> p c f"))
                    for m in range(8):
                        psem = ("d2", "d3")[m % 2]
                        if m >= 1:
                            c.wait(nc.sync, "c1", m + 1)
                        dma(nc, "sync", E, psem,
                            wos_f[:].rearrange("p (h f) -> p h f", h=16),
                            wov[ds(l, 1), ds(3, 1), :, ds(m * 128, 128)]
                            .rearrange("a e (hh p) f -> p (a e hh) f", p=64))
                        c.wait(nc.vector, psem)
                        if m >= 1:
                            c.wait(nc.vector, "c2", m)
                        iv = nc.vector.tensor_copy(wos[:], wos_f[:])
                        c.inc(iv, "c1")
                        c.wait(nc.tensor, "c1", m + 2)
                        c.wait(nc.tensor, "d", 32)
                        if m >= 2:
                            c.wait(nc.tensor, "a", m - 1)
                        pbank = PS_A[:, (m % 2) * 512:(m % 2) * 512 + 512]
                        for hx in range(16):
                            i1 = nc.tensor.matmul(
                                pbank, wos[0:64, hx * 128:(hx + 1) * 128],
                                ctx_tc[0:64, hx * 512:(hx + 1) * 512],
                                start=(hx == 0), stop=(hx == 15))
                        c.inc(i1, "c2")
                        c.wait(nc.scalar, "c2", m + 1)
                        c.wait(nc.scalar, "c1", 1)
                        ie = nc.scalar.activation(ofp[:, m * 512:(m + 1) * 512], pbank,
                                                  AF.Identity, bias=bos[:, m:m + 1])
                        c.inc(ie, "a")
                        c.wait(nc.vector, "a", m + 1)
                        iv = nc.vector.tensor_tensor(ofp[:, m * 512:(m + 1) * 512],
                                                     ofp[:, m * 512:(m + 1) * 512],
                                                     xres[:, m * 512:(m + 1) * 512], ALU.add)
                        c.inc(iv, "c3")
                    c.wait(nc.sync, "c3", 8)
                    dma(nc, "sync", E, "d",
                        E.xb[:, :, ds(b * T + tc * 512, 512)].rearrange("c p f -> p c f"),
                        ofp[:].rearrange("p (c f) -> p c f", c=FC))
                    c.wait(nc.sync, "d")
                    close(E)

def emit_c_ffn(nc, E, st, l):
    import contextlib
    PV = E.pv
    pvec = E.pvec
    bd, PS_A, PS_B = E.bd, E.psA, E.psB
    LSTR = PV["l1_bq_1"] - PV["l1_bq_0"]
    base0 = PV["l1_bq_0"]
    OFF = dict(bq=0, bk=16, bv=32, u=48, u2=64, bo=80, b1=88, b2=120,
               ln1w=128, ln1b=136, ln2w=144, ln2b=152)
    lbase = base0 + l * LSTR
    with contextlib.ExitStack() as s2:
        E.uid += 1
        xn2 = s2.enter_context(nc.sbuf_tensor("c_xn2%d" % E.uid, [128, FC * N], BF16))
        lnst2 = s2.enter_context(nc.sbuf_tensor("c_lnst2%d" % E.uid, [128, 56], F32))
        c = E.c
        iv = nc.vector.tensor_copy(lnst2[:, 0:16], pvec[:, ds(lbase + OFF["ln2w"], 16)])
        c.inc(iv, "c1")
        iv = nc.vector.tensor_copy(lnst2[:, 16:48], pvec[:, ds(lbase + OFF["b1"], 32)])
        c.inc(iv, "c1")
        iv = nc.vector.tensor_copy(lnst2[:, 48:56], pvec[:, ds(lbase + OFF["b2"], 8)])
        c.inc(iv, "c1")
        c.wait(nc.sync, "c1", 3)
        close(E)
        c_ln_pass(nc, E, E.xb, xn2, lnst2, 0, 8, f"l1f_{E.uid}")
        E.uid += 1
        with contextlib.ExitStack() as sF:
            stg = sF.enter_context(nc.sbuf_tensor(f"c_fstg_{E.uid}", [128, 8 * 1024], F32))
            w1qb = sF.enter_context(nc.sbuf_tensor(f"c_w1q_{E.uid}", [128, 8 * 1024], BF16))
            w2qb = sF.enter_context(nc.sbuf_tensor(f"c_w2q_{E.uid}", [128, 8 * 1024], BF16))
            aq = sF.enter_context(nc.sbuf_tensor(f"c_aq_{E.uid}", [128, 8 * 512], BF16))
            y_sb = sF.enter_context(nc.sbuf_tensor(f"c_ysb_{E.uid}", [128, 8 * 512], F32))
            xrs = sF.enter_context(nc.sbuf_tensor(f"c_xrs_{E.uid}", [128, FC * 512], F32))
            bq8 = sF.enter_context(nc.sbuf_tensor(f"c_bq8_{E.uid}", [128, 8], F32))
            E.uid += 1
            c = E.c
            iv = nc.vector.memset(y_sb[:], 0.0)
            c.inc(iv, "c1")
            c.wait(nc.sync, "c1", 1)
            close(E)
            with nc.Fori(0, 8) as tc:
                with nc.Fori(0, 4) as qq:
                    c = E.c
                    for wi in range(2):
                        psem = ("d", "d2")[wi]
                        if wi >= 1:
                            c.wait(nc.sync, "c1", wi)
                        if wi == 0:
                            srcap = E.l1_w1[ds(l, 1), :, ds(qq * 1024, 1024)]
                        else:
                            srcap = E.l1_w2[ds(l, 1), ds(qq * 1024, 1024), :]
                        dma(nc, "sync", E, psem,
                            stg[:].rearrange("p (c f) -> p c f", c=8),
                            srcap.rearrange("o (kc p) f -> p (o kc) f", p=128))
                        c.wait(nc.vector, psem)
                        wdst = (w1qb, w2qb)[wi]
                        iv = nc.vector.tensor_copy(
                            wdst[:].rearrange("p (c f) -> p c f", c=8),
                            stg[:].rearrange("p (c f) -> p c f", c=8))
                        c.inc(iv, "c1")
                    iv = nc.vector.tensor_copy(bq8[:], lnst2[:, ds(16 + qq * 8, 8)])
                    c.inc(iv, "c1")
                    c.wait(nc.sync, "c1", 3)
                    close(E)
                    c = E.c
                    n2 = 0; na = 0; n3 = 0
                    for mh in range(8):
                        psh = PS_A[:, (mh % 2) * 512:(mh % 2) * 512 + 512]
                        if mh >= 2:
                            c.wait(nc.tensor, "a", mh - 1)
                        for kc in range(8):
                            i1 = nc.tensor.matmul(
                                psh, w1qb[:, kc * 1024 + mh * 128:kc * 1024 + (mh + 1) * 128],
                                xn2[:, ds(kc * N + tc * 512, 512)],
                                start=(kc == 0), stop=(kc == 7))
                        c.inc(i1, "c2"); n2 += 1
                        c.wait(nc.scalar, "c2", n2)
                        ie = nc.scalar.activation(aq[:, mh * 512:(mh + 1) * 512], psh,
                                                  AF.Gelu, bias=bq8[:, mh:mh + 1])
                        c.inc(ie, "a"); na += 1
                    for mo in range(8):
                        psy = PS_B[:, (mo % 2) * 512:(mo % 2) * 512 + 512]
                        c.wait(nc.tensor, "a", 8)
                        if mo >= 2:
                            c.wait(nc.tensor, "c3", n3 - 1)
                        for kch in range(8):
                            i1 = nc.tensor.matmul(
                                psy, w2qb[:, kch * 1024 + mo * 128:kch * 1024 + (mo + 1) * 128],
                                aq[:, kch * 512:(kch + 1) * 512],
                                start=(kch == 0), stop=(kch == 7))
                        c.inc(i1, "c2"); n2 += 1
                        c.wait(nc.vector, "c2", n2)
                        iv = nc.vector.tensor_tensor(
                            y_sb[:, mo * 512:(mo + 1) * 512],
                            y_sb[:, mo * 512:(mo + 1) * 512], psy, ALU.add)
                        c.inc(iv, "c3"); n3 += 1
                    c.wait(nc.sync, "c3", n3)
                    close(E)
                c = E.c
                dma(nc, "sync", E, "d", xrs[:].rearrange("p (c f) -> p c f", c=FC),
                    E.xb[:, :, ds(tc * 512, 512)].rearrange("c p f -> p c f"))
                c.wait(nc.vector, "d")
                for mo in range(8):
                    iv = nc.vector.tensor_tensor(y_sb[:, mo * 512:(mo + 1) * 512],
                                                 y_sb[:, mo * 512:(mo + 1) * 512],
                                                 xrs[:, mo * 512:(mo + 1) * 512], ALU.add)
                    c.inc(iv, "c1")
                    c.wait(nc.scalar, "c1", mo + 1)
                    ie = nc.scalar.activation(xrs[:, mo * 512:(mo + 1) * 512],
                                              y_sb[:, mo * 512:(mo + 1) * 512],
                                              AF.Identity, bias=lnst2[:, 48 + mo:48 + mo + 1])
                    c.inc(ie, "a")
                c.wait(nc.sync, "a", 8)
                dma(nc, "sync", E, "d",
                    E.xa[:, :, ds(tc * 512, 512)].rearrange("c p f -> p c f"),
                    xrs[:].rearrange("p (c f) -> p c f", c=FC))
                c.wait(nc.sync, "d")
                c.wait(nc.vector, "a", 8)
                iv = nc.vector.memset(y_sb[:], 0.0)
                c.inc(iv, "c1")
                c.wait(nc.sync, "c1", 9)
                close(E)



PHASE_BIGTABS = {
    "IN": [], "A": ["C0", "S0P", "M_FULL"], "PL": [], "PLX": [], "PLR": [],
    "B": ["ONES_ROW"], "C1": ["T1", "T2P", "R2_BF"], "C2": [], "D": [],
}

PHASE_USES = {
    "IN": [],
    "A": ["l0_wq", "l0_wk", "l0_wv", "l0_wo"],
    "PL": [],
    "PLX": [],
    "PLR": [],
    "B": ["router", "mw1", "mw3", "mw2"],
    "C1": ["l1_qkvo"],
    "C2": ["l1_w1", "l1_w2"],
    "D": ["d_w1t", "d_w2t"],
}

LAUNCH_PLAN = (
    [("IN", None, [], ["XA"]),
     ("A", None, ["XA"], ["XB"]),
     ("PL", None, ["XB"], ["XA"]),
     ("B", None, ["XA"], ["XB"]),
     ("PLX", None, ["XB"], ["XA", "XC"])]
    + sum([[("C1", l, ["XA"], ["XB"]), ("C2", l, ["XB"], ["XA"])] for l in range(4)], [])
    + [("PLR", None, ["XA", "XC"], ["XB"]),
       ("D", None, ["XB"], [])]
)


def build_launch(phase, arg, reads, writes, pv_cols, pv_ncol):
    import contextlib
    E = make_base(
        pv_ncol,
        xa_kind=("ExternalInput" if "XA" in reads else
                 ("ExternalOutput" if "XA" in writes else "Internal")),
        xb_kind=("ExternalInput" if "XB" in reads else
                 ("ExternalOutput" if "XB" in writes else "Internal")),
        xc_kind=("ExternalInput" if "XC" in reads else
                 ("ExternalOutput" if "XC" in writes else "Internal")),
        with_x=(phase == "IN"),
        with_out=(phase == "D"),
        uses=PHASE_USES[phase],
        big_tabs=PHASE_BIGTABS[phase],
    )
    E.pv = pv_cols
    nc = E.nc
    with contextlib.ExitStack() as st:
        if phase == "IN":
            emit_input(nc, E, st)
        elif phase == "A":
            emit_phase_a(nc, E, st)
        elif phase == "PL":
            emit_postln(nc, E, st, E.xb, E.xa, None)
        elif phase == "PLX":
            emit_postln(nc, E, st, E.xb, E.xa, None, dst2=E.xc)
        elif phase == "PLR":
            emit_postln(nc, E, st, E.xa, E.xb, None, res_dram=E.xc)
        elif phase == "B":
            emit_phase_b(nc, E, st)
        elif phase == "C1":
            emit_c_attn(nc, E, st, arg)
        elif phase == "C2":
            emit_c_ffn(nc, E, st, arg)
        elif phase == "D":
            E.d_src = E.xb
            emit_phase_d(nc, E, st)
        nc.all_engine_barrier()
    return E


_PROG_CACHE = {}


def get_launch_program(phase, arg, reads, writes, pv_cols, pv_ncol):
    key = (phase, arg)
    if key not in _PROG_CACHE:
        _PROG_CACHE[key] = build_launch(phase, arg, reads, writes, pv_cols, pv_ncol)
    return _PROG_CACHE[key]



def reg_snapshot(nc):
    return {a.name for a in nc.m.functions[0].allocations
            if type(a).__name__ == "Register"}


def reg_reclaim(nc, baseline):
    import bass_rust
    eng_map = {"Pool": nc.gpsimd, "Activation": nc.scalar, "PE": nc.tensor,
               "DVE": nc.vector, "SP": nc.sync}
    n = 0
    for a in list(nc.m.functions[0].allocations):
        if type(a).__name__ != "Register":
            continue
        if a.name in baseline or not a.allocated:
            continue
        ename = str(a.engine).split(".")[-1]
        h = bass_rust.RegisterHandle(a.name, a.engine)
        try:
            eng_map[ename].free_register(h)
            n += 1
        except ValueError:
            pass
    return n


def build_fused(pv_cols, pv_ncol):
    import contextlib
    E = make_base(pv_ncol, xa_kind="Internal", with_x=True, xb_kind="Internal",
                  xc_kind="Internal", with_out=True, uses=None,
                  big_tabs=list(BIG_TABLES))
    E.pv = pv_cols
    nc = E.nc
    base = reg_snapshot(nc)

    def rr():
        reg_reclaim(nc, base)

    with contextlib.ExitStack() as st:
        emit_input(nc, E, st)
    rr()
    with contextlib.ExitStack() as st:
        emit_phase_a(nc, E, st)
    rr()
    with contextlib.ExitStack() as st:
        emit_postln(nc, E, st, E.xb, E.xa, None)
    rr()
    with contextlib.ExitStack() as st:
        emit_phase_b(nc, E, st)
    rr()
    with contextlib.ExitStack() as st:
        emit_postln(nc, E, st, E.xb, E.xa, None, dst2=E.xc)
    rr()
    for l in range(4):
        with contextlib.ExitStack() as st:
            emit_c_attn(nc, E, st, l)
        rr()
        with contextlib.ExitStack() as st:
            emit_c_ffn(nc, E, st, l)
        rr()
    with contextlib.ExitStack() as st:
        emit_postln(nc, E, st, E.xa, E.xb, None, res_dram=E.xc)
    rr()
    with contextlib.ExitStack() as st:
        E.d_src = E.xb
        emit_phase_d(nc, E, st)
    nc.all_engine_barrier()
    return E


FUSED = True
_FUSED_CACHE = []


def kernel(**inputs):
    from concourse.bass_utils import run_bass_kernel_spmd
    NCORES = 8
    gs = []
    cols = None
    for core in range(NCORES):
        g, cols = host_prep(inputs, core)
        gs.append(g)
    pv_ncol = gs[0]["pvec"].shape[1]
    if FUSED:
        if not _FUSED_CACHE:
            _FUSED_CACHE.append(build_fused(cols, pv_ncol))
        E = _FUSED_CACHE[0]
        in_maps = []
        for core in range(NCORES):
            g = gs[core]
            m = {k: g[k] for k in WEIGHT_SPECS}
            m["pvec"] = g["pvec"]
            for k in TABLE_SPECS:
                m[k] = g[k]
            m["x_in"] = g["x_in"]
            in_maps.append(m)
        res = run_bass_kernel_spmd(E.nc, in_maps, core_ids=list(range(NCORES)))
        out = [np.asarray(res.results[c]["OUT"], np.float32) for c in range(NCORES)]
        return np.concatenate(out, axis=0).reshape(16, T, D)
    return kernel_unfused(gs, cols, pv_ncol)


def kernel_unfused(gs, cols, pv_ncol):
    from concourse.bass_utils import run_bass_kernel_spmd
    NCORES = 8
    state = [dict() for _ in range(NCORES)]
    out = [None] * NCORES
    for (phase, arg, reads, writes) in LAUNCH_PLAN:
        E = get_launch_program(phase, arg, reads, writes, cols, pv_ncol)
        in_maps = []
        for core in range(NCORES):
            g = gs[core]
            m = {k: g[k] for k in PHASE_USES[phase]}
            m["pvec"] = g["pvec"]
            for k in SMALL_TABLES:
                m[k] = g[k]
            for k in PHASE_BIGTABS[phase]:
                m[k] = g[k]
            if phase == "IN":
                m["x_in"] = g["x_in"]
            for r in reads:
                m[r] = state[core][r]
            in_maps.append(m)
        res = run_bass_kernel_spmd(E.nc, in_maps, core_ids=list(range(NCORES)))
        for core in range(NCORES):
            for w in writes:
                state[core][w] = np.asarray(res.results[core][w])
            if phase == "D":
                out[core] = np.asarray(res.results[core]["OUT"], np.float32)
    return np.concatenate(out, axis=0).reshape(16, T, D)


# version 07 (final)
# speedup vs baseline: 1.0627x; 1.0627x over previous
import numpy as np
import ml_dtypes
import concourse.bass as bass
import concourse.mybir as mybir

F32 = mybir.dt.float32
BF16 = mybir.dt.bfloat16
AF = mybir.ActivationFunctionType
ALU = mybir.AluOpType
AX = mybir.AxisListType.X
ds = bass.ds

B = 2
T = 2048
D = 1024
N = B * T
FC = D // 128
EPS_MOD = 1e-5
EPS_L1 = 1e-6


def host_tables():
    t = {}
    ident = np.eye(128)
    t["IDENT_BF"] = ident.astype(ml_dtypes.bfloat16)
    t["IDENT_F32"] = ident.astype(np.float32)
    p64 = np.zeros((128, 128))
    for i in range(32):
        p64[i + 32, i] = 1.0
        p64[i, i + 32] = 1.0
    t["P64_F32"] = p64.astype(np.float32)
    id64 = np.zeros((128, 128))
    id64[:64, :64] = np.eye(64)
    t["ID64_BF"] = id64.astype(ml_dtypes.bfloat16)
    t["ONES_1x128_F32"] = np.ones((1, 128), np.float32)
    t["ONES_128x1_BF"] = np.ones((128, 1), ml_dtypes.bfloat16)
    t["ONES_128x1_F32"] = np.ones((128, 1), np.float32)

    HD = 64
    nb = HD // 4
    inv = 1.0 / (10000.0 ** (np.arange(nb) / nb))
    ang = np.arange(T)[:, None] * inv[None, :]
    sin_h = np.repeat(np.sin(ang), 2, axis=-1)
    cos_h = np.repeat(np.cos(ang), 2, axis=-1)
    sin = np.concatenate([sin_h, sin_h], axis=-1).T
    cos = np.concatenate([cos_h, cos_h], axis=-1).T
    s0p = sin.copy()
    s0p[:32] *= -1.0
    t["C0"] = cos.astype(np.float32)
    t["S0P"] = s0p.astype(np.float32)

    f = 1.0 / (10000.0 ** (np.arange(0, HD, 2) / HD))
    tt = np.arange(T)
    sin_t = np.sin(tt[None, :] * f[:, None])
    cos_t = np.cos(tt[None, :] * f[:, None])
    T1 = np.concatenate([sin_t, sin_t], axis=0)
    T2p = np.concatenate([cos_t, -cos_t], axis=0)
    R2 = np.concatenate([cos_t, sin_t], axis=0)
    t["T1"] = T1.astype(np.float32)
    t["T2P"] = T2p.astype(np.float32)
    t["R2_BF"] = R2.astype(ml_dtypes.bfloat16)

    NEG = np.float32(-8e9)
    def m_l0(krange, qrange):
        k = np.arange(*krange)[:, None]
        q = np.arange(*qrange)[None, :]
        allowed = (np.abs(q - k) <= 12) | (q < 8) | (k < 8)
        return np.where(allowed, 0.0, NEG).astype(np.float32)
    t["M_FULL"] = m_l0((0, 128), (0, T))
    t["M_B1"] = m_l0((128, 256), (0, 384))
    k = np.arange(128)[:, None]
    q = np.arange(384)[None, :]
    band = (np.abs((q - 128) - k) <= 12)
    t["M_B"] = np.where(band, 0.0, NEG).astype(np.float32)
    qa = np.arange(128)[None, :]
    t["M_A"] = np.where(qa < 8, 0.0, NEG).astype(np.float32) * np.ones((128, 1), np.float32)
    kp = np.arange(128)[:, None]
    qc = np.arange(128)[None, :]
    t["M_DIAG"] = np.where(kp > qc, np.float32(-1e30), 0.0).astype(np.float32)
    sel = np.zeros((9, 9 * 128), np.float32)
    for e in range(9):
        sel[e, e * 128:(e + 1) * 128] = 1.0
    t["SEL"] = sel
    t["ONES_ROW"] = np.ones((1, N), np.float32)
    return t


def host_prep(inputs, core):
    g = dict()
    x = np.asarray(inputs["L0_attn_x"])
    g["x_in"] = np.ascontiguousarray(x[2 * core:2 * core + 2])

    g["l0_wq"] = np.ascontiguousarray(inputs["L0_attn_Wq"])
    g["l0_wk"] = np.ascontiguousarray(inputs["L0_attn_Wk"])
    g["l0_wv"] = np.ascontiguousarray(inputs["L0_attn_Wv"])
    g["l0_wo"] = np.ascontiguousarray(inputs["L0_attn_Wo"])
    g["router"] = np.ascontiguousarray(inputs["L0_ffn_w_router"])
    g["mw1"] = np.ascontiguousarray(np.concatenate(
        [inputs["L0_ffn_w1"], np.asarray(inputs["L0_ffn_sw1"])[None]], axis=0))
    g["mw3"] = np.ascontiguousarray(np.concatenate(
        [inputs["L0_ffn_w3"], np.asarray(inputs["L0_ffn_sw3"])[None]], axis=0))
    g["mw2"] = np.ascontiguousarray(np.concatenate(
        [inputs["L0_ffn_w2"], np.asarray(inputs["L0_ffn_sw2"])[None]], axis=0))
    wq = np.asarray(inputs["L1_attn_Wq"])
    wk = np.asarray(inputs["L1_attn_Wk"])
    wv = np.asarray(inputs["L1_attn_Wv"])
    wo = np.asarray(inputs["L1_attn_Wo"])
    qkvo = np.stack([wq.transpose(0, 2, 1), wk.transpose(0, 2, 1),
                     wv.transpose(0, 2, 1), wo.transpose(0, 2, 1)], axis=1)
    g["l1_qkvo"] = np.ascontiguousarray(qkvo)
    g["l1_w1"] = np.ascontiguousarray(np.asarray(inputs["L1_attn_W1"]).transpose(0, 2, 1))
    g["l1_w2"] = np.ascontiguousarray(np.asarray(inputs["L1_attn_W2"]).transpose(0, 2, 1))
    g["d_w1t"] = np.ascontiguousarray(np.asarray(inputs["L1_ffn_W1"]).T)
    g["d_w2t"] = np.ascontiguousarray(np.asarray(inputs["L1_ffn_W2"]).T)

    cols = {}
    buf = []

    def add(name, arr128):
        cols[name] = len(buf)
        assert arr128.shape[0] == 128
        for j in range(arr128.shape[1]):
            buf.append(np.ascontiguousarray(arr128[:, j]))
        return cols[name]

    def headify(vec, nh, base_row=0):
        a = np.zeros((128, nh), np.float32)
        a[base_row:base_row + 64] = np.asarray(vec, np.float32).reshape(nh, 64).T
        return a

    def fcify(vec):
        v = np.asarray(vec, np.float32)
        nfc = v.size // 128
        return v.reshape(nfc, 128).T.copy()

    add("l0_bq", headify(inputs["L0_attn_bq"], 16))
    add("l0_bk", headify(inputs["L0_attn_bk"], 4))
    add("l0_bv", headify(inputs["L0_attn_bv"], 4))
    hw = np.asarray(inputs["L0_attn_head_weights"], np.float32)
    hw128 = np.zeros((128, 16), np.float32)
    hw128[:64] = hw[None, :]
    add("l0_hw", hw128)
    add("l0_bo", fcify(inputs["L0_attn_bo"]))
    for l in range(4):
        add(f"l1_bq_{l}", headify(inputs["L1_attn_bq"][l], 16, base_row=0))
        add(f"l1_bk_{l}", headify(inputs["L1_attn_bk"][l], 16, base_row=0))
        add(f"l1_bv_{l}", headify(inputs["L1_attn_bv"][l], 16, base_row=0))
        u = np.asarray(inputs["L1_attn_u_bias"][l], np.float32)
        u2 = np.concatenate([u[:, 32:], u[:, :32]], axis=1)
        add(f"l1_u_{l}", headify(u.reshape(-1), 16, base_row=0))
        add(f"l1_u2_{l}", headify(u2.reshape(-1), 16, base_row=0))
        add(f"l1_bo_{l}", fcify(inputs["L1_attn_bo"][l]))
        add(f"l1_b1_{l}", fcify(inputs["L1_attn_b1"][l]))
        add(f"l1_b2_{l}", fcify(inputs["L1_attn_b2"][l]))
        add(f"l1_ln1w_{l}", fcify(inputs["L1_attn_ln1_w"][l]))
        add(f"l1_ln1b_{l}", fcify(inputs["L1_attn_ln1_b"][l]))
        add(f"l1_ln2w_{l}", fcify(inputs["L1_attn_ln2_w"][l]))
        add(f"l1_ln2b_{l}", fcify(inputs["L1_attn_ln2_b"][l]))
    add("d_b1", fcify(inputs["L1_ffn_b1"]))
    add("d_b2", fcify(inputs["L1_ffn_b2"]))
    pv = np.stack(buf, axis=1).astype(np.float32)
    g["pvec"] = np.ascontiguousarray(pv)
    g.update({k: np.ascontiguousarray(v) for k, v in host_tables().items()})
    return g, cols



class Body:
    def __init__(self, nc, names=("d", "d2", "d3", "g", "a", "c1", "c2", "c3", "c4", "c5")):
        self.nc = nc
        self.names = names
        self.sems = {n: nc.alloc_semaphore("bsem_" + n) for n in names}

    def __getitem__(self, n):
        return self.sems[n]

    def close_body(self):
        nc = self.nc
        nc.all_engine_barrier()
        for h in self.sems.values():
            nc.gpsimd.sem_clear(h)
        nc.all_engine_barrier()


class Ctr:
    def __init__(self, bd):
        self.bd = bd
        self.n = {k: 0 for k in bd.names}

    def inc(self, ins, s, k=1):
        self.n[s] += k
        ins.then_inc(self.bd[s], k)
        return self.n[s]

    def wait(self, eng, s, v=None):
        eng.wait_ge(self.bd[s], self.n[s] if v is None else v)


class NS:
    def __init__(self, **kw):
        self.__dict__.update(kw)


def close(E):
    E.bd.close_body()
    E.c = Ctr(E.bd)


def dma(nc, eng, E, sem, out, in_, **kw):
    getattr(nc, eng).dma_start(out=out, in_=in_, **kw).then_inc(E.bd[sem], 16)
    E.c.n[sem] += 16


SMALL_TABLES = {
    "SEL": ([9, 9 * 128], F32),
    "IDENT_BF": ([128, 128], BF16), "IDENT_F32": ([128, 128], F32),
    "P64_F32": ([128, 128], F32), "ID64_BF": ([128, 128], BF16),
    "ONES_1x128_F32": ([1, 128], F32), "ONES_128x1_BF": ([128, 1], BF16),
    "ONES_128x1_F32": ([128, 1], F32),
    "M_B1": ([128, 384], F32), "M_B": ([128, 384], F32),
    "M_A": ([128, 128], F32), "M_DIAG": ([128, 128], F32),
}
BIG_TABLES = {
    "ONES_ROW": ([1, N], F32),
    "C0": ([64, T], F32), "S0P": ([64, T], F32),
    "T1": ([64, T], F32), "T2P": ([64, T], F32), "R2_BF": ([64, T], BF16),
    "M_FULL": ([128, T], F32),
}
TABLE_SPECS = dict(SMALL_TABLES, **BIG_TABLES)

WEIGHT_SPECS = {
    "l0_wq": [D, D], "l0_wk": [D, 256], "l0_wv": [D, 256], "l0_wo": [D, D],
    "router": [D, 8],
    "mw1": [9, D, 2048], "mw3": [9, D, 2048], "mw2": [9, 2048, D],
    "l1_qkvo": [4, 4, D, D], "l1_w1": [4, D, 4096], "l1_w2": [4, 4096, D],
    "d_w1t": [D, 4096], "d_w2t": [4096, D],
}


def make_base(pv_ncol, xa_kind="Internal", with_x=True, xb_kind="Internal",
              xc_kind="Internal", with_out=True, uses=None, big_tabs=None):
    nc = bass.Bass("TRN2", target_bir_lowering=False, debug=False)
    E = NS(nc=nc)
    E.xa = nc.dram_tensor("XA", [FC, 128, N], F32, kind=xa_kind).ap()
    E.xb = nc.dram_tensor("XB", [FC, 128, N], F32, kind=xb_kind).ap()
    E.xc = nc.dram_tensor("XC", [FC, 128, N], F32, kind=xc_kind).ap()
    if with_out:
        E.out = nc.dram_tensor("OUT", [B, T, D], F32, kind="ExternalOutput").ap()
    E.gscr = nc.dram_tensor("GSCR", [128, 2], F32, kind="Internal").ap()
    E.wq2d = nc.dram_tensor("WQ2D", [32, 128, D], BF16, kind="Internal").ap()
    E.ctxd = nc.dram_tensor("CTXD", [16, 64, T], BF16, kind="Internal").ap()
    E.mwbf = nc.dram_tensor("MWBF", [9, 2, 384, 8192], BF16, kind="Internal").ap()
    if with_x:
        E.x_in = nc.dram_tensor("x_in", [B, T, D], F32, kind="ExternalInput").ap()
    for k, shp in WEIGHT_SPECS.items():
        if uses is None or k in uses:
            setattr(E, k, nc.dram_tensor(k, shp, F32, kind="ExternalInput").ap())
    E.pvec_d = nc.dram_tensor("pvec", [128, pv_ncol], F32, kind="ExternalInput").ap()
    E.tab_d = {}
    for k, (shp, dt) in SMALL_TABLES.items():
        E.tab_d[k] = nc.dram_tensor(k, shp, dt, kind="ExternalInput").ap()
    for k in (big_tabs or []):
        shp, dt = BIG_TABLES[k]
        E.tab_d[k] = nc.dram_tensor(k, shp, dt, kind="ExternalInput").ap()

    E.pvec = nc.alloc_sbuf_tensor("pvec_sb", [128, pv_ncol], F32)
    sb = {}
    for k, (shp, dt) in SMALL_TABLES.items():
        sb[k] = nc.alloc_sbuf_tensor(k + "_sb", shp, dt)
    E.ident_bf = sb["IDENT_BF"]; E.ident_f32 = sb["IDENT_F32"]
    E.p64 = sb["P64_F32"]; E.id64_bf = sb["ID64_BF"]
    E.ones_1x128 = sb["ONES_1x128_F32"]; E.ones_128x1_bf = sb["ONES_128x1_BF"]
    E.ones_128x1_f32 = sb["ONES_128x1_F32"]
    E.sel = sb["SEL"]
    E.m_b1 = sb["M_B1"]
    E.m_b = sb["M_B"]; E.m_a = sb["M_A"]; E.m_diag = sb["M_DIAG"]

    E.psA = nc.alloc_psum_tensor("psA", [128, 2048], F32)
    E.psB = nc.alloc_psum_tensor("psB", [128, 2048], F32)
    E.bd = Body(nc)
    E.c = Ctr(E.bd)
    E.uid = 0
    E.cpool = nc.alloc_sbuf_tensor("cpool", [128, 2], F32)
    nc.vector.memset(E.cpool[:, 0:1], EPS_MOD)
    nc.vector.memset(E.cpool[:, 1:2], EPS_L1)
    nc.const_aps.register(F32, EPS_MOD, E.cpool[:, 0:1])
    nc.const_aps.register(F32, EPS_L1, E.cpool[:, 1:2])
    dma(nc, "sync", E, "d", E.pvec[:], E.pvec_d)
    for k in SMALL_TABLES:
        dma(nc, "sync", E, "d", sb[k][:], E.tab_d[k])
    E.c.wait(nc.sync, "d")
    close(E)
    return E


def load_tables(nc, E, st, names):
    out = {}
    E.uid += 1
    for k in names:
        shp, dt = TABLE_SPECS[k]
        h = st.enter_context(nc.sbuf_tensor(k + "_ph%d" % E.uid, shp, dt))
        dma(nc, "sync", E, "d", h[:], E.tab_d[k])
        out[k] = h
    E.c.wait(nc.sync, "d")
    close(E)
    return out



def emit_phase_d(nc, E, st):
    d_src = getattr(E, "d_src", E.xa)
    PV = E.pv
    pvec = E.pvec
    bd, PS_A, PS_B = E.bd, E.psA, E.psB
    outv = E.out.rearrange("b t d -> (b t) d")

    wq1 = st.enter_context(nc.sbuf_tensor("d_wq1", [128, 8 * 4096], BF16))
    rgb = st.enter_context(nc.sbuf_tensor("d_rgb", [128, 2], F32))

    with nc.sbuf_tensor("d_wbuf", [128, 2 * 4096], F32) as wbuf, \
         nc.sbuf_tensor("d_gacc", [128, 2], F32) as gacc, \
         nc.sbuf_tensor("d_gtmp", [128, 2], F32) as gtmp, \
         nc.sbuf_tensor("d_grow", [1, 256], F32) as grow, \
         nc.sbuf_tensor("d_rg", [1, 2], F32) as rg, \
         nc.sbuf_tensor("d_sgn", [128, 4096], BF16) as sgn, \
         nc.sbuf_tensor("d_msk", [128, 4096], BF16) as msk, \
         nc.sbuf_tensor("d_azf", [128, 4096], F32) as azf, \
         nc.sbuf_tensor("d_qot", [128, 4096], BF16) as qot:
        c = E.c
        nc.vector.memset(gacc[:], 0.0).then_inc(bd["c1"], 1)
        c.n["c1"] += 1
        chunks = [(E.d_w1t, kc, 4096, 0) for kc in range(8)] + \
                 [(E.d_w2t, kc, 1024, 1) for kc in range(32)]
        for i, (src, kc, w, col) in enumerate(chunks):
            buf = wbuf[:, (i % 2) * 4096:(i % 2) * 4096 + w]
            psem = ("d", "a")[i % 2]
            if i >= 2:
                c.wait(nc.sync, "c2", 2 * (i - 2) + 1)
            dma(nc, "sync", E, psem, buf, src[ds(kc * 128, 128), :])
            c.wait(nc.vector, psem)
            c.wait(nc.vector, "c1", 1)
            c.wait(nc.vector, "c2", 2 * i)
            nc.vector.reduce_sum(gtmp[:, col:col + 1], buf, axis=AX,
                                 apply_absolute_value=True).then_inc(bd["c2"], 1)
            c.n["c2"] += 1
            c.wait(nc.vector, "c2")
            nc.vector.tensor_tensor(gacc[:, col:col + 1], gacc[:, col:col + 1],
                                    gtmp[:, col:col + 1], ALU.add).then_inc(bd["c2"], 1)
            c.n["c2"] += 1
        c.wait(nc.scalar, "c2")
        dma(nc, "scalar", E, "g", E.gscr, gacc[:])
        c.wait(nc.scalar, "g")
        nc.scalar.dma_start(out=grow[0:1, 0:128],
                            in_=E.gscr[:, 0:1].rearrange("p c -> c p"),
                            allow_slow_non_contiguous=True).then_inc(bd["g"], 16)
        nc.scalar.dma_start(out=grow[0:1, 128:256],
                            in_=E.gscr[:, 1:2].rearrange("p c -> c p"),
                            allow_slow_non_contiguous=True).then_inc(bd["g"], 16)
        c.n["g"] += 32
        c.wait(nc.vector, "g")
        nc.vector.reduce_sum(rg[:, 0:1], grow[0:1, 0:128], axis=AX).then_inc(bd["c3"], 1)
        nc.vector.reduce_sum(rg[:, 1:2], grow[0:1, 128:256], axis=AX).then_inc(bd["c3"], 1)
        c.n["c3"] += 2
        c.wait(nc.vector, "c3")
        nc.vector.tensor_scalar(rg[:], rg[:], 1.0 / (4096.0 * 1024.0), 1e-5,
                                ALU.mult, ALU.add).then_inc(bd["c3"], 1)
        c.n["c3"] += 1
        c.wait(nc.vector, "c3")
        nc.vector.reciprocal(rg[:], rg[:]).then_inc(bd["c3"], 1)
        c.n["c3"] += 1
        c.wait(nc.tensor, "c3")
        nc.tensor.matmul(PS_A[:, 0:2], E.ones_1x128[0:1, :], rg[:, 0:2],
                         start=True, stop=True).then_inc(bd["c4"], 1)
        c.n["c4"] += 1
        c.wait(nc.scalar, "c4")
        nc.scalar.copy(rgb[:], PS_A[:, 0:2]).then_inc(bd["c5"], 1)
        close(E)

        for (src, nkc, w, col, to_sb) in ((E.d_w1t, 8, 4096, 0, True),
                                          (E.d_w2t, 32, 1024, 1, False)):
            c = E.c
            for kc in range(nkc):
                buf = wbuf[:, (kc % 2) * 4096:(kc % 2) * 4096 + w]
                psem = ("d", "a")[kc % 2]
                if kc >= 2:
                    c.wait(nc.sync, "c2", 2 * (kc - 2) + 2)
                dma(nc, "sync", E, psem, buf, src[ds(kc * 128, 128), :])
                c.wait(nc.scalar, psem)
                if kc >= 1:
                    c.wait(nc.scalar, "c1", kc)
                nc.scalar.activation(sgn[:, 0:w], buf, AF.Sign).then_inc(bd["c2"], 1)
                nc.scalar.activation(azf[:, 0:w], buf, AF.Abs,
                                     scale=rgb[:, col:col + 1]).then_inc(bd["c2"], 1)
                c.n["c2"] += 2
                c.wait(nc.vector, "c2")
                nc.vector.tensor_single_scalar(msk[:, 0:w], azf[:, 0:w], 0.5,
                                               ALU.is_ge).then_inc(bd["c3"], 1)
                c.n["c3"] += 1
                c.wait(nc.vector, "c3")
                if to_sb:
                    dst = wq1[:, kc * 4096:(kc + 1) * 4096]
                else:
                    if kc >= 1:
                        c.wait(nc.vector, "g", 16 * kc)
                    dst = qot[:, 0:w]
                ins = nc.vector.tensor_tensor(dst, sgn[:, 0:w], msk[:, 0:w], ALU.mult)
                c.inc(ins, "c1")
                if not to_sb:
                    c.wait(nc.scalar, "c1", kc + 1)
                    nc.scalar.dma_start(
                        out=E.wq2d[ds(kc, 1)].rearrange("o p f -> (o p) f"),
                        in_=qot[:, 0:w]).then_inc(bd["g"], 16)
                    c.n["g"] += 16
            if not to_sb:
                c.wait(nc.sync, "g")
            close(E)

    with nc.sbuf_tensor("d_xf", [128, FC * 512], F32) as xf, \
         nc.sbuf_tensor("d_xbf", [128, FC * 512], BF16) as xbf, \
         nc.sbuf_tensor("d_asb", [128, 32 * 512], BF16) as a_sb, \
         nc.sbuf_tensor("d_w2q", [128, 4 * 1024], BF16) as w2q, \
         nc.sbuf_tensor("d_wq1s", [128, 8 * 128], BF16) as wq1s, \
         nc.sbuf_tensor("d_wq1s2", [128, 8 * 128], BF16) as wq1s2, \
         nc.sbuf_tensor("d_bcol", [128, 2], F32) as bcol, \
         nc.sbuf_tensor("d_yf", [128, FC * 512], F32) as yf, \
         nc.sbuf_tensor("d_tmp", [128, 512], F32) as tmp, \
         nc.sbuf_tensor("d_zbf", [128, 2 * 512], BF16) as zbf, \
         nc.sbuf_tensor("d_zsq", [128, 2 * 512], BF16) as zsq, \
         nc.sbuf_tensor("d_rows", [1, 6 * 512], F32) as rows, \
         nc.sbuf_tensor("d_obuf", [128, 4 * 1024], F32) as obuf:
        with nc.Fori(0, 8) as tc:
            c = E.c
            dma(nc, "sync", E, "d",
                xf[:].rearrange("p (c f) -> p c f", c=FC),
                d_src[:, :, ds(tc * 512, 512)].rearrange("c p f -> p c f"))
            c.wait(nc.vector, "d")
            nc.vector.tensor_copy(xbf[:], xf[:])
            close(E)
            with nc.Fori(0, 16) as mhh:
                c = E.c
                w1v = wq1[:].rearrange("p (kc f) -> p kc f", kc=8)
                i1 = nc.vector.tensor_copy(
                    wq1s[:].rearrange("p (kc f) -> p kc f", kc=8),
                    w1v[:, :, ds(mhh * 256, 128)])
                c.inc(i1, "c1")
                i2 = nc.vector.tensor_copy(
                    wq1s2[:].rearrange("p (kc f) -> p kc f", kc=8),
                    w1v[:, :, ds(mhh * 256 + 128, 128)])
                c.inc(i2, "c1")
                i3 = nc.vector.tensor_copy(bcol[:], pvec[:, ds(PV["d_b1"] + mhh * 2, 2)])
                c.inc(i3, "c1")
                for half, wsrc in ((0, wq1s), (1, wq1s2)):
                    c.wait(nc.tensor, "c1", half + 2)
                    ps = PS_A[:, half * 512:half * 512 + 512]
                    for kc in range(8):
                        ins = nc.tensor.matmul(ps, wsrc[:, kc * 128:(kc + 1) * 128],
                                               xbf[:, kc * 512:(kc + 1) * 512],
                                               start=(kc == 0), stop=(kc == 7))
                    c.inc(ins, "c2")
                    c.wait(nc.scalar, "c2", half + 1)
                    c.wait(nc.scalar, "c1", 3)
                    ins = nc.scalar.activation(
                        a_sb[:, ds((mhh * 2 + half) * 512, 512)], ps, AF.Relu,
                        bias=bcol[:, half:half + 1])
                    c.inc(ins, "c3")
                c.wait(nc.sync, "c3", 2)
                close(E)
            psy = [PS_A[:, i * 512:(i + 1) * 512] for i in range(4)] + \
                  [PS_B[:, i * 512:(i + 1) * 512] for i in range(4)]
            c = E.c
            dma(nc, "sync", E, "d", w2q[:].rearrange("p (c f) -> p c f", c=4),
                E.wq2d[ds(0, 4)].rearrange("c p f -> p c f"))
            c.wait(nc.tensor, "d")
            for ch in range(4):
                for mo in range(8):
                    ins = nc.tensor.matmul(
                        psy[mo], w2q[:, ch * 1024 + mo * 128:ch * 1024 + (mo + 1) * 128],
                        a_sb[:, ch * 512:(ch + 1) * 512], start=(ch == 0), stop=False)
            c.inc(ins, "c2")
            close(E)
            with nc.Fori(1, 7) as my:
                c = E.c
                dma(nc, "sync", E, "d", w2q[:].rearrange("p (c f) -> p c f", c=4),
                    E.wq2d[ds(my * 4, 4)].rearrange("c p f -> p c f"))
                c.wait(nc.tensor, "d")
                for ch in range(4):
                    for mo in range(8):
                        ins = nc.tensor.matmul(
                            psy[mo], w2q[:, ch * 1024 + mo * 128:ch * 1024 + (mo + 1) * 128],
                            a_sb[:, ds(my * 2048 + ch * 512, 512)],
                            start=False, stop=False)
                c.inc(ins, "c2")
                close(E)
            c = E.c
            dma(nc, "sync", E, "d", w2q[:].rearrange("p (c f) -> p c f", c=4),
                E.wq2d[ds(28, 4)].rearrange("c p f -> p c f"))
            c.wait(nc.tensor, "d")
            for ch in range(4):
                for mo in range(8):
                    ins = nc.tensor.matmul(
                        psy[mo], w2q[:, ch * 1024 + mo * 128:ch * 1024 + (mo + 1) * 128],
                        a_sb[:, 28 * 512 + ch * 512:28 * 512 + (ch + 1) * 512],
                        start=False, stop=(ch == 3))
            c.inc(ins, "c2")
            close(E)
            c = E.c
            for mo in range(8):
                ins = nc.scalar.activation(
                    yf[:, mo * 512:(mo + 1) * 512], psy[mo], AF.Identity,
                    bias=pvec[:, PV["d_b2"] + mo:PV["d_b2"] + mo + 1])
                c.inc(ins, "a")
            for mo in range(8):
                c.wait(nc.vector, "a", mo + 1)
                ins = nc.vector.tensor_tensor(yf[:, mo * 512:(mo + 1) * 512],
                                              yf[:, mo * 512:(mo + 1) * 512],
                                              xf[:, mo * 512:(mo + 1) * 512], ALU.add)
                c.inc(ins, "c1")
            for kc in range(8):
                zb = zbf[:, (kc % 2) * 512:(kc % 2) * 512 + 512]
                zq = zsq[:, (kc % 2) * 512:(kc % 2) * 512 + 512]
                c.wait(nc.vector, "c1", kc + 1)
                if kc >= 2:
                    c.wait(nc.vector, "c2", 2 * (kc - 2) + 2)
                i1 = nc.vector.tensor_copy(zb, yf[:, kc * 512:(kc + 1) * 512])
                c.inc(i1, "c1")
                c.wait(nc.scalar, "c1", kc + 1)
                if kc >= 2:
                    c.wait(nc.scalar, "c2", 2 * (kc - 2) + 2)
                i2 = nc.scalar.activation(zq, yf[:, kc * 512:(kc + 1) * 512], AF.Square)
                c.inc(i2, "a")
                c.wait(nc.tensor, "c1", 8 + kc + 1)
                c.wait(nc.tensor, "a", 8 + kc + 1)
                i3 = nc.tensor.matmul(PS_A[0:1, 0:512], E.ones_128x1_bf[:], zb,
                                      start=(kc == 0), stop=(kc == 7))
                i4 = nc.tensor.matmul(PS_A[0:1, 512:1024], E.ones_128x1_bf[:], zq,
                                      start=(kc == 0), stop=(kc == 7))
                c.inc(i3, "c2"); c.inc(i4, "c2")
            m_r = rows[:, 0:512]; q_r = rows[:, 512:1024]; v_r = rows[:, 1024:1536]
            r_r = rows[:, 1536:2048]; cc_r = rows[:, 2048:2560]
            c.wait(nc.vector, "c2", 16)
            iv = nc.vector.tensor_scalar(m_r, PS_A[0:1, 0:512], 1.0 / D, None, ALU.mult)
            c.inc(iv, "c1")
            iv = nc.vector.tensor_scalar(q_r, PS_A[0:1, 512:1024], 1.0 / D, None, ALU.mult)
            c.inc(iv, "c1")
            c.wait(nc.vector, "c1", 18)
            iv = nc.vector.tensor_tensor(v_r, m_r, m_r, ALU.mult)
            c.inc(iv, "c1")
            c.wait(nc.vector, "c1", 19)
            iv = nc.vector.tensor_tensor(v_r, q_r, v_r, ALU.subtract)
            c.inc(iv, "c1")
            c.wait(nc.scalar, "c1", 20)
            iv = nc.scalar.activation(v_r, v_r, AF.Ln, bias=float(EPS_MOD))
            c.inc(iv, "a")
            c.wait(nc.scalar, "a", 17)
            iv = nc.scalar.activation(r_r, v_r, AF.Exp, scale=-0.5)
            c.inc(iv, "a")
            c.wait(nc.vector, "a", 18)
            iv = nc.vector.scalar_tensor_tensor(cc_r, m_r, -1.0, r_r, ALU.mult, ALU.mult)
            c.inc(iv, "c1")
            c.wait(nc.tensor, "c1", 21)
            c.wait(nc.tensor, "a", 18)
            i1 = nc.tensor.matmul(PS_B[:, 0:512], E.ones_1x128[0:1, :], r_r,
                                  start=True, stop=True)
            i2 = nc.tensor.matmul(PS_B[:, 512:1024], E.ones_1x128[0:1, :], cc_r,
                                  start=True, stop=True)
            c.inc(i1, "c2"); c.inc(i2, "c2")
            tpb = [PS_B[:, 1024:1024 + 128], PS_B[:, 1536:1536 + 128]]
            ntr = 0
            for kc in range(8):
                c.wait(nc.vector, "c2", 18)
                c.wait(nc.vector, "c1", 21 + 2 * kc)
                iv = nc.vector.tensor_tensor(tmp[:], yf[:, kc * 512:(kc + 1) * 512],
                                             PS_B[:, 0:512], ALU.mult)
                c.inc(iv, "c1")
                c.wait(nc.vector, "c1", 22 + 2 * kc)
                iv = nc.vector.tensor_tensor(xf[:, kc * 512:(kc + 1) * 512], tmp[:],
                                             PS_B[:, 512:1024], ALU.add)
                c.inc(iv, "c1")
                for tt in range(4):
                    c.wait(nc.tensor, "c1", 23 + 2 * kc)
                    if ntr >= 2:
                        c.wait(nc.tensor, "a", 18 + (ntr - 2) + 1)
                    it = nc.tensor.matmul(
                        tpb[ntr % 2],
                        xf[:, kc * 512 + tt * 128:kc * 512 + (tt + 1) * 128],
                        E.ident_f32[:], start=True, stop=True)
                    c.inc(it, "c2")
                    c.wait(nc.scalar, "c2", 18 + ntr + 1)
                    ie = nc.scalar.copy(
                        obuf[:, tt * 1024 + kc * 128:tt * 1024 + (kc + 1) * 128],
                        tpb[ntr % 2])
                    c.inc(ie, "a")
                    ntr += 1
            c.wait(nc.scalar, "a", 50)
            for tt in range(4):
                nc.scalar.dma_start(
                    out=outv[ds(tc * 512 + tt * 128, 128), :],
                    in_=obuf[:, tt * 1024:(tt + 1) * 1024]).then_inc(bd["g"], 16)
                c.n["g"] += 16
            c.wait(nc.sync, "g")
            close(E)

def build_for_test(phase, pv_cols, pv_ncol, phase_arg=None):
    import contextlib
    E = make_base(pv_ncol, xa_kind="ExternalInput", with_x=False,
                  xb_kind="ExternalOutput" if phase in ("A", "B", "C1") else "Internal",
                  big_tabs=list(BIG_TABLES))
    E.pv = pv_cols
    nc = E.nc
    with contextlib.ExitStack() as st:
        if phase == "D":
            emit_phase_d(nc, E, st)
        elif phase == "A":
            emit_phase_a(nc, E, st)
        elif phase == "B":
            emit_phase_b(nc, E, st)
        elif phase == "C1":
            emit_c_attn(nc, E, st, phase_arg)
        elif phase == "C2":
            emit_c_ffn(nc, E, st, phase_arg)
        nc.all_engine_barrier()
    return E



def emit_input(nc, E, st):
    bd, PS_B = E.bd, E.psB
    xv = E.x_in.rearrange("b t d -> (b t) d")
    with nc.sbuf_tensor("in_xt", [128, D], F32) as xt, \
         nc.sbuf_tensor("in_fm", [128, FC * 128], F32) as fm:
        with nc.Fori(0, 32) as tt:
            c = E.c
            dma(nc, "sync", E, "d", xt[:], xv[ds(tt * 128, 128), :])
            c.wait(nc.tensor, "d")
            for fc in range(FC):
                slot = PS_B[:, (fc % 2) * 512:(fc % 2) * 512 + 128]
                if fc >= 2:
                    c.wait(nc.tensor, "a", fc - 1)
                it = nc.tensor.matmul(slot, xt[:, fc * 128:(fc + 1) * 128],
                                      E.ident_f32[:], start=True, stop=True)
                c.inc(it, "c2")
                c.wait(nc.scalar, "c2", fc + 1)
                ie = nc.scalar.copy(fm[:, fc * 128:(fc + 1) * 128], slot)
                c.inc(ie, "a")
            c.wait(nc.sync, "a", FC)
            dma(nc, "sync", E, "d",
                E.xa[:, :, ds(tt * 128, 128)].rearrange("c p f -> p c f"),
                fm[:].rearrange("p (c f) -> p c f", c=FC))
            c.wait(nc.sync, "d")
            close(E)



def emit_postln(nc, E, st, src_dram, dst_dram, xbf_dst=None, xbf_half=None,
                dst2=None, res_dram=None):
    bd, PS_A, PS_B = E.bd, E.psA, E.psB
    with nc.sbuf_tensor(f"pl_zf_{E.uid}", [128, FC * 512], F32) as zf, \
         nc.sbuf_tensor(f"pl_of_{E.uid}", [128, FC * 512], F32) as of, \
         nc.sbuf_tensor(f"pl_zb_{E.uid}", [128, 2 * 512], BF16) as zbf, \
         nc.sbuf_tensor(f"pl_zq_{E.uid}", [128, 2 * 512], BF16) as zsq, \
         nc.sbuf_tensor(f"pl_tmp_{E.uid}", [128, 512], F32) as tmp, \
         nc.sbuf_tensor(f"pl_rows_{E.uid}", [1, 5 * 512], F32) as rows, \
         nc.sbuf_tensor(f"pl_rsf_{E.uid}", [128, FC * 512 if res_dram is not None else 1], F32) as rsf:
        E.uid += 1
        with nc.Fori(0, 8) as tc:
            c = E.c
            dma(nc, "sync", E, "d", zf[:].rearrange("p (c f) -> p c f", c=FC),
                src_dram[:, :, ds(tc * 512, 512)].rearrange("c p f -> p c f"))
            if res_dram is not None:
                dma(nc, "sync", E, "d2", rsf[:].rearrange("p (c f) -> p c f", c=FC),
                    res_dram[:, :, ds(tc * 512, 512)].rearrange("c p f -> p c f"))
                c.wait(nc.vector, "d")
                c.wait(nc.vector, "d2")
                iv = nc.vector.tensor_tensor(zf[:], zf[:], rsf[:], ALU.add)
                c.inc(iv, "c3")
                c.wait(nc.scalar, "c3", 1)
                c.wait(nc.vector, "c3", 1)
                c.wait(nc.tensor, "c3", 1)
            c.wait(nc.vector, "d")
            c.wait(nc.scalar, "d")
            for kc in range(8):
                zb = zbf[:, (kc % 2) * 512:(kc % 2) * 512 + 512]
                zq = zsq[:, (kc % 2) * 512:(kc % 2) * 512 + 512]
                if kc >= 2:
                    c.wait(nc.vector, "c2", 2 * (kc - 2) + 2)
                    c.wait(nc.scalar, "c2", 2 * (kc - 2) + 2)
                i1 = nc.vector.tensor_copy(zb, zf[:, kc * 512:(kc + 1) * 512])
                c.inc(i1, "c1")
                i2 = nc.scalar.activation(zq, zf[:, kc * 512:(kc + 1) * 512], AF.Square)
                c.inc(i2, "a")
                c.wait(nc.tensor, "c1", kc + 1)
                c.wait(nc.tensor, "a", kc + 1)
                i3 = nc.tensor.matmul(PS_A[0:1, 0:512], E.ones_128x1_bf[:], zb,
                                      start=(kc == 0), stop=(kc == 7))
                i4 = nc.tensor.matmul(PS_A[0:1, 512:1024], E.ones_128x1_bf[:], zq,
                                      start=(kc == 0), stop=(kc == 7))
                c.inc(i3, "c2"); c.inc(i4, "c2")
            m_r = rows[:, 0:512]; q_r = rows[:, 512:1024]; v_r = rows[:, 1024:1536]
            r_r = rows[:, 1536:2048]; cc_r = rows[:, 2048:2560]
            c.wait(nc.vector, "c2", 16)
            iv = nc.vector.tensor_scalar(m_r, PS_A[0:1, 0:512], 1.0 / D, None, ALU.mult)
            c.inc(iv, "c1")
            iv = nc.vector.tensor_scalar(q_r, PS_A[0:1, 512:1024], 1.0 / D, None, ALU.mult)
            c.inc(iv, "c1")
            c.wait(nc.vector, "c1", 10)
            iv = nc.vector.tensor_tensor(v_r, m_r, m_r, ALU.mult)
            c.inc(iv, "c1")
            c.wait(nc.vector, "c1", 11)
            iv = nc.vector.tensor_tensor(v_r, q_r, v_r, ALU.subtract)
            c.inc(iv, "c1")
            c.wait(nc.scalar, "c1", 12)
            iv = nc.scalar.activation(v_r, v_r, AF.Ln, bias=float(EPS_MOD))
            c.inc(iv, "a")
            c.wait(nc.scalar, "a", 9)
            iv = nc.scalar.activation(r_r, v_r, AF.Exp, scale=-0.5)
            c.inc(iv, "a")
            c.wait(nc.vector, "a", 10)
            iv = nc.vector.scalar_tensor_tensor(cc_r, m_r, -1.0, r_r, ALU.mult, ALU.mult)
            c.inc(iv, "c1")
            c.wait(nc.tensor, "c1", 13)
            c.wait(nc.tensor, "a", 10)
            i1 = nc.tensor.matmul(PS_B[:, 0:512], E.ones_1x128[0:1, :], r_r,
                                  start=True, stop=True)
            i2 = nc.tensor.matmul(PS_B[:, 512:1024], E.ones_1x128[0:1, :], cc_r,
                                  start=True, stop=True)
            c.inc(i1, "c2"); c.inc(i2, "c2")
            nv = 13
            for kc in range(8):
                c.wait(nc.vector, "c2", 18)
                c.wait(nc.vector, "c1", nv)
                iv = nc.vector.tensor_tensor(tmp[:], zf[:, kc * 512:(kc + 1) * 512],
                                             PS_B[:, 0:512], ALU.mult)
                c.inc(iv, "c1"); nv += 1
                c.wait(nc.vector, "c1", nv)
                iv = nc.vector.tensor_tensor(of[:, kc * 512:(kc + 1) * 512], tmp[:],
                                             PS_B[:, 512:1024], ALU.add)
                c.inc(iv, "c1"); nv += 1
            if xbf_dst is not None:
                c.wait(nc.vector, "c1", nv)
                assert xbf_half is None
                dst = xbf_dst[:].rearrange("p (c f) -> p c f", c=FC)[:, :, ds(tc * 512, 512)]
                iv = nc.vector.tensor_copy(
                    dst, of[:].rearrange("p (c f) -> p c f", c=FC))
                c.inc(iv, "c1"); nv += 1
            c.wait(nc.sync, "c1", nv)
            dma(nc, "sync", E, "d",
                dst_dram[:, :, ds(tc * 512, 512)].rearrange("c p f -> p c f"),
                of[:].rearrange("p (c f) -> p c f", c=FC))
            if dst2 is not None:
                dma(nc, "sync", E, "d3",
                    dst2[:, :, ds(tc * 512, 512)].rearrange("c p f -> p c f"),
                    of[:].rearrange("p (c f) -> p c f", c=FC))
                c.wait(nc.sync, "d3")
            c.wait(nc.sync, "d")
            close(E)



def emit_phase_a(nc, E, st):
    PV = E.pv
    pvec = E.pvec
    bd, PS_A, PS_B = E.bd, E.psA, E.psB
    tabs = load_tables(nc, E, st, ["C0", "S0P", "M_FULL"])
    C0, S0P, M_FULL = tabs["C0"], tabs["S0P"], tabs["M_FULL"]

    xbf = st.enter_context(nc.sbuf_tensor("a_xbf", [128, FC * T], BF16))
    wstg = st.enter_context(nc.sbuf_tensor("a_wstg", [128, 512], F32))
    ws = st.enter_context(nc.sbuf_tensor("a_ws", [128, 8 * 64], BF16))
    ws2 = st.enter_context(nc.sbuf_tensor("a_ws2", [128, 8 * 64], BF16))
    wos_f = st.enter_context(nc.sbuf_tensor("a_wosf", [64, 16 * 128], F32))
    wos = st.enter_context(nc.sbuf_tensor("a_wos", [64, 16 * 128], BF16))
    bcol = st.enter_context(nc.sbuf_tensor("a_bcol", [128, 2], F32))
    qfp = st.enter_context(nc.sbuf_tensor("a_qfp", [64, T], F32))
    tmpa = st.enter_context(nc.sbuf_tensor("a_tmpa", [64, T], F32))
    tmpb = st.enter_context(nc.sbuf_tensor("a_tmpb", [64, T], F32))
    qp = st.enter_context(nc.sbuf_tensor("a_qp", [64, T], BF16))
    kr = st.enter_context(nc.sbuf_tensor("a_kr", [64, T], BF16))
    vbfh = st.enter_context(nc.sbuf_tensor("a_vbfh", [64, T], BF16))
    vt = st.enter_context(nc.sbuf_tensor("a_vt", [128, 16 * 65], BF16))
    pb0 = st.enter_context(nc.sbuf_tensor("a_pb0", [128, 1024], BF16))
    pb1 = st.enter_context(nc.sbuf_tensor("a_pb1", [128, 1024], BF16))
    pb2 = st.enter_context(nc.sbuf_tensor("a_pb2", [128, 1], BF16))
    cfp = st.enter_context(nc.sbuf_tensor("a_cfp", [64, T], F32))
    rrow = st.enter_context(nc.sbuf_tensor("a_rrow", [1, T], F32))
    ctmp = st.enter_context(nc.sbuf_tensor("a_ctmp", [64, T], BF16))
    ctx_tc = st.enter_context(nc.sbuf_tensor("a_ctxtc", [64, 16 * 512], BF16))
    xres = st.enter_context(nc.sbuf_tensor("a_xres", [128, FC * 512], F32))
    ofp = st.enter_context(nc.sbuf_tensor("a_ofp", [128, FC * 512], F32))

    nc.vector.memset(vt[:].rearrange("p (kt c) -> p kt c", kt=16)[:, :, 64:65], 1.0)
    close(E)

    wqv = E.l0_wq.rearrange("(c p) m -> p c m", p=128)
    wkv = E.l0_wk.rearrange("(c p) m -> p c m", p=128)
    wvv = E.l0_wv.rearrange("(c p) m -> p c m", p=128)
    wov = E.l0_wo.rearrange("(h p) m -> p h m", p=64)

    with nc.Fori(0, B) as b:
        with nc.Fori(0, 4) as tc:
            c = E.c
            dma(nc, "sync", E, "d", xres[:].rearrange("p (c f) -> p c f", c=FC),
                E.xa[:, :, ds(b * T + tc * 512, 512)].rearrange("c p f -> p c f"))
            c.wait(nc.vector, "d")
            iv = nc.vector.tensor_copy(
                xbf[:].rearrange("p (c f) -> p c f", c=FC)[:, :, ds(tc * 512, 512)],
                xres[:].rearrange("p (c f) -> p c f", c=FC))
            c.inc(iv, "c1")
            c.wait(nc.sync, "c1")
            close(E)
        with nc.Fori(0, 4) as g:
            c = E.c
            dma(nc, "sync", E, "d", wstg[:, 0:512].rearrange("p (c f) -> p c f", c=8),
                wkv[:, :, ds(g * 64, 64)])
            c.wait(nc.vector, "d")
            iv = nc.vector.tensor_copy(ws[:], wstg[:, 0:512])
            c.inc(iv, "c1")
            c.wait(nc.sync, "c1", 1)
            dma(nc, "sync", E, "d2", wstg[:, 0:512].rearrange("p (c f) -> p c f", c=8),
                wvv[:, :, ds(g * 64, 64)])
            c.wait(nc.vector, "d2")
            c.wait(nc.vector, "c1", 1)
            iv = nc.vector.tensor_copy(ws2[:], wstg[:, 0:512])
            c.inc(iv, "c1")
            iv = nc.vector.tensor_copy(bcol[:, 0:1], pvec[:, ds(PV["l0_bk"] + g, 1)])
            c.inc(iv, "c1")
            iv = nc.vector.tensor_copy(bcol[:, 1:2], pvec[:, ds(PV["l0_bv"] + g, 1)])
            c.inc(iv, "c1")
            c.wait(nc.tensor, "c1", 2)
            for tc in range(4):
                for kc in range(8):
                    ins = nc.tensor.matmul(
                        PS_A[0:64, tc * 512:(tc + 1) * 512],
                        ws[:, kc * 64:(kc + 1) * 64],
                        xbf[:, kc * T + tc * 512:kc * T + (tc + 1) * 512],
                        start=(kc == 0), stop=(kc == 7))
                c.inc(ins, "c2")
            c.wait(nc.scalar, "c1", 4)
            for tc in range(4):
                c.wait(nc.scalar, "c2", tc + 1)
                ie = nc.scalar.activation(qfp[:, tc * 512:(tc + 1) * 512],
                                          PS_A[0:64, tc * 512:(tc + 1) * 512],
                                          AF.Identity, bias=bcol[0:64, 0:1])
                c.inc(ie, "a")
            c.wait(nc.tensor, "a", 4)
            for tc in range(4):
                ins = nc.tensor.matmul(PS_B[0:64, tc * 512:(tc + 1) * 512],
                                       E.p64[0:64, 0:64], qfp[:, tc * 512:(tc + 1) * 512],
                                       start=True, stop=True)
                c.inc(ins, "c4")
            c.wait(nc.vector, "a", 4)
            iv = nc.vector.tensor_tensor(tmpa[:], qfp[:], C0[:], ALU.mult)
            c.inc(iv, "c3")
            c.wait(nc.vector, "c4", 4)
            iv = nc.vector.tensor_tensor(tmpb[:], PS_B[0:64, 0:T], S0P[:], ALU.mult)
            c.inc(iv, "c3")
            c.wait(nc.vector, "c3", 2)
            iv = nc.vector.tensor_tensor(kr[:], tmpa[:], tmpb[:], ALU.add)
            c.inc(iv, "c3")
            c.wait(nc.tensor, "c3", 3)
            for tc in range(4):
                for kc in range(8):
                    ins = nc.tensor.matmul(
                        PS_A[0:64, tc * 512:(tc + 1) * 512],
                        ws2[:, kc * 64:(kc + 1) * 64],
                        xbf[:, kc * T + tc * 512:kc * T + (tc + 1) * 512],
                        start=(kc == 0), stop=(kc == 7))
                c.inc(ins, "c2")
            for tc in range(4):
                c.wait(nc.scalar, "c2", 4 + tc + 1)
                ie = nc.scalar.activation(vbfh[:, tc * 512:(tc + 1) * 512],
                                          PS_A[0:64, tc * 512:(tc + 1) * 512],
                                          AF.Identity, bias=bcol[0:64, 1:2])
                c.inc(ie, "a")
            for kt in range(16):
                c.wait(nc.tensor, "a", 4 + min(4, (kt // 4) + 1))
                if kt >= 2:
                    c.wait(nc.tensor, "a", 8 + (kt - 2) + 1)
                slot = PS_B[:, 1024 + (kt % 2) * 512:1024 + (kt % 2) * 512 + 64]
                ins = nc.tensor.matmul(slot, vbfh[:, kt * 128:(kt + 1) * 128],
                                       E.ident_bf[0:64, 0:64], start=True, stop=True)
                c.inc(ins, "c4")
                c.wait(nc.scalar, "c4", 4 + kt + 1)
                ie = nc.scalar.copy(vt[:, kt * 65:kt * 65 + 64], slot)
                c.inc(ie, "a")
            close(E)
            with nc.Fori(0, 4) as hh:
                h = g * 4 + hh
                c = E.c
                dma(nc, "sync", E, "d", wstg[:, 0:512].rearrange("p (c f) -> p c f", c=8),
                    wqv[:, :, ds(h * 64, 64)])
                c.wait(nc.vector, "d")
                iv = nc.vector.tensor_copy(ws[:], wstg[:, 0:512])
                c.inc(iv, "c1")
                iv = nc.vector.tensor_copy(bcol[:, 0:1], pvec[:, ds(PV["l0_bq"] + h, 1)])
                c.inc(iv, "c1")
                iv = nc.vector.tensor_copy(bcol[:, 1:2], pvec[:, ds(PV["l0_hw"] + h, 1)])
                c.inc(iv, "c1")
                c.wait(nc.tensor, "c1", 1)
                for tc in range(4):
                    for kc in range(8):
                        ins = nc.tensor.matmul(
                            PS_A[0:64, tc * 512:(tc + 1) * 512],
                            ws[:, kc * 64:(kc + 1) * 64],
                            xbf[:, kc * T + tc * 512:kc * T + (tc + 1) * 512],
                            start=(kc == 0), stop=(kc == 7))
                    c.inc(ins, "c2")
                c.wait(nc.scalar, "c1", 2)
                for tc in range(4):
                    c.wait(nc.scalar, "c2", tc + 1)
                    ie = nc.scalar.activation(qfp[:, tc * 512:(tc + 1) * 512],
                                              PS_A[0:64, tc * 512:(tc + 1) * 512],
                                              AF.Identity, bias=bcol[0:64, 0:1])
                    c.inc(ie, "a")
                c.wait(nc.tensor, "a", 4)
                for tc in range(4):
                    ins = nc.tensor.matmul(PS_B[0:64, tc * 512:(tc + 1) * 512],
                                           E.p64[0:64, 0:64],
                                           qfp[:, tc * 512:(tc + 1) * 512],
                                           start=True, stop=True)
                    c.inc(ins, "c4")
                c.wait(nc.vector, "a", 4)
                iv = nc.vector.tensor_tensor(tmpa[:], qfp[:], C0[:], ALU.mult)
                c.inc(iv, "c3")
                c.wait(nc.vector, "c4", 4)
                iv = nc.vector.tensor_tensor(tmpb[:], PS_B[0:64, 0:T], S0P[:], ALU.mult)
                c.inc(iv, "c3")
                c.wait(nc.vector, "c3", 2)
                iv = nc.vector.tensor_tensor(tmpa[:], tmpa[:], tmpb[:], ALU.add)
                c.inc(iv, "c3")
                c.wait(nc.vector, "c3", 3)
                c.wait(nc.vector, "c1", 3)
                iv = nc.vector.tensor_scalar_mul(qp[:], tmpa[:], bcol[0:64, 1:2])
                c.inc(iv, "c3")
                close(E)
                c = E.c
                na = 0
                nm = 0
                nmm = 0
                ctx_open = {}
                plan = []
                for kt in range(16):
                    if kt == 0:
                        plan.append((0, [(0, T, "full")]))
                    elif kt == 1:
                        plan.append((1, [(0, 384, "b1")]))
                    else:
                        w = 256 if kt == 15 else 384
                        plan.append((kt, [(0, 128, "qa"), (128 * (kt - 1), w, "b")]))
                bank_last = {}
                for kt, pieces in plan:
                    for (q0, w, mk) in pieces:
                        for j in range(0, w, 512):
                            seg = min(512, w - j)
                            b0 = (q0 + j) // 512
                            bank_last[(kt, q0 + j)] = None
                touch = {}
                for kt, pieces in plan:
                    for (q0, w, mk) in pieces:
                        lo = q0 // 512
                        hi = (q0 + w - 1) // 512
                        for bk in range(lo, hi + 1):
                            touch[bk] = (kt, q0)
                pbufs = [pb0, pb1, pb2]
                nch = 0
                for kt, pieces in plan:
                    for (q0, w, mk) in pieces:
                        jj0 = 0
                        while jj0 < w:
                            chunk = min(1024, w - jj0)
                            qa0 = q0 + jj0
                            reg = PS_A[:, (nch % 2) * 1024:(nch % 2) * 1024 + 1024]
                            if nch >= 2:
                                c.wait(nc.tensor, "a", nch - 1)
                            kk = 0
                            while kk < chunk:
                                seg = min(512, chunk - kk)
                                ins = nc.tensor.matmul(
                                    reg[:, kk:kk + seg],
                                    kr[:, kt * 128:(kt + 1) * 128],
                                    qp[:, qa0 + kk:qa0 + kk + seg],
                                    start=True, stop=True)
                                c.inc(ins, "c2"); nmm += 1
                                kk += seg
                            c.wait(nc.vector, "c2", c.n["c2"])
                            if mk == "full":
                                msrc = M_FULL[:, jj0:jj0 + chunk]
                            elif mk == "b1":
                                msrc = E.m_b1[:, jj0:jj0 + chunk]
                            elif mk == "qa":
                                msrc = E.m_a[:, 0:chunk]
                            else:
                                msrc = E.m_b[:, jj0:jj0 + chunk]
                            iv = nc.vector.tensor_tensor(reg[:, 0:chunk],
                                                         reg[:, 0:chunk], msrc, ALU.add)
                            c.inc(iv, "c1"); nm += 1
                            pbuf = pbufs[nch % 2]
                            c.wait(nc.scalar, "c1", nm)
                            c.wait(nc.scalar, "c2", c.n["c2"])
                            ie = nc.scalar.activation(pbuf[:, 0:chunk], reg[:, 0:chunk],
                                                      AF.Exp, scale=0.125)
                            c.inc(ie, "a"); na += 1; nch += 1
                            c.wait(nc.tensor, "a", na)
                            kk = 0
                            while kk < chunk:
                                qabs = qa0 + kk
                                bank = qabs // 512
                                seg = min(512 - (qabs % 512), chunk - kk)
                                stt_flag = bank not in ctx_open
                                ctx_open[bank] = True
                                ins = nc.tensor.matmul(
                                    PS_B[0:65, qabs:qabs + seg],
                                    vt[:, kt * 65:(kt + 1) * 65],
                                    pbuf[:, kk:kk + seg],
                                    start=stt_flag, stop=(touch[bank] == (kt, q0)))
                                c.inc(ins, "c2")
                                kk += seg
                            jj0 += chunk
                c.wait(nc.vector, "c2", c.n["c2"])
                iv = nc.vector.reciprocal(rrow[:], PS_B[64:65, 0:T])
                c.inc(iv, "c1"); nm += 1
                c.wait(nc.tensor, "c1", nm)
                for tc in range(4):
                    ins = nc.tensor.matmul(PS_A[0:64, tc * 512:(tc + 1) * 512],
                                           E.ones_1x128[0:1, 0:64],
                                           rrow[:, tc * 512:(tc + 1) * 512],
                                           start=True, stop=True)
                    c.inc(ins, "c4")
                c.wait(nc.scalar, "c2", c.n["c2"])
                ie = nc.scalar.copy(cfp[:], PS_B[0:64, 0:T])
                c.inc(ie, "a"); na += 1
                c.wait(nc.vector, "a", na)
                c.wait(nc.vector, "c4", 4)
                iv = nc.vector.tensor_tensor(ctmp[:], cfp[:], PS_A[0:64, 0:T], ALU.mult)
                c.inc(iv, "c1")
                c.wait(nc.scalar, "c1", c.n["c1"])
                nc.scalar.dma_start(
                    out=E.ctxd[ds(h, 1)].rearrange("o p f -> (o p) f"),
                    in_=ctmp[:]).then_inc(bd["g"], 16)
                c.n["g"] += 16
                c.wait(nc.sync, "g")
                close(E)
        with nc.Fori(0, 4) as tc:
            c = E.c
            dma(nc, "sync", E, "d", ctx_tc[:].rearrange("p (h f) -> p h f", h=16),
                E.ctxd[:, :, ds(tc * 512, 512)].rearrange("h p f -> p h f"))
            dma(nc, "sync", E, "d", xres[:].rearrange("p (c f) -> p c f", c=FC),
                E.xa[:, :, ds(b * T + tc * 512, 512)].rearrange("c p f -> p c f"))
            for m in range(8):
                psem = ("d2", "d3")[m % 2]
                if m >= 1:
                    c.wait(nc.sync, "c1", m)
                dma(nc, "sync", E, psem, wos_f[:].rearrange("p (h f) -> p h f", h=16),
                    wov[:, :, ds(m * 128, 128)])
                c.wait(nc.vector, psem)
                if m >= 1:
                    c.wait(nc.vector, "c2", m)
                iv = nc.vector.tensor_copy(wos[:], wos_f[:])
                c.inc(iv, "c1")
                c.wait(nc.tensor, "c1", m + 1)
                c.wait(nc.tensor, "d", 32)
                if m >= 2:
                    c.wait(nc.tensor, "a", m - 1)
                pbank = PS_A[:, (m % 2) * 512:(m % 2) * 512 + 512]
                for hx in range(16):
                    ins = nc.tensor.matmul(
                        pbank, wos[0:64, hx * 128:(hx + 1) * 128],
                        ctx_tc[0:64, hx * 512:(hx + 1) * 512],
                        start=(hx == 0), stop=(hx == 15))
                c.inc(ins, "c2")
                c.wait(nc.scalar, "c2", m + 1)
                ie = nc.scalar.activation(ofp[:, m * 512:(m + 1) * 512], pbank,
                                          AF.Identity,
                                          bias=pvec[:, PV["l0_bo"] + m:PV["l0_bo"] + m + 1])
                c.inc(ie, "a")
                c.wait(nc.vector, "a", m + 1)
                iv = nc.vector.tensor_tensor(ofp[:, m * 512:(m + 1) * 512],
                                             ofp[:, m * 512:(m + 1) * 512],
                                             xres[:, m * 512:(m + 1) * 512], ALU.add)
                c.inc(iv, "c3")
            c.wait(nc.sync, "c3", 8)
            dma(nc, "sync", E, "d",
                E.xb[:, :, ds(b * T + tc * 512, 512)].rearrange("c p f -> p c f"),
                ofp[:].rearrange("p (c f) -> p c f", c=FC))
            c.wait(nc.sync, "d")
            close(E)



def emit_phase_b(nc, E, st):
    PV = E.pv
    bd, PS_A, PS_B = E.bd, E.psA, E.psB

    stg2 = st.enter_context(nc.sbuf_tensor("b_stg2", [128, 8192], F32))
    xq = st.enter_context(nc.sbuf_tensor("b_xq", [128, 8 * 1024], BF16))
    oacc = st.enter_context(nc.sbuf_tensor("b_oacc", [128, 8 * 1024], F32))
    w1b = st.enter_context(nc.sbuf_tensor("b_w1b", [128, 8 * 1024], BF16))
    w3b = st.enter_context(nc.sbuf_tensor("b_w3b", [128, 8 * 1024], BF16))
    w2b = st.enter_context(nc.sbuf_tensor("b_w2b", [128, 8 * 1024], BF16))
    a_sb = st.enter_context(nc.sbuf_tensor("b_asb", [128, 8 * 512], BF16))
    sfp = st.enter_context(nc.sbuf_tensor("b_sfp", [128, 512], F32))
    t2f = st.enter_context(nc.sbuf_tensor("b_t2f", [128, 512], F32))
    cwb = st.enter_context(nc.sbuf_tensor("b_cwb", [128, 1024], F32))
    sel_s = st.enter_context(nc.sbuf_tensor("b_sel", [9, 128], F32))
    rbf = st.enter_context(nc.sbuf_tensor("b_rbf", [128, 64], BF16))
    lrow = st.enter_context(nc.sbuf_tensor("b_lrow", [8, 512], F32))
    ltok = st.enter_context(nc.sbuf_tensor("b_ltok", [128, 4 * 8], F32))
    prb = st.enter_context(nc.sbuf_tensor("b_prb", [128, 8], F32))
    tmp8 = st.enter_context(nc.sbuf_tensor("b_tmp8", [128, 8], F32))
    cw_tok = st.enter_context(nc.sbuf_tensor("b_cwtok", [128, 8], F32))
    sc1 = st.enter_context(nc.sbuf_tensor("b_sc1", [128, 6], F32))
    cwt = st.enter_context(nc.sbuf_tensor("b_cwt", [9, N], F32))
    xres = st.enter_context(nc.sbuf_tensor("b_xres", [128, FC * 512], F32))

    c = E.c
    dma(nc, "sync", E, "d", stg2[:, 0:64].rearrange("p (c f) -> p c f", c=8),
        E.router.rearrange("(c p) f -> p c f", p=128))
    c.wait(nc.vector, "d")
    iv = nc.vector.tensor_copy(rbf[:], stg2[:, 0:64])
    c.inc(iv, "c1")
    dma(nc, "sync", E, "d2", cwt[8:9, :], E.tab_d["ONES_ROW"])
    c.wait(nc.sync, "d2")
    c.wait(nc.sync, "c1", 1)
    close(E)

    with nc.Fori(0, 8) as tc:
        c = E.c
        dma(nc, "sync", E, "d",
            stg2[:, 0:4096].rearrange("p (c f) -> p c f", c=FC)[:, :, 0:512],
            E.xa[:, :, ds(tc * 512, 512)].rearrange("c p f -> p c f"))
        c.wait(nc.vector, "d")
        iv = nc.vector.tensor_copy(xq[:, 0:4096].rearrange("p (c f) -> p c f", c=8),
                                   stg2[:, 0:4096].rearrange("p (c f) -> p c f", c=FC)[:, :, 0:512])
        c.inc(iv, "c1")
        c.wait(nc.tensor, "c1", 1)
        for kc in range(8):
            ins = nc.tensor.matmul(PS_A[0:8, 0:512], rbf[:, kc * 8:(kc + 1) * 8],
                                   xq[:, kc * 512:(kc + 1) * 512],
                                   start=(kc == 0), stop=(kc == 7))
        c.inc(ins, "c2")
        c.wait(nc.scalar, "c2", 1)
        ie = nc.scalar.copy(lrow[:], PS_A[0:8, 0:512])
        c.inc(ie, "a")
        n1, na, n2 = 1, 1, 1
        for tt in range(4):
            c.wait(nc.tensor, "a", 1)
            it = nc.tensor.matmul(PS_B[:, tt * 512:tt * 512 + 8],
                                  lrow[0:8, tt * 128:(tt + 1) * 128],
                                  E.ident_f32[0:8, 0:8], start=True, stop=True)
            c.inc(it, "c2"); n2 += 1
            c.wait(nc.scalar, "c2", n2)
            ie = nc.scalar.copy(ltok[:, tt * 8:(tt + 1) * 8], PS_B[:, tt * 512:tt * 512 + 8])
            c.inc(ie, "a"); na += 1
            lt = ltok[:, tt * 8:(tt + 1) * 8]
            c.wait(nc.vector, "a", na)
            iv = nc.vector.reduce_max(sc1[:, 0:1], lt, axis=AX, negate=True)
            c.inc(iv, "c1"); n1 += 1
            c.wait(nc.scalar, "c1", n1)
            ie = nc.scalar.activation(prb[:], lt, AF.Exp, bias=sc1[:, 0:1],
                                      accum_out=sc1[:, 1:2])
            c.inc(ie, "a"); na += 1
            c.wait(nc.vector, "a", na)
            iv = nc.vector.reciprocal(sc1[:, 2:3], sc1[:, 1:2])
            c.inc(iv, "c1"); n1 += 1
            c.wait(nc.vector, "c1", n1)
            iv = nc.vector.tensor_scalar_mul(prb[:], prb[:], sc1[:, 2:3])
            c.inc(iv, "c1"); n1 += 1
            c.wait(nc.vector, "c1", n1)
            iv = nc.vector.reduce_max(sc1[:, 3:4], prb[:], axis=AX)
            c.inc(iv, "c1"); n1 += 1
            c.wait(nc.vector, "c1", n1)
            iv = nc.vector.tensor_single_scalar(cw_tok[:], prb[:], sc1[:, 3:4], ALU.is_ge)
            c.inc(iv, "c1"); n1 += 1
            c.wait(nc.vector, "c1", n1)
            iv = nc.vector.scalar_tensor_tensor(tmp8[:], cw_tok[:], -1e3, prb[:],
                                                ALU.mult, ALU.add)
            c.inc(iv, "c1"); n1 += 1
            c.wait(nc.vector, "c1", n1)
            iv = nc.vector.reduce_max(sc1[:, 4:5], tmp8[:], axis=AX)
            c.inc(iv, "c1"); n1 += 1
            c.wait(nc.vector, "c1", n1)
            iv = nc.vector.tensor_single_scalar(cw_tok[:], prb[:], sc1[:, 4:5], ALU.is_ge)
            c.inc(iv, "c1"); n1 += 1
            c.wait(nc.vector, "c1", n1)
            iv = nc.vector.tensor_tensor(sc1[:, 5:6], sc1[:, 3:4], sc1[:, 4:5], ALU.add)
            c.inc(iv, "c1"); n1 += 1
            c.wait(nc.vector, "c1", n1)
            iv = nc.vector.reciprocal(sc1[:, 5:6], sc1[:, 5:6])
            c.inc(iv, "c1"); n1 += 1
            c.wait(nc.vector, "c1", n1)
            iv = nc.vector.tensor_tensor(cw_tok[:], cw_tok[:], prb[:], ALU.mult)
            c.inc(iv, "c1"); n1 += 1
            c.wait(nc.vector, "c1", n1)
            iv = nc.vector.tensor_scalar_mul(cw_tok[:], cw_tok[:], sc1[:, 5:6])
            c.inc(iv, "c1"); n1 += 1
            c.wait(nc.tensor, "c1", n1)
            if tt >= 1:
                c.wait(nc.tensor, "a", na)
            it = nc.tensor.matmul(PS_B[0:8, 1024 + tt * 128:1024 + (tt + 1) * 128],
                                  cw_tok[:], E.ident_f32[:], start=True, stop=True)
            c.inc(it, "c2"); n2 += 1
            c.wait(nc.scalar, "c2", n2)
            ie = nc.scalar.copy(cwt[0:8, ds(tc * 512 + tt * 128, 128)],
                                PS_B[0:8, 1024 + tt * 128:1024 + (tt + 1) * 128])
            c.inc(ie, "a"); na += 1
        c.wait(nc.sync, "a", na)
        close(E)

    def b_quarter(qi, cached):
        c = E.c
        for r in range(2):
            dma(nc, "sync", E, ("d", "d2")[r],
                stg2[:, 0:4096].rearrange("p (c f) -> p c f", c=FC)[:, :, 0:512],
                E.xa[:, :, ds(qi * 1024 + r * 512, 512)].rearrange("c p f -> p c f"))
            c.wait(nc.vector, ("d", "d2")[r])
            if r == 1:
                c.wait(nc.vector, "c1", 1)
            iv = nc.vector.tensor_copy(
                xq[:].rearrange("p (c f) -> p c f", c=8)[:, :, r * 512:(r + 1) * 512],
                stg2[:, 0:4096].rearrange("p (c f) -> p c f", c=FC)[:, :, 0:512])
            c.inc(iv, "c1")
            if r == 0:
                c.wait(nc.sync, "c1", 1)
        iv = nc.vector.memset(oacc[:], 0.0)
        c.inc(iv, "c1")
        c.wait(nc.sync, "c1", 3)
        close(E)
        with nc.Fori(0, 9) as e:
            c = E.c
            iv = nc.vector.tensor_copy(sel_s[:], E.sel[:, ds(e * 128, 128)])
            c.inc(iv, "c1")
            c.wait(nc.tensor, "c1", 1)
            for half in range(2):
                it = nc.tensor.matmul(PS_B[:, 1024 + half * 512:1024 + (half + 1) * 512],
                                      sel_s[:], cwt[0:9, ds(qi * 1024 + half * 512, 512)],
                                      start=True, stop=True)
                c.inc(it, "c2")
            c.wait(nc.scalar, "c2", 2)
            ie = nc.scalar.copy(cwb[:], PS_B[:, 1024:2048])
            c.inc(ie, "a")
            c.wait(nc.sync, "a", 1)
            close(E)
            with nc.Fori(0, 2) as hh:
                if not cached:
                    c = E.c
                    for wi, (wsrc, wdst) in enumerate(((E.mw1, w1b), (E.mw3, w3b), (E.mw2, w2b))):
                        psem = ("d", "d2", "d3")[wi]
                        sbuf_half = stg2[:]
                        if wi >= 1:
                            c.wait(nc.sync, "c1", wi)
                        if wi < 2:
                            srcap = wsrc[ds(e, 1), :, ds(hh * 1024, 1024)]
                        else:
                            srcap = wsrc[ds(e, 1), ds(hh * 1024, 1024), :]
                        dma(nc, "sync", E, psem,
                            sbuf_half.rearrange("p (c f) -> p c f", c=8),
                            srcap.rearrange("o (kc p) f -> p (o kc) f", p=128))
                        c.wait(nc.vector, psem)
                        iv = nc.vector.tensor_copy(
                            wdst[:].rearrange("p (c f) -> p c f", c=8),
                            sbuf_half.rearrange("p (c f) -> p c f", c=8))
                        c.inc(iv, "c1")
                    c.wait(nc.sync, "c1", 3)
                    close(E)
                c = E.c
                if cached:
                    for wi, wdst in enumerate((w1b, w3b, w2b)):
                        dma(nc, ("sync", "scalar", "sync")[wi], E,
                            ("d", "d2", "d3")[wi], wdst[:],
                            E.mwbf[ds(e, 1), ds(hh, 1), wi * 128:(wi + 1) * 128, :]
                                .rearrange("a b p f -> (a b p) f"))
                else:
                    for wi, wsb_ in enumerate((w1b, w3b, w2b)):
                        dma(nc, "scalar", E, "c4",
                            E.mwbf[ds(e, 1), ds(hh, 1), wi * 128:(wi + 1) * 128, :]
                                .rearrange("a b p f -> (a b p) f"),
                            wsb_[:])
                n1 = n2 = na = n3 = 0
                for tc in range(2):
                    for mh in range(8):
                        psh1 = PS_A[:, (mh % 2) * 512:(mh % 2) * 512 + 512]
                        psh3 = PS_A[:, 1024 + (mh % 2) * 512:1024 + (mh % 2) * 512 + 512]
                        if (tc, mh) > (0, 1):
                            nprev = (tc * 8 + mh) - 2
                            c.wait(nc.tensor, "a", nprev + 1)
                            c.wait(nc.tensor, "c1", 2 * nprev + 2)
                        if cached and tc == 0 and mh == 0:
                            c.wait(nc.tensor, "d", 16)
                            c.wait(nc.tensor, "d2", 16)
                        for kc in range(8):
                            i1 = nc.tensor.matmul(psh1, w1b[:, kc * 1024 + mh * 128:kc * 1024 + (mh + 1) * 128],
                                                  xq[:, kc * 1024 + tc * 512:kc * 1024 + tc * 512 + 512],
                                                  start=(kc == 0), stop=(kc == 7))
                        c.inc(i1, "c2"); n2 += 1
                        for kc in range(8):
                            i1 = nc.tensor.matmul(psh3, w3b[:, kc * 1024 + mh * 128:kc * 1024 + (mh + 1) * 128],
                                                  xq[:, kc * 1024 + tc * 512:kc * 1024 + tc * 512 + 512],
                                                  start=(kc == 0), stop=(kc == 7))
                        c.inc(i1, "c2"); n2 += 1
                        c.wait(nc.scalar, "c2", n2 - 1)
                        if n1 >= 1:
                            c.wait(nc.scalar, "c1", n1 - 1)
                        ie = nc.scalar.activation(sfp[:], psh1, AF.Silu)
                        c.inc(ie, "a"); na += 1
                        c.wait(nc.vector, "a", na)
                        c.wait(nc.vector, "c2", n2)
                        c.wait(nc.vector, "c1", n1)
                        iv = nc.vector.tensor_tensor(t2f[:], sfp[:], psh3, ALU.mult)
                        c.inc(iv, "c1"); n1 += 1
                        c.wait(nc.vector, "c1", n1)
                        iv = nc.vector.tensor_tensor(
                            a_sb[:, mh * 512:(mh + 1) * 512], t2f[:],
                            cwb[:, tc * 512:(tc + 1) * 512], ALU.mult)
                        c.inc(iv, "c1"); n1 += 1
                    if cached and tc == 0:
                        c.wait(nc.tensor, "d3", 16)
                    for mo in range(8):
                        psy = PS_B[:, (mo % 2) * 512:(mo % 2) * 512 + 512]
                        c.wait(nc.tensor, "c1", n1)
                        if mo >= 2:
                            c.wait(nc.tensor, "c3", n3 - 1)
                        for kch in range(8):
                            i1 = nc.tensor.matmul(psy, w2b[:, kch * 1024 + mo * 128:kch * 1024 + (mo + 1) * 128],
                                                  a_sb[:, kch * 512:(kch + 1) * 512],
                                                  start=(kch == 0), stop=(kch == 7))
                        c.inc(i1, "c2"); n2 += 1
                        c.wait(nc.vector, "c2", n2)
                        iv = nc.vector.tensor_tensor(
                            oacc[:, mo * 1024 + tc * 512:mo * 1024 + tc * 512 + 512],
                            oacc[:, mo * 1024 + tc * 512:mo * 1024 + tc * 512 + 512],
                            psy, ALU.add)
                        c.inc(iv, "c3"); n3 += 1
                c.wait(nc.sync, "c3", n3)
                if not cached:
                    c.wait(nc.sync, "c4")
                close(E)
        c = E.c
        for r in range(2):
            dma(nc, "sync", E, ("d", "d2")[r],
                xres[:].rearrange("p (c f) -> p c f", c=FC),
                E.xa[:, :, ds(qi * 1024 + r * 512, 512)].rearrange("c p f -> p c f"))
            c.wait(nc.vector, ("d", "d2")[r])
            for mo in range(8):
                iv = nc.vector.tensor_tensor(
                    oacc[:, mo * 1024 + r * 512:mo * 1024 + r * 512 + 512],
                    oacc[:, mo * 1024 + r * 512:mo * 1024 + r * 512 + 512],
                    xres[:, mo * 512:(mo + 1) * 512], ALU.add)
                c.inc(iv, "c1")
            c.wait(nc.sync, "c1")
            dma(nc, "sync", E, "d3",
                E.xb[:, :, ds(qi * 1024 + r * 512, 512)].rearrange("c p f -> p c f"),
                oacc[:].rearrange("p (c f) -> p c f", c=8)[:, :, r * 512:(r + 1) * 512])
        c.wait(nc.sync, "d")
        c.wait(nc.sync, "d2")
        c.wait(nc.sync, "d3")
        close(E)

    with nc.Fori(0, 1) as qi:
        b_quarter(qi, cached=False)
    if getattr(E, "rr", None) is not None:
        E.rr()
    with nc.Fori(1, 4) as qi:
        b_quarter(qi, cached=True)



def c_ln_pass(nc, E, src_dram, xnbf, lnsta, woff, boff, tag):
    bd, PS_A, PS_B = E.bd, E.psA, E.psB
    import contextlib
    with contextlib.ExitStack() as stl:
        zf = stl.enter_context(nc.sbuf_tensor(f"cl_zf_{tag}", [128, FC * 512], F32))
        zbf = stl.enter_context(nc.sbuf_tensor(f"cl_zb_{tag}", [128, 2 * 512], BF16))
        zsq = stl.enter_context(nc.sbuf_tensor(f"cl_zq_{tag}", [128, 2 * 512], BF16))
        tmp = stl.enter_context(nc.sbuf_tensor(f"cl_tmp_{tag}", [128, 512], F32))
        tmp2 = stl.enter_context(nc.sbuf_tensor(f"cl_tmp2_{tag}", [128, 512], F32))
        rows = stl.enter_context(nc.sbuf_tensor(f"cl_rows_{tag}", [1, 5 * 512], F32))
        with nc.Fori(0, 8) as tc:
            c = E.c
            dma(nc, "sync", E, "d", zf[:].rearrange("p (c f) -> p c f", c=FC),
                src_dram[:, :, ds(tc * 512, 512)].rearrange("c p f -> p c f"))
            c.wait(nc.vector, "d")
            c.wait(nc.scalar, "d")
            for kc in range(8):
                zb = zbf[:, (kc % 2) * 512:(kc % 2) * 512 + 512]
                zq = zsq[:, (kc % 2) * 512:(kc % 2) * 512 + 512]
                if kc >= 2:
                    c.wait(nc.vector, "c2", 2 * (kc - 2) + 2)
                    c.wait(nc.scalar, "c2", 2 * (kc - 2) + 2)
                i1 = nc.vector.tensor_copy(zb, zf[:, kc * 512:(kc + 1) * 512])
                c.inc(i1, "c1")
                i2 = nc.scalar.activation(zq, zf[:, kc * 512:(kc + 1) * 512], AF.Square)
                c.inc(i2, "a")
                c.wait(nc.tensor, "c1", kc + 1)
                c.wait(nc.tensor, "a", kc + 1)
                i3 = nc.tensor.matmul(PS_A[0:1, 0:512], E.ones_128x1_bf[:], zb,
                                      start=(kc == 0), stop=(kc == 7))
                i4 = nc.tensor.matmul(PS_A[0:1, 512:1024], E.ones_128x1_bf[:], zq,
                                      start=(kc == 0), stop=(kc == 7))
                c.inc(i3, "c2"); c.inc(i4, "c2")
            m_r = rows[:, 0:512]; q_r = rows[:, 512:1024]; v_r = rows[:, 1024:1536]
            r_r = rows[:, 1536:2048]; cc_r = rows[:, 2048:2560]
            c.wait(nc.vector, "c2", 16)
            iv = nc.vector.tensor_scalar(m_r, PS_A[0:1, 0:512], 1.0 / D, None, ALU.mult)
            c.inc(iv, "c1")
            iv = nc.vector.tensor_scalar(q_r, PS_A[0:1, 512:1024], 1.0 / D, None, ALU.mult)
            c.inc(iv, "c1")
            c.wait(nc.vector, "c1", 10)
            iv = nc.vector.tensor_tensor(v_r, m_r, m_r, ALU.mult)
            c.inc(iv, "c1")
            c.wait(nc.vector, "c1", 11)
            iv = nc.vector.tensor_tensor(v_r, q_r, v_r, ALU.subtract)
            c.inc(iv, "c1")
            c.wait(nc.scalar, "c1", 12)
            iv = nc.scalar.activation(v_r, v_r, AF.Ln, bias=float(EPS_L1))
            c.inc(iv, "a")
            c.wait(nc.scalar, "a", 9)
            iv = nc.scalar.activation(r_r, v_r, AF.Exp, scale=-0.5)
            c.inc(iv, "a")
            c.wait(nc.vector, "a", 10)
            iv = nc.vector.scalar_tensor_tensor(cc_r, m_r, -1.0, r_r, ALU.mult, ALU.mult)
            c.inc(iv, "c1")
            c.wait(nc.tensor, "c1", 13)
            c.wait(nc.tensor, "a", 10)
            i1 = nc.tensor.matmul(PS_B[:, 0:512], E.ones_1x128[0:1, :], r_r,
                                  start=True, stop=True)
            i2 = nc.tensor.matmul(PS_B[:, 512:1024], E.ones_1x128[0:1, :], cc_r,
                                  start=True, stop=True)
            c.inc(i1, "c2"); c.inc(i2, "c2")
            nv = 13
            nsc = 10
            for kc in range(8):
                c.wait(nc.vector, "c2", 18)
                c.wait(nc.vector, "a", nsc)
                iv = nc.vector.tensor_tensor(tmp[:], zf[:, kc * 512:(kc + 1) * 512],
                                             PS_B[:, 0:512], ALU.mult)
                c.inc(iv, "c1"); nv += 1
                c.wait(nc.vector, "c1", nv)
                iv = nc.vector.tensor_tensor(tmp2[:], tmp[:], PS_B[:, 512:1024], ALU.add)
                c.inc(iv, "c1"); nv += 1
                c.wait(nc.scalar, "c1", nv)
                ie = nc.scalar.activation(
                    xnbf[:, ds(kc * N + tc * 512, 512)],
                    tmp2[:], AF.Identity, scale=lnsta[:, woff + kc:woff + kc + 1],
                    bias=lnsta[:, boff + kc:boff + kc + 1])
                c.inc(ie, "a"); nsc += 1
            c.wait(nc.sync, "a", nsc)
            close(E)


def c_ln_pass_b(nc, E, src_dram, b, xnbf, lnsta, woff, boff, tag):
    bd, PS_A, PS_B = E.bd, E.psA, E.psB
    import contextlib
    with contextlib.ExitStack() as stl:
        zf = stl.enter_context(nc.sbuf_tensor(f"cb_zf_{tag}", [128, FC * 512], F32))
        zbf = stl.enter_context(nc.sbuf_tensor(f"cb_zb_{tag}", [128, 2 * 512], BF16))
        zsq = stl.enter_context(nc.sbuf_tensor(f"cb_zq_{tag}", [128, 2 * 512], BF16))
        tmp = stl.enter_context(nc.sbuf_tensor(f"cb_tmp_{tag}", [128, 512], F32))
        tmp2 = stl.enter_context(nc.sbuf_tensor(f"cb_tmp2_{tag}", [128, 512], F32))
        rows = stl.enter_context(nc.sbuf_tensor(f"cb_rows_{tag}", [1, 5 * 512], F32))
        with nc.Fori(0, 4) as tc:
            c = E.c
            dma(nc, "sync", E, "d", zf[:].rearrange("p (c f) -> p c f", c=FC),
                src_dram[:, :, ds(b * T + tc * 512, 512)].rearrange("c p f -> p c f"))
            c.wait(nc.vector, "d")
            c.wait(nc.scalar, "d")
            for kc in range(8):
                zb = zbf[:, (kc % 2) * 512:(kc % 2) * 512 + 512]
                zq = zsq[:, (kc % 2) * 512:(kc % 2) * 512 + 512]
                if kc >= 2:
                    c.wait(nc.vector, "c2", 2 * (kc - 2) + 2)
                    c.wait(nc.scalar, "c2", 2 * (kc - 2) + 2)
                i1 = nc.vector.tensor_copy(zb, zf[:, kc * 512:(kc + 1) * 512])
                c.inc(i1, "c1")
                i2 = nc.scalar.activation(zq, zf[:, kc * 512:(kc + 1) * 512], AF.Square)
                c.inc(i2, "a")
                c.wait(nc.tensor, "c1", kc + 1)
                c.wait(nc.tensor, "a", kc + 1)
                i3 = nc.tensor.matmul(PS_A[0:1, 0:512], E.ones_128x1_bf[:], zb,
                                      start=(kc == 0), stop=(kc == 7))
                i4 = nc.tensor.matmul(PS_A[0:1, 512:1024], E.ones_128x1_bf[:], zq,
                                      start=(kc == 0), stop=(kc == 7))
                c.inc(i3, "c2"); c.inc(i4, "c2")
            m_r = rows[:, 0:512]; q_r = rows[:, 512:1024]; v_r = rows[:, 1024:1536]
            r_r = rows[:, 1536:2048]; cc_r = rows[:, 2048:2560]
            c.wait(nc.vector, "c2", 16)
            iv = nc.vector.tensor_scalar(m_r, PS_A[0:1, 0:512], 1.0 / D, None, ALU.mult)
            c.inc(iv, "c1")
            iv = nc.vector.tensor_scalar(q_r, PS_A[0:1, 512:1024], 1.0 / D, None, ALU.mult)
            c.inc(iv, "c1")
            c.wait(nc.vector, "c1", 10)
            iv = nc.vector.tensor_tensor(v_r, m_r, m_r, ALU.mult)
            c.inc(iv, "c1")
            c.wait(nc.vector, "c1", 11)
            iv = nc.vector.tensor_tensor(v_r, q_r, v_r, ALU.subtract)
            c.inc(iv, "c1")
            c.wait(nc.scalar, "c1", 12)
            iv = nc.scalar.activation(v_r, v_r, AF.Ln, bias=float(EPS_L1))
            c.inc(iv, "a")
            c.wait(nc.scalar, "a", 9)
            iv = nc.scalar.activation(r_r, v_r, AF.Exp, scale=-0.5)
            c.inc(iv, "a")
            c.wait(nc.vector, "a", 10)
            iv = nc.vector.scalar_tensor_tensor(cc_r, m_r, -1.0, r_r, ALU.mult, ALU.mult)
            c.inc(iv, "c1")
            c.wait(nc.tensor, "c1", 13)
            c.wait(nc.tensor, "a", 10)
            i1 = nc.tensor.matmul(PS_B[:, 0:512], E.ones_1x128[0:1, :], r_r,
                                  start=True, stop=True)
            i2 = nc.tensor.matmul(PS_B[:, 512:1024], E.ones_1x128[0:1, :], cc_r,
                                  start=True, stop=True)
            c.inc(i1, "c2"); c.inc(i2, "c2")
            nv = 13
            nsc = 10
            for kc in range(8):
                c.wait(nc.vector, "c2", 18)
                c.wait(nc.vector, "a", nsc)
                iv = nc.vector.tensor_tensor(tmp[:], zf[:, kc * 512:(kc + 1) * 512],
                                             PS_B[:, 0:512], ALU.mult)
                c.inc(iv, "c1"); nv += 1
                c.wait(nc.vector, "c1", nv)
                iv = nc.vector.tensor_tensor(tmp2[:], tmp[:], PS_B[:, 512:1024], ALU.add)
                c.inc(iv, "c1"); nv += 1
                c.wait(nc.scalar, "c1", nv)
                ie = nc.scalar.activation(
                    xnbf[:, ds(kc * T + tc * 512, 512)],
                    tmp2[:], AF.Identity, scale=lnsta[:, woff + kc:woff + kc + 1],
                    bias=lnsta[:, boff + kc:boff + kc + 1])
                c.inc(ie, "a"); nsc += 1
            c.wait(nc.sync, "a", nsc)
            close(E)


def emit_c_attn(nc, E, st, l):
    import contextlib
    PV = E.pv
    pvec = E.pvec
    bd, PS_A, PS_B = E.bd, E.psA, E.psB
    LSTR = PV["l1_bq_1"] - PV["l1_bq_0"]
    base0 = PV["l1_bq_0"]
    OFF = dict(bq=0, bk=16, bv=32, u=48, u2=64, bo=80, b1=88, b2=120,
               ln1w=128, ln1b=136, ln2w=144, ln2b=152)
    lbase = base0 + l * LSTR
    tabs = load_tables(nc, E, st, ["T1", "T2P", "R2_BF"])
    T1, T2P, R2 = tabs["T1"], tabs["T2P"], tabs["R2_BF"]
    with contextlib.ExitStack() as s1:
        E.uid += 1
        xnbf = s1.enter_context(nc.sbuf_tensor("c_xnbf%d" % E.uid, [128, FC * T], BF16))
        lnsta = s1.enter_context(nc.sbuf_tensor("c_lnsta%d" % E.uid, [128, 48], F32))
        c = E.c
        iv = nc.vector.tensor_copy(lnsta[:, 0:16], pvec[:, ds(lbase + OFF["ln1w"], 16)])
        c.inc(iv, "c1")
        c.wait(nc.sync, "c1", 1)
        close(E)
        with nc.Fori(0, B) as b:
            def srcsl(tcoff):
                return E.xa[:, :, ds(b * T + tcoff * 512, 512)]
            c_ln_pass_b(nc, E, E.xa, b, xnbf, lnsta, 0, 8, f"l1a_{E.uid}")
            E.uid += 1
            with contextlib.ExitStack() as sA:
                wstg = sA.enter_context(nc.sbuf_tensor(f"c_wstg_{E.uid}", [128, 1536], F32))
                wqs = sA.enter_context(nc.sbuf_tensor(f"c_wqs_{E.uid}", [128, 512], BF16))
                wks = sA.enter_context(nc.sbuf_tensor(f"c_wks_{E.uid}", [128, 512], BF16))
                wvs = sA.enter_context(nc.sbuf_tensor(f"c_wvs_{E.uid}", [128, 512], BF16))
                bcol = sA.enter_context(nc.sbuf_tensor(f"c_bcol_{E.uid}", [128, 5], F32))
                qfp = sA.enter_context(nc.sbuf_tensor(f"c_qfp_{E.uid}", [64, T], F32))
                qbf0 = sA.enter_context(nc.sbuf_tensor(f"c_qbf0_{E.uid}", [64, T], BF16))
                tmpa = sA.enter_context(nc.sbuf_tensor(f"c_tmpa_{E.uid}", [64, T], F32))
                tmpb = sA.enter_context(nc.sbuf_tensor(f"c_tmpb_{E.uid}", [64, T], F32))
                qp = sA.enter_context(nc.sbuf_tensor(f"c_qp_{E.uid}", [128, T], BF16))
                kr = sA.enter_context(nc.sbuf_tensor(f"c_kr_{E.uid}", [128, T], BF16))
                vbf = sA.enter_context(nc.sbuf_tensor(f"c_vbf_{E.uid}", [64, T], BF16))
                vt = sA.enter_context(nc.sbuf_tensor(f"c_vt_{E.uid}", [128, 16 * 65], BF16))
                pb = [sA.enter_context(nc.sbuf_tensor(f"c_pb{i}_{E.uid}", [128, 1024], BF16)) for i in range(2)]
                cfp = sA.enter_context(nc.sbuf_tensor(f"c_cfp_{E.uid}", [64, T], F32))
                ctmp = sA.enter_context(nc.sbuf_tensor(f"c_ctmp_{E.uid}", [64, T], BF16))
                rrow = sA.enter_context(nc.sbuf_tensor(f"c_rrow_{E.uid}", [1, T], F32))
                E.uid += 1
                c = E.c
                iv = nc.vector.tensor_copy(kr[0:64, :], R2[:])
                c.inc(iv, "c1")
                iv = nc.vector.memset(
                    vt[:].rearrange("p (kt c) -> p kt c", kt=16)[:, :, 64:65], 1.0)
                c.inc(iv, "c1")
                c.wait(nc.sync, "c1", 2)
                close(E)
                qkvov = E.l1_qkvo
                with nc.Fori(0, 16) as h:
                    c = E.c
                    dma(nc, "sync", E, "d",
                        wstg[:].rearrange("p (a c f) -> p a c f", a=3, c=8),
                        qkvov[ds(l, 1), 0:3, :, ds(h * 64, 64)]
                        .rearrange("o a (c p) f -> p (o a) c f", p=128))
                    c.wait(nc.vector, "d")
                    for wi, wdst in ((0, wqs), (1, wks), (2, wvs)):
                        iv = nc.vector.tensor_copy(wdst[:], wstg[:, wi * 512:(wi + 1) * 512])
                        c.inc(iv, "c1")
                    iv = nc.vector.tensor_copy(
                        bcol[:],
                        pvec[:, ds(lbase, 80)].rearrange("p (a e) -> p a e", a=5)[:, :, ds(h, 1)].rearrange("p a e -> p (a e)"))
                    c.inc(iv, "c1")
                    c.wait(nc.tensor, "c1", 1)
                    for tc in range(4):
                        for kc in range(8):
                            i1 = nc.tensor.matmul(
                                PS_A[0:64, tc * 512:(tc + 1) * 512],
                                wqs[:, kc * 64:(kc + 1) * 64],
                                xnbf[:, kc * T + tc * 512:kc * T + (tc + 1) * 512],
                                start=(kc == 0), stop=(kc == 7))
                        c.inc(i1, "c2")
                    c.wait(nc.scalar, "c1", 4)
                    for tc in range(4):
                        c.wait(nc.scalar, "c2", tc + 1)
                        ie = nc.scalar.activation(qfp[:, tc * 512:(tc + 1) * 512],
                                                  PS_A[0:64, tc * 512:(tc + 1) * 512],
                                                  AF.Identity, bias=bcol[0:64, 0:1])
                        c.inc(ie, "a")
                        ie = nc.scalar.activation(qbf0[:, tc * 512:(tc + 1) * 512],
                                                  PS_A[0:64, tc * 512:(tc + 1) * 512],
                                                  AF.Identity, bias=bcol[0:64, 0:1])
                        c.inc(ie, "a")
                    c.wait(nc.tensor, "a", 8)
                    for tc in range(4):
                        i1 = nc.tensor.matmul(PS_B[0:64, tc * 512:(tc + 1) * 512],
                                              E.id64_bf[0:64, 0:64],
                                              qbf0[:, tc * 512:(tc + 1) * 512],
                                              start=True, stop=True)
                        c.inc(i1, "c4")
                    for tc in range(4):
                        c.wait(nc.scalar, "c4", tc + 1)
                        ie = nc.scalar.copy(qp[64:128, tc * 512:(tc + 1) * 512],
                                            PS_B[0:64, tc * 512:(tc + 1) * 512])
                        c.inc(ie, "a")
                    for tc in range(4):
                        i1 = nc.tensor.matmul(PS_A[0:64, tc * 512:(tc + 1) * 512],
                                              E.p64[0:64, 0:64],
                                              qfp[:, tc * 512:(tc + 1) * 512],
                                              start=True, stop=True)
                        c.inc(i1, "c4")
                    c.wait(nc.vector, "a", 8)
                    iv = nc.vector.scalar_tensor_tensor(tmpa[:], qfp[:], bcol[0:64, 3:4],
                                                        T1[:], ALU.add, ALU.mult)
                    c.inc(iv, "c3")
                    c.wait(nc.vector, "c4", 8)
                    iv = nc.vector.scalar_tensor_tensor(tmpb[:], PS_A[0:64, 0:T],
                                                        bcol[0:64, 4:5], T2P[:],
                                                        ALU.add, ALU.mult)
                    c.inc(iv, "c3")
                    c.wait(nc.vector, "c3", 2)
                    iv = nc.vector.tensor_tensor(qp[0:64, :], tmpa[:], tmpb[:], ALU.add)
                    c.inc(iv, "c3")
                    c.wait(nc.tensor, "c3", 2)
                    for tc in range(4):
                        for kc in range(8):
                            i1 = nc.tensor.matmul(
                                PS_A[0:64, tc * 512:(tc + 1) * 512],
                                wks[:, kc * 64:(kc + 1) * 64],
                                xnbf[:, kc * T + tc * 512:kc * T + (tc + 1) * 512],
                                start=(kc == 0), stop=(kc == 7))
                        c.inc(i1, "c2")
                    for tc in range(4):
                        c.wait(nc.scalar, "c2", 4 + tc + 1)
                        ie = nc.scalar.activation(qbf0[:, tc * 512:(tc + 1) * 512],
                                                  PS_A[0:64, tc * 512:(tc + 1) * 512],
                                                  AF.Identity, bias=bcol[0:64, 1:2])
                        c.inc(ie, "a")
                    c.wait(nc.tensor, "a", 16)
                    for tc in range(4):
                        i1 = nc.tensor.matmul(PS_B[0:64, tc * 512:(tc + 1) * 512],
                                              E.id64_bf[0:64, 0:64],
                                              qbf0[:, tc * 512:(tc + 1) * 512],
                                              start=True, stop=True)
                        c.inc(i1, "c4")
                    for tc in range(4):
                        c.wait(nc.scalar, "c4", 8 + tc + 1)
                        ie = nc.scalar.copy(kr[64:128, tc * 512:(tc + 1) * 512],
                                            PS_B[0:64, tc * 512:(tc + 1) * 512])
                        c.inc(ie, "a")
                    c.wait(nc.tensor, "a", 16)
                    for tc in range(4):
                        for kc in range(8):
                            i1 = nc.tensor.matmul(
                                PS_A[0:64, tc * 512:(tc + 1) * 512],
                                wvs[:, kc * 64:(kc + 1) * 64],
                                xnbf[:, kc * T + tc * 512:kc * T + (tc + 1) * 512],
                                start=(kc == 0), stop=(kc == 7))
                        c.inc(i1, "c2")
                    for tc in range(4):
                        c.wait(nc.scalar, "c2", 8 + tc + 1)
                        ie = nc.scalar.activation(vbf[:, tc * 512:(tc + 1) * 512],
                                                  PS_A[0:64, tc * 512:(tc + 1) * 512],
                                                  AF.Identity, bias=bcol[0:64, 2:3])
                        c.inc(ie, "a")
                    nmm = 12; n4 = 12; na = 24; n3 = 3; n1 = 4
                    for kt in range(16):
                        c.wait(nc.tensor, "a", 20 + (kt // 4) + 1)
                        if kt >= 2:
                            c.wait(nc.tensor, "a", 24 + (kt - 2) + 1)
                        slot = PS_B[:, 1024 + (kt % 2) * 512:1024 + (kt % 2) * 512 + 64]
                        i1 = nc.tensor.matmul(slot, vbf[:, kt * 128:(kt + 1) * 128],
                                              E.ident_bf[0:64, 0:64], start=True, stop=True)
                        c.inc(i1, "c4"); n4 += 1
                        c.wait(nc.scalar, "c4", n4)
                        ie = nc.scalar.copy(vt[:, kt * 65:kt * 65 + 64], slot)
                        c.inc(ie, "a"); na += 1
                    c.wait(nc.sync, "a", na)
                    close(E)
                    c = E.c
                    na = 0; n1 = 0; n2 = 0
                    last_kt = {bk: min(15, 4 * bk + 3) for bk in range(4)}
                    first_touch = set()
                    nch = 0
                    nmask = 0
                    for kt in range(16):
                        q0 = kt * 128
                        w = T - q0
                        jj = 0
                        while jj < w:
                            chunk = min(1024, w - jj)
                            qa = q0 + jj
                            reg = PS_A[:, (nch % 2) * 1024:(nch % 2) * 1024 + 1024]
                            if nch >= 2:
                                c.wait(nc.tensor, "a", nch - 1)
                            kk = 0
                            while kk < chunk:
                                seg = min(512, chunk - kk)
                                i1 = nc.tensor.matmul(reg[:, kk:kk + seg],
                                                      kr[:, kt * 128:(kt + 1) * 128],
                                                      qp[:, qa + kk:qa + kk + seg],
                                                      start=True, stop=True)
                                c.inc(i1, "c2"); n2 += 1
                                kk += seg
                            if jj == 0:
                                c.wait(nc.vector, "c2", n2)
                                iv = nc.vector.tensor_tensor(reg[:, 0:128], reg[:, 0:128],
                                                             E.m_diag[:], ALU.add)
                                c.inc(iv, "c1"); nmask += 1
                            pbx = pb[nch % 2]
                            c.wait(nc.scalar, "c1", nmask)
                            c.wait(nc.scalar, "c2", n2)
                            ie = nc.scalar.activation(pbx[:, 0:chunk], reg[:, 0:chunk],
                                                      AF.Exp, scale=0.125)
                            c.inc(ie, "a"); na += 1; nch += 1
                            c.wait(nc.tensor, "a", na)
                            kk = 0
                            while kk < chunk:
                                qq = qa + kk
                                bank = qq // 512
                                seg = min(512 - (qq % 512), chunk - kk)
                                stt_f = bank not in first_touch
                                first_touch.add(bank)
                                stp_f = (last_kt[bank] == kt) and (qq + seg == min(T, (bank + 1) * 512))
                                i1 = nc.tensor.matmul(PS_B[0:65, qq:qq + seg],
                                                      vt[:, kt * 65:(kt + 1) * 65],
                                                      pbx[:, kk:kk + seg],
                                                      start=stt_f, stop=stp_f)
                                c.inc(i1, "c2"); n2 += 1
                                kk += seg
                            jj += chunk
                    n1 = nmask
                    c.wait(nc.vector, "c2", n2)
                    iv = nc.vector.reciprocal(rrow[:], PS_B[64:65, 0:T])
                    c.inc(iv, "c1"); n1 += 1
                    c.wait(nc.tensor, "c1", n1)
                    for tc in range(4):
                        i1 = nc.tensor.matmul(PS_A[0:64, tc * 512:(tc + 1) * 512],
                                              E.ones_1x128[0:1, 0:64],
                                              rrow[:, tc * 512:(tc + 1) * 512],
                                              start=True, stop=True)
                        c.inc(i1, "c4")
                    c.wait(nc.scalar, "c2", n2)
                    ie = nc.scalar.copy(cfp[:], PS_B[0:64, 0:T])
                    c.inc(ie, "a"); na += 1
                    c.wait(nc.vector, "a", na)
                    c.wait(nc.vector, "c4", 4)
                    iv = nc.vector.tensor_tensor(ctmp[:], cfp[:], PS_A[0:64, 0:T], ALU.mult)
                    c.inc(iv, "c1"); n1 += 1
                    c.wait(nc.scalar, "c1", n1)
                    nc.scalar.dma_start(
                        out=E.ctxd[ds(h, 1)].rearrange("o p f -> (o p) f"),
                        in_=ctmp[:]).then_inc(bd["g"], 16)
                    c.n["g"] += 16
                    c.wait(nc.sync, "g")
                    close(E)
            with contextlib.ExitStack() as sW:
                ctx_tc = sW.enter_context(nc.sbuf_tensor(f"c_ctc_{E.uid}", [64, 16 * 512], BF16))
                wos_f = sW.enter_context(nc.sbuf_tensor(f"c_wosf_{E.uid}", [64, 16 * 128], F32))
                wos = sW.enter_context(nc.sbuf_tensor(f"c_wos_{E.uid}", [64, 16 * 128], BF16))
                xres = sW.enter_context(nc.sbuf_tensor(f"c_xres_{E.uid}", [128, FC * 512], F32))
                ofp = sW.enter_context(nc.sbuf_tensor(f"c_ofp_{E.uid}", [128, FC * 512], F32))
                bos = sW.enter_context(nc.sbuf_tensor(f"c_bos_{E.uid}", [128, 8], F32))
                E.uid += 1
                wov = E.l1_qkvo
                with nc.Fori(0, 4) as tc:
                    c = E.c
                    iv = nc.vector.tensor_copy(bos[:], pvec[:, ds(lbase + OFF["bo"], 8)])
                    c.inc(iv, "c1")
                    dma(nc, "sync", E, "d", ctx_tc[:].rearrange("p (h f) -> p h f", h=16),
                        E.ctxd[:, :, ds(tc * 512, 512)].rearrange("h p f -> p h f"))
                    dma(nc, "sync", E, "d", xres[:].rearrange("p (c f) -> p c f", c=FC),
                        E.xa[:, :, ds(b * T + tc * 512, 512)].rearrange("c p f -> p c f"))
                    for m in range(8):
                        psem = ("d2", "d3")[m % 2]
                        if m >= 1:
                            c.wait(nc.sync, "c1", m + 1)
                        dma(nc, "sync", E, psem,
                            wos_f[:].rearrange("p (h f) -> p h f", h=16),
                            wov[ds(l, 1), ds(3, 1), :, ds(m * 128, 128)]
                            .rearrange("a e (hh p) f -> p (a e hh) f", p=64))
                        c.wait(nc.vector, psem)
                        if m >= 1:
                            c.wait(nc.vector, "c2", m)
                        iv = nc.vector.tensor_copy(wos[:], wos_f[:])
                        c.inc(iv, "c1")
                        c.wait(nc.tensor, "c1", m + 2)
                        c.wait(nc.tensor, "d", 32)
                        if m >= 2:
                            c.wait(nc.tensor, "a", m - 1)
                        pbank = PS_A[:, (m % 2) * 512:(m % 2) * 512 + 512]
                        for hx in range(16):
                            i1 = nc.tensor.matmul(
                                pbank, wos[0:64, hx * 128:(hx + 1) * 128],
                                ctx_tc[0:64, hx * 512:(hx + 1) * 512],
                                start=(hx == 0), stop=(hx == 15))
                        c.inc(i1, "c2")
                        c.wait(nc.scalar, "c2", m + 1)
                        c.wait(nc.scalar, "c1", 1)
                        ie = nc.scalar.activation(ofp[:, m * 512:(m + 1) * 512], pbank,
                                                  AF.Identity, bias=bos[:, m:m + 1])
                        c.inc(ie, "a")
                        c.wait(nc.vector, "a", m + 1)
                        iv = nc.vector.tensor_tensor(ofp[:, m * 512:(m + 1) * 512],
                                                     ofp[:, m * 512:(m + 1) * 512],
                                                     xres[:, m * 512:(m + 1) * 512], ALU.add)
                        c.inc(iv, "c3")
                    c.wait(nc.sync, "c3", 8)
                    dma(nc, "sync", E, "d",
                        E.xb[:, :, ds(b * T + tc * 512, 512)].rearrange("c p f -> p c f"),
                        ofp[:].rearrange("p (c f) -> p c f", c=FC))
                    c.wait(nc.sync, "d")
                    close(E)

def emit_c_ffn(nc, E, st, l):
    import contextlib
    PV = E.pv
    pvec = E.pvec
    bd, PS_A, PS_B = E.bd, E.psA, E.psB
    LSTR = PV["l1_bq_1"] - PV["l1_bq_0"]
    base0 = PV["l1_bq_0"]
    OFF = dict(bq=0, bk=16, bv=32, u=48, u2=64, bo=80, b1=88, b2=120,
               ln1w=128, ln1b=136, ln2w=144, ln2b=152)
    lbase = base0 + l * LSTR
    with contextlib.ExitStack() as s2:
        E.uid += 1
        xn2 = s2.enter_context(nc.sbuf_tensor("c_xn2%d" % E.uid, [128, FC * N], BF16))
        lnst2 = s2.enter_context(nc.sbuf_tensor("c_lnst2%d" % E.uid, [128, 56], F32))
        c = E.c
        iv = nc.vector.tensor_copy(lnst2[:, 0:16], pvec[:, ds(lbase + OFF["ln2w"], 16)])
        c.inc(iv, "c1")
        iv = nc.vector.tensor_copy(lnst2[:, 16:48], pvec[:, ds(lbase + OFF["b1"], 32)])
        c.inc(iv, "c1")
        iv = nc.vector.tensor_copy(lnst2[:, 48:56], pvec[:, ds(lbase + OFF["b2"], 8)])
        c.inc(iv, "c1")
        c.wait(nc.sync, "c1", 3)
        close(E)
        c_ln_pass(nc, E, E.xb, xn2, lnst2, 0, 8, f"l1f_{E.uid}")
        E.uid += 1
        with contextlib.ExitStack() as sF:
            stg = sF.enter_context(nc.sbuf_tensor(f"c_fstg_{E.uid}", [128, 8 * 1024], F32))
            w1qb = sF.enter_context(nc.sbuf_tensor(f"c_w1q_{E.uid}", [128, 8 * 1024], BF16))
            w2qb = sF.enter_context(nc.sbuf_tensor(f"c_w2q_{E.uid}", [128, 8 * 1024], BF16))
            aq = sF.enter_context(nc.sbuf_tensor(f"c_aq_{E.uid}", [128, 8 * 512], BF16))
            y_sb = sF.enter_context(nc.sbuf_tensor(f"c_ysb_{E.uid}", [128, 8 * 512], F32))
            xrs = sF.enter_context(nc.sbuf_tensor(f"c_xrs_{E.uid}", [128, FC * 512], F32))
            bq8 = sF.enter_context(nc.sbuf_tensor(f"c_bq8_{E.uid}", [128, 8], F32))
            E.uid += 1
            c = E.c
            iv = nc.vector.memset(y_sb[:], 0.0)
            c.inc(iv, "c1")
            c.wait(nc.sync, "c1", 1)
            close(E)
            with nc.Fori(0, 8) as tc:
                with nc.Fori(0, 4) as qq:
                    c = E.c
                    for wi in range(2):
                        psem = ("d", "d2")[wi]
                        if wi >= 1:
                            c.wait(nc.sync, "c1", wi)
                        if wi == 0:
                            srcap = E.l1_w1[ds(l, 1), :, ds(qq * 1024, 1024)]
                        else:
                            srcap = E.l1_w2[ds(l, 1), ds(qq * 1024, 1024), :]
                        dma(nc, "sync", E, psem,
                            stg[:].rearrange("p (c f) -> p c f", c=8),
                            srcap.rearrange("o (kc p) f -> p (o kc) f", p=128))
                        c.wait(nc.vector, psem)
                        wdst = (w1qb, w2qb)[wi]
                        iv = nc.vector.tensor_copy(
                            wdst[:].rearrange("p (c f) -> p c f", c=8),
                            stg[:].rearrange("p (c f) -> p c f", c=8))
                        c.inc(iv, "c1")
                    iv = nc.vector.tensor_copy(bq8[:], lnst2[:, ds(16 + qq * 8, 8)])
                    c.inc(iv, "c1")
                    c.wait(nc.sync, "c1", 3)
                    close(E)
                    c = E.c
                    n2 = 0; na = 0; n3 = 0
                    for mh in range(8):
                        psh = PS_A[:, (mh % 2) * 512:(mh % 2) * 512 + 512]
                        if mh >= 2:
                            c.wait(nc.tensor, "a", mh - 1)
                        for kc in range(8):
                            i1 = nc.tensor.matmul(
                                psh, w1qb[:, kc * 1024 + mh * 128:kc * 1024 + (mh + 1) * 128],
                                xn2[:, ds(kc * N + tc * 512, 512)],
                                start=(kc == 0), stop=(kc == 7))
                        c.inc(i1, "c2"); n2 += 1
                        c.wait(nc.scalar, "c2", n2)
                        ie = nc.scalar.activation(aq[:, mh * 512:(mh + 1) * 512], psh,
                                                  AF.Gelu, bias=bq8[:, mh:mh + 1])
                        c.inc(ie, "a"); na += 1
                    for mo in range(8):
                        psy = PS_B[:, (mo % 2) * 512:(mo % 2) * 512 + 512]
                        c.wait(nc.tensor, "a", 8)
                        if mo >= 2:
                            c.wait(nc.tensor, "c3", n3 - 1)
                        for kch in range(8):
                            i1 = nc.tensor.matmul(
                                psy, w2qb[:, kch * 1024 + mo * 128:kch * 1024 + (mo + 1) * 128],
                                aq[:, kch * 512:(kch + 1) * 512],
                                start=(kch == 0), stop=(kch == 7))
                        c.inc(i1, "c2"); n2 += 1
                        c.wait(nc.vector, "c2", n2)
                        iv = nc.vector.tensor_tensor(
                            y_sb[:, mo * 512:(mo + 1) * 512],
                            y_sb[:, mo * 512:(mo + 1) * 512], psy, ALU.add)
                        c.inc(iv, "c3"); n3 += 1
                    c.wait(nc.sync, "c3", n3)
                    close(E)
                c = E.c
                dma(nc, "sync", E, "d", xrs[:].rearrange("p (c f) -> p c f", c=FC),
                    E.xb[:, :, ds(tc * 512, 512)].rearrange("c p f -> p c f"))
                c.wait(nc.vector, "d")
                for mo in range(8):
                    iv = nc.vector.tensor_tensor(y_sb[:, mo * 512:(mo + 1) * 512],
                                                 y_sb[:, mo * 512:(mo + 1) * 512],
                                                 xrs[:, mo * 512:(mo + 1) * 512], ALU.add)
                    c.inc(iv, "c1")
                    c.wait(nc.scalar, "c1", mo + 1)
                    ie = nc.scalar.activation(xrs[:, mo * 512:(mo + 1) * 512],
                                              y_sb[:, mo * 512:(mo + 1) * 512],
                                              AF.Identity, bias=lnst2[:, 48 + mo:48 + mo + 1])
                    c.inc(ie, "a")
                c.wait(nc.sync, "a", 8)
                dma(nc, "sync", E, "d",
                    E.xa[:, :, ds(tc * 512, 512)].rearrange("c p f -> p c f"),
                    xrs[:].rearrange("p (c f) -> p c f", c=FC))
                c.wait(nc.sync, "d")
                c.wait(nc.vector, "a", 8)
                iv = nc.vector.memset(y_sb[:], 0.0)
                c.inc(iv, "c1")
                c.wait(nc.sync, "c1", 9)
                close(E)



PHASE_BIGTABS = {
    "IN": [], "A": ["C0", "S0P", "M_FULL"], "PL": [], "PLX": [], "PLR": [],
    "B": ["ONES_ROW"], "C1": ["T1", "T2P", "R2_BF"], "C2": [], "D": [],
}

PHASE_USES = {
    "IN": [],
    "A": ["l0_wq", "l0_wk", "l0_wv", "l0_wo"],
    "PL": [],
    "PLX": [],
    "PLR": [],
    "B": ["router", "mw1", "mw3", "mw2"],
    "C1": ["l1_qkvo"],
    "C2": ["l1_w1", "l1_w2"],
    "D": ["d_w1t", "d_w2t"],
}

LAUNCH_PLAN = (
    [("IN", None, [], ["XA"]),
     ("A", None, ["XA"], ["XB"]),
     ("PL", None, ["XB"], ["XA"]),
     ("B", None, ["XA"], ["XB"]),
     ("PLX", None, ["XB"], ["XA", "XC"])]
    + sum([[("C1", l, ["XA"], ["XB"]), ("C2", l, ["XB"], ["XA"])] for l in range(4)], [])
    + [("PLR", None, ["XA", "XC"], ["XB"]),
       ("D", None, ["XB"], [])]
)


def build_launch(phase, arg, reads, writes, pv_cols, pv_ncol):
    import contextlib
    E = make_base(
        pv_ncol,
        xa_kind=("ExternalInput" if "XA" in reads else
                 ("ExternalOutput" if "XA" in writes else "Internal")),
        xb_kind=("ExternalInput" if "XB" in reads else
                 ("ExternalOutput" if "XB" in writes else "Internal")),
        xc_kind=("ExternalInput" if "XC" in reads else
                 ("ExternalOutput" if "XC" in writes else "Internal")),
        with_x=(phase == "IN"),
        with_out=(phase == "D"),
        uses=PHASE_USES[phase],
        big_tabs=PHASE_BIGTABS[phase],
    )
    E.pv = pv_cols
    nc = E.nc
    with contextlib.ExitStack() as st:
        if phase == "IN":
            emit_input(nc, E, st)
        elif phase == "A":
            emit_phase_a(nc, E, st)
        elif phase == "PL":
            emit_postln(nc, E, st, E.xb, E.xa, None)
        elif phase == "PLX":
            emit_postln(nc, E, st, E.xb, E.xa, None, dst2=E.xc)
        elif phase == "PLR":
            emit_postln(nc, E, st, E.xa, E.xb, None, res_dram=E.xc)
        elif phase == "B":
            emit_phase_b(nc, E, st)
        elif phase == "C1":
            emit_c_attn(nc, E, st, arg)
        elif phase == "C2":
            emit_c_ffn(nc, E, st, arg)
        elif phase == "D":
            E.d_src = E.xb
            emit_phase_d(nc, E, st)
        nc.all_engine_barrier()
    return E


_PROG_CACHE = {}


def get_launch_program(phase, arg, reads, writes, pv_cols, pv_ncol):
    key = (phase, arg)
    if key not in _PROG_CACHE:
        _PROG_CACHE[key] = build_launch(phase, arg, reads, writes, pv_cols, pv_ncol)
    return _PROG_CACHE[key]



def reg_snapshot(nc):
    return {a.name for a in nc.m.functions[0].allocations
            if type(a).__name__ == "Register"}


def reg_reclaim(nc, baseline):
    import bass_rust
    eng_map = {"Pool": nc.gpsimd, "Activation": nc.scalar, "PE": nc.tensor,
               "DVE": nc.vector, "SP": nc.sync}
    n = 0
    for a in list(nc.m.functions[0].allocations):
        if type(a).__name__ != "Register":
            continue
        if a.name in baseline or not a.allocated:
            continue
        ename = str(a.engine).split(".")[-1]
        h = bass_rust.RegisterHandle(a.name, a.engine)
        try:
            eng_map[ename].free_register(h)
            n += 1
        except ValueError:
            pass
    return n


def build_fused(pv_cols, pv_ncol):
    import contextlib
    E = make_base(pv_ncol, xa_kind="Internal", with_x=True, xb_kind="Internal",
                  xc_kind="Internal", with_out=True, uses=None,
                  big_tabs=list(BIG_TABLES))
    E.pv = pv_cols
    nc = E.nc
    base = reg_snapshot(nc)

    def rr():
        reg_reclaim(nc, base)

    E.rr = rr
    with contextlib.ExitStack() as st:
        emit_input(nc, E, st)
    rr()
    with contextlib.ExitStack() as st:
        emit_phase_a(nc, E, st)
    rr()
    with contextlib.ExitStack() as st:
        emit_postln(nc, E, st, E.xb, E.xa, None)
    rr()
    with contextlib.ExitStack() as st:
        emit_phase_b(nc, E, st)
    rr()
    with contextlib.ExitStack() as st:
        emit_postln(nc, E, st, E.xb, E.xa, None, dst2=E.xc)
    rr()
    for l in range(4):
        with contextlib.ExitStack() as st:
            emit_c_attn(nc, E, st, l)
        rr()
        with contextlib.ExitStack() as st:
            emit_c_ffn(nc, E, st, l)
        rr()
    with contextlib.ExitStack() as st:
        emit_postln(nc, E, st, E.xa, E.xb, None, res_dram=E.xc)
    rr()
    with contextlib.ExitStack() as st:
        E.d_src = E.xb
        emit_phase_d(nc, E, st)
    nc.all_engine_barrier()
    return E


FUSED = True
_FUSED_CACHE = []


def kernel(**inputs):
    from concourse.bass_utils import run_bass_kernel_spmd
    NCORES = 8
    gs = []
    cols = None
    for core in range(NCORES):
        g, cols = host_prep(inputs, core)
        gs.append(g)
    pv_ncol = gs[0]["pvec"].shape[1]
    if FUSED:
        if not _FUSED_CACHE:
            _FUSED_CACHE.append(build_fused(cols, pv_ncol))
        E = _FUSED_CACHE[0]
        in_maps = []
        for core in range(NCORES):
            g = gs[core]
            m = {k: g[k] for k in WEIGHT_SPECS}
            m["pvec"] = g["pvec"]
            for k in TABLE_SPECS:
                m[k] = g[k]
            m["x_in"] = g["x_in"]
            in_maps.append(m)
        res = run_bass_kernel_spmd(E.nc, in_maps, core_ids=list(range(NCORES)))
        out = [np.asarray(res.results[c]["OUT"], np.float32) for c in range(NCORES)]
        return np.concatenate(out, axis=0).reshape(16, T, D)
    return kernel_unfused(gs, cols, pv_ncol)


def kernel_unfused(gs, cols, pv_ncol):
    from concourse.bass_utils import run_bass_kernel_spmd
    NCORES = 8
    state = [dict() for _ in range(NCORES)]
    out = [None] * NCORES
    for (phase, arg, reads, writes) in LAUNCH_PLAN:
        E = get_launch_program(phase, arg, reads, writes, cols, pv_ncol)
        in_maps = []
        for core in range(NCORES):
            g = gs[core]
            m = {k: g[k] for k in PHASE_USES[phase]}
            m["pvec"] = g["pvec"]
            for k in SMALL_TABLES:
                m[k] = g[k]
            for k in PHASE_BIGTABS[phase]:
                m[k] = g[k]
            if phase == "IN":
                m["x_in"] = g["x_in"]
            for r in reads:
                m[r] = state[core][r]
            in_maps.append(m)
        res = run_bass_kernel_spmd(E.nc, in_maps, core_ids=list(range(NCORES)))
        for core in range(NCORES):
            for w in writes:
                state[core][w] = np.asarray(res.results[core][w])
            if phase == "D":
                out[core] = np.asarray(res.results[core]["OUT"], np.float32)
    return np.concatenate(out, axis=0).reshape(16, T, D)
